# version 04
# baseline (speedup 1.0000x reference)
import numpy as np
import ml_dtypes
from contextlib import ExitStack

import concourse.bass as bass
import concourse.mybir as mybir
from concourse.bass_utils import run_bass_kernel_spmd

F32 = mybir.dt.float32
BF16 = mybir.dt.bfloat16
F16 = mybir.dt.float16
AF = mybir.ActivationFunctionType
ALU = mybir.AluOpType
AX = mybir.AxisListType
BF = ml_dtypes.bfloat16

NCORES = 8
TOK = 2048
D = 1024
C = 128
NCH = TOK // C
HEADS = 4
DK = 256
DV = 512
EPS = 1e-5
LN16 = float(np.log(16.0))

ENGS = ("tensor", "vector", "scalar", "gpsimd", "sync")


class Prog:

    NSLOT = 4

    def __init__(self, nc):
        self.nc = nc
        self.ops = {e: [] for e in ENGS}
        self.nops = {e: 0 for e in ENGS}
        self.qcount = {}
        self.last_write = {}
        self.readers = {}
        self.last_psum = {}
        self.waited = {e: {} for e in ENGS}
        self.need_inc = {e: set() for e in ENGS}

    def _src_key(self, tok):
        return ("E", tok[1]) if tok[0] == "E" else ("Q",) + tuple(tok[1])

    def rec(self, eng, emit, reads=(), writes=(), queue=None, tag=""):
        deps = {}

        def add(tok):
            k = self._src_key(tok)
            v = tok[2]
            if k not in deps or deps[k] < v:
                deps[k] = v

        for r in reads:
            t = self.last_write.get(r)
            if t is not None:
                add(t)
            if r.startswith("ps_"):
                t2 = self.last_psum.get(r)
                if t2 is not None:
                    add(t2)
        for w in writes:
            t = self.last_write.get(w)
            if t is not None:
                add(t)
            for kk, vv in self.readers.get(w, {}).items():
                add(("E", kk[1], vv) if kk[0] == "E" else ("Q", (kk[1], kk[2]), vv))
            if w.startswith("ps_"):
                t2 = self.last_psum.get(w)
                if t2 is not None:
                    add(t2)

        if queue is not None:
            idx = None
            n = self.qcount.get(queue, 0) + 1
            self.qcount[queue] = n
            slot = (n - 1) % self.NSLOT
            gen = (n - 1) // self.NSLOT + 1
            my = ("Q", (queue, slot), gen)
            if gen > 1:
                add(("Q", (queue, slot), gen - 1))
        else:
            self.nops[eng] += 1
            idx = self.nops[eng]
            my = ("E", eng, idx)
        myk = self._src_key(my)

        waits = []
        for k, v in deps.items():
            if k[0] == "E" and k[1] == "tensor" and eng == "tensor":
                continue
            got = self.waited[eng].get(k, 0)
            if v <= got:
                continue
            waits.append((k, v))
            self.waited[eng][k] = v
            if k[0] == "E":
                self.need_inc[k[1]].add(v)

        for r in reads:
            self.readers.setdefault(r, {})[myk] = max(
                self.readers.get(r, {}).get(myk, 0), my[2]
            )
            if r.startswith("ps_"):
                self.last_psum[r] = my
        for w in writes:
            self.last_write[w] = my
            self.readers[w] = {}
            if w.startswith("ps_"):
                self.last_psum[w] = my

        self.ops[eng].append(
            dict(emit=emit, waits=waits, queue=queue, idx=idx, tag=tag)
        )
        return my

    def _queue_toks(self):
        toks = []
        for q, n in self.qcount.items():
            for slot in range(min(n, self.NSLOT)):
                gen = (n - 1 - slot) // self.NSLOT + 1
                toks.append(("Q", (q, slot), gen))
        return toks

    def barrier(self):
        toks = []
        for e in ENGS:
            if self.nops[e]:
                toks.append(("E", e, self.nops[e]))
        toks += self._queue_toks()
        for e in ENGS:
            for t in toks:
                k = self._src_key(t)
                if k == ("E", e):
                    continue
                v = t[2]
                if self.waited[e].get(k, 0) < v:
                    self.waited[e][k] = v
                    if k[0] == "E":
                        self.need_inc[k[1]].add(v)
                    self.ops[e].append(dict(emit=None, waits=[(k, v)], queue=None,
                                            idx=None, tag="barrier"))

    def finalize(self, es, block):
        sems = {}
        for e in ENGS:
            sems[("E", e)] = es.enter_context(self.nc.semaphore(f"sem_{e}"))
        for q in self.qcount:
            for slot in range(self.NSLOT):
                sems[("Q", q, slot)] = es.enter_context(
                    self.nc.semaphore(f"sem_{q}_{slot}"))
        inc_prefix = {}
        for e in ENGS:
            pref = {}
            cnt = 0
            marks = self.need_inc[e]
            for i in range(1, self.nops[e] + 1):
                if i in marks:
                    cnt += 1
                pref[i] = cnt
            inc_prefix[e] = pref

        def val_of(k, v):
            if k[0] == "Q":
                return 16 * v
            return inc_prefix[k[1]].get(v, 0)

        qseen = {}

        def body(e):
            def run(eng):
                for op in self.ops[e]:
                    for (k, v) in op["waits"]:
                        eng.wait_ge(sems[k], val_of(k, v))
                    if op["emit"] is None:
                        continue
                    inst = op["emit"](eng)
                    if op["queue"] is not None:
                        q = op["queue"]
                        n = qseen.get(q, 0)
                        qseen[q] = n + 1
                        inst.then_inc(sems[("Q", q, n % self.NSLOT)], 16)
                    elif op["idx"] in self.need_inc[e]:
                        inst.then_inc(sems[("E", e)], 1)
            return run

        for e in ENGS:
            if self.ops[e]:
                getattr(block, e)(body(e))


def _tile_consts():
    i = np.arange(C)
    trib = ((i[:, None] <= i[None, :]).astype(np.float32) / 16.0).astype(BF)
    trir = ((i[:, None] > i[None, :]).astype(np.float32) / 16.0).astype(BF)
    mask = (i[:, None] <= i[None, :]).astype(np.float32)
    identb = np.eye(C, dtype=np.float32).astype(BF)
    identf = np.eye(C, dtype=np.float32)
    onesb = np.ones((C, C), dtype=np.float32).astype(BF)
    esel = np.zeros((8, 512), dtype=np.float32)
    for p in range(512):
        esel[p // 64, p] = 1.0
    return dict(trib=trib, trir=trir, mask=mask, identb=identb, identf=identf,
                onesb=onesb, esel=esel.astype(BF))


def _cmat(core):
    m_ = np.zeros((8, 8), dtype=np.float32)
    for r in range(8):
        for m in range(8):
            if r < m < core:
                m_[m, r] = 1.0
    for r in range(8):
        if r >= core:
            m_[:, r] = 0.0
    return m_.astype(BF)


def _wmask(core):
    m = np.zeros((8,), np.float32)
    m[:core] = 1.0
    return m



def build_phase_a():
    nc = bass.Bass("TRN2", target_bir_lowering=False, debug=False)
    ap = {}
    ap["x"] = nc.dram_tensor("x", [TOK, D], F32, kind="ExternalInput").ap()
    for n, shp in [("wq", [D, D]), ("wk", [D, D]), ("wv", [D, 2 * D]),
                   ("wgk1", [D, 16]), ("wgk2", [16, D]), ("bgk2", [1, D])]:
        ap[n] = nc.dram_tensor(n, shp, F32, kind="ExternalInput").ap()
    for n, shp in [("ctrib", [C, C]), ("ctrir", [C, C]), ("cidentb", [C, C]),
                   ("conesb", [C, C])]:
        ap[n] = nc.dram_tensor(n, shp, BF16, kind="ExternalInput").ap()
    ap["cmask"] = nc.dram_tensor("cmask", [C, C], F32, kind="ExternalInput").ap()
    ap["cln16"] = nc.dram_tensor("cln16", [C, 1], F32, kind="ExternalInput").ap()
    ap["o16_d"] = nc.dram_tensor("o16_d", [NCH, C, 2 * D], F16, kind="ExternalOutput").ap()
    ap["qgT_d"] = nc.dram_tensor("qgT_d", [NCH, C, D], BF16, kind="ExternalOutput").ap()
    ap["base_d"] = nc.dram_tensor("base_d", [NCH, C, D], F32, kind="ExternalOutput").ap()
    ap["xnT_d"] = nc.dram_tensor("xnT_d", [NCH, C, D], BF16, kind="ExternalOutput").ap()
    ap["T_d"] = nc.dram_tensor("T_d", [C, 8 * DV], BF16, kind="ExternalOutput").ap()
    ap["btot_d"] = nc.dram_tensor("btot_d", [1, D], F32, kind="ExternalOutput").ap()

    es = ExitStack()
    with es:
        sb = lambda n, s, t: es.enter_context(nc.sbuf_tensor(n, s, t))[:]
        ps = lambda n, s, t: es.enter_context(nc.psum_tensor(n, s, t))[:]
        w_q = sb("w_q", [128, 8 * D], BF16)
        w_k = sb("w_k", [128, 8 * D], BF16)
        w_v = sb("w_v", [128, 8 * 2 * D], BF16)
        w_gk1 = sb("w_gk1", [128, 8 * 16], BF16)
        w_gk2 = sb("w_gk2", [16, D], BF16)
        s_bgk2 = sb("s_bgk2", [1, D], BF16)
        c_trib = sb("c_trib", [C, C], BF16)
        c_trir = sb("c_trir", [C, C], BF16)
        c_identb = sb("c_identb", [C, C], BF16)
        c_ones = sb("c_ones", [C, C], BF16)
        c_mask = sb("c_mask", [C, C], F32)
        c_ln16 = sb("c_ln16", [C, 1], F32)
        s_S = sb("s_S", [128, 8 * DV], F32)
        s_Sb = sb("s_Sb", [128, 8 * DV], BF16)
        s_bprow = sb("s_bprow", [1, D], F32)
        s_bphi = sb("s_bphi", [1, D], BF16)
        s_bplo = sb("s_bplo", [1, D], BF16)
        t_x = sb("t_x", [C, D], F32)
        t_st = sb("t_st", [C, 8], F32)
        t_xn = sb("t_xn", [C, D], F32)
        t_xnb = sb("t_xnb", [C, D], BF16)
        t_xnT = sb("t_xnT", [C, D], BF16)
        t_base = sb("t_base", [C, D], F32)
        t_f1 = sb("t_f1", [C, D], F32)
        t_gk = sb("t_gk", [C, D], F32)
        t_ghi = sb("t_ghi", [C, D], BF16)
        t_glo = sb("t_glo", [C, D], BF16)
        t_z1e = sb("t_z1e", [C, 16], BF16)
        t_z1b = sb("t_z1b", [16, C], BF16)
        t_lam = sb("t_lam", [C, D], BF16)
        t_lami = sb("t_lami", [C, D], BF16)
        t_lrev = sb("t_lrev", [C, D], BF16)
        t_lglob = sb("t_lglob", [C, D], BF16)
        t_qh = sb("t_qh", [C, D], BF16)
        t_qg = sb("t_qg", [C, D], BF16)
        t_kt = sb("t_kt", [C, D], BF16)
        t_kh = sb("t_kh", [C, D], BF16)
        t_v = sb("t_v", [C, 2 * D], BF16)
        t_qhT = sb("t_qhT", [C, D], BF16)
        t_ktT = sb("t_ktT", [C, D], BF16)
        t_qgT = sb("t_qgT", [C, D], BF16)
        t_pt = sb("t_pt", [C, 4 * C], BF16)
        t_o16 = sb("t_o16", [C, 2 * D], F16)
        t_dc = sb("t_dc", [C, 8], F32)
        ps_w1 = ps("ps_w1", [128, 1024], F32)
        ps_w2 = ps("ps_w2", [128, 1024], F32)
        ps_h1 = ps("ps_h1", [128, 512], F32)
        ps_h2 = ps("ps_h2", [128, 512], F32)
        ps_tb1 = ps("ps_tb1", [128, 1024], BF16)
        ps_tb2 = ps("ps_tb2", [128, 1024], BF16)

        p = Prog(nc)
        R = p.rec

        def ld(engq, dst, src, rd, wr):
            R(engq[0], lambda e, d=dst, s=src: e.dma_start(out=d, in_=s),
              reads=rd, writes=wr, queue=engq[1])

        gq = ("gpsimd", "qg")
        sq = ("sync", "qs")
        for kt in range(8):
            ld(gq, w_q[:, kt * D:(kt + 1) * D],
               ap["wq"][kt * 128:(kt + 1) * 128, :], ["d:wq"], ["w_q"])
            ld(gq, w_k[:, kt * D:(kt + 1) * D],
               ap["wk"][kt * 128:(kt + 1) * 128, :], ["d:wk"], ["w_k"])
            ld(gq, w_v[:, kt * 2 * D:(kt + 1) * 2 * D],
               ap["wv"][kt * 128:(kt + 1) * 128, :], ["d:wv"], ["w_v"])
        ld(gq, w_gk1.rearrange("p (kt n) -> p kt n", kt=8),
           ap["wgk1"].rearrange("(kt p) n -> p kt n", p=128), ["d:wgk1"], ["w_gk1"])
        ld(gq, w_gk2, ap["wgk2"], ["d:wgk2"], ["w_gk2"])
        ld(gq, s_bgk2, ap["bgk2"], ["d:bgk2"], ["s_bgk2"])
        ld(gq, c_trib, ap["ctrib"], ["d:ctrib"], ["c_trib"])
        ld(gq, c_trir, ap["ctrir"], ["d:ctrir"], ["c_trir"])
        ld(gq, c_identb, ap["cidentb"], ["d:cidentb"], ["c_identb"])
        ld(gq, c_ones, ap["conesb"], ["d:conesb"], ["c_ones"])
        ld(sq, c_mask, ap["cmask"], ["d:cmask"], ["c_mask"])
        ld(sq, c_ln16, ap["cln16"], ["d:cln16"], ["c_ln16"])
        R("vector", lambda e: e.memset(s_S, 0.0), writes=["s_S"])
        R("vector", lambda e: e.memset(s_Sb, 0.0), writes=["s_Sb"])
        R("vector", lambda e: e.memset(s_bprow, 0.0), writes=["s_bprow"])
        R("vector", lambda e: e.memset(s_bphi, 0.0), writes=["s_bphi"])
        R("vector", lambda e: e.memset(s_bplo, 0.0), writes=["s_bplo"])

        def mm(out, lhsT, rhs, start, stop, rd, wr):
            R("tensor", lambda e, o=out, l=lhsT, r=rhs, s1=start, s2=stop:
              e.matmul(o, l, r, start=s1, stop=s2), reads=rd, writes=wr)

        def tr(out, in_, ident, rd, wr):
            R("tensor", lambda e, o=out, i=in_, d=ident: e.transpose(o, i, d),
              reads=rd, writes=wr)

        for c in range(NCH):
            xc = ap["x"][c * C:(c + 1) * C, :]
            R("sync", lambda e, s=xc: e.dma_start(out=t_x, in_=s),
              reads=["d:x"], writes=["t_x"], queue="qs")
            R("vector", lambda e: e.reduce_sum(t_st[:, 0:1], t_x, axis=AX.X),
              reads=["t_x"], writes=["t_st"])
            R("scalar", lambda e: e.activation(out=t_f1, in_=t_x, func=AF.Square,
                                               accum_out=t_st[:, 1:2]),
              reads=["t_x"], writes=["t_f1", "t_st"])
            R("vector", lambda e: e.tensor_scalar(out=t_st[:, 2:3], in0=t_st[:, 0:1],
                                                  scalar1=1.0 / D, scalar2=None, op0=ALU.mult),
              reads=["t_st"], writes=["t_st"])
            R("vector", lambda e: e.tensor_scalar(out=t_st[:, 3:4], in0=t_st[:, 1:2],
                                                  scalar1=1.0 / D, scalar2=None, op0=ALU.mult),
              reads=["t_st"], writes=["t_st"])
            R("vector", lambda e: e.tensor_mul(t_st[:, 4:5], t_st[:, 2:3], t_st[:, 2:3]),
              reads=["t_st"], writes=["t_st"])
            R("vector", lambda e: e.tensor_sub(t_st[:, 4:5], t_st[:, 3:4], t_st[:, 4:5]),
              reads=["t_st"], writes=["t_st"])
            R("vector", lambda e: e.tensor_scalar(out=t_st[:, 4:5], in0=t_st[:, 4:5],
                                                  scalar1=1.0, scalar2=EPS, op0=ALU.mult,
                                                  op1=ALU.add),
              reads=["t_st"], writes=["t_st"])
            R("scalar", lambda e: e.activation(out=t_st[:, 5:6], in_=t_st[:, 4:5],
                                               func=AF.Sqrt),
              reads=["t_st"], writes=["t_st"])
            R("vector", lambda e: e.reciprocal(t_st[:, 6:7], t_st[:, 5:6]),
              reads=["t_st"], writes=["t_st"])
            R("vector", lambda e: e.tensor_scalar(out=t_xn, in0=t_x, scalar1=t_st[:, 2:3],
                                                  scalar2=t_st[:, 6:7], op0=ALU.subtract,
                                                  op1=ALU.mult),
              reads=["t_x", "t_st"], writes=["t_xn"])
            R("vector", lambda e: e.tensor_copy(t_xnb, t_xn), reads=["t_xn"], writes=["t_xnb"])
            R("vector", lambda e: e.tensor_add(t_base, t_x, t_xn),
              reads=["t_x", "t_xn"], writes=["t_base"])
            R("sync", lambda e, cc=c: e.dma_start(out=ap["base_d"][cc], in_=t_base),
              reads=["t_base"], writes=["d:base"], queue="qs")
            for j in range(8):
                tr(ps_tb1[:, j * C:(j + 1) * C], t_xnb[:, j * C:(j + 1) * C], c_identb,
                   ["t_xnb", "c_identb"], ["ps_tb1"])
            R("scalar", lambda e: e.copy(t_xnT, ps_tb1), reads=["ps_tb1"], writes=["t_xnT"])
            R("sync", lambda e, cc=c: e.dma_start(out=ap["xnT_d"][cc], in_=t_xnT),
              reads=["t_xnT"], writes=["d:xnT"], queue="qs")
            for j in range(8):
                mm(ps_h1[:, 0:16], t_xnT[:, j * C:(j + 1) * C],
                   w_gk1[:, j * 16:(j + 1) * 16], j == 0, j == 7, ["t_xnT", "w_gk1"], ["ps_h1"])
            R("vector", lambda e: e.tensor_copy(t_z1e, ps_h1[:, 0:16]),
              reads=["ps_h1"], writes=["t_z1e"])
            tr(ps_tb2[0:16, 0:C], t_z1e, c_identb, ["t_z1e", "c_identb"], ["ps_tb2"])
            R("scalar", lambda e: e.copy(t_z1b, ps_tb2[0:16, 0:C]),
              reads=["ps_tb2"], writes=["t_z1b"])
            for nh in range(2):
                sl = slice(nh * 512, (nh + 1) * 512)
                mm(ps_w1[:, sl], t_z1b, w_gk2[:, sl], True, False, ["t_z1b", "w_gk2"], ["ps_w1"])
                mm(ps_w1[:, sl], c_ones[0:1, :], s_bgk2[:, sl], False, True,
                   ["c_ones", "s_bgk2"], ["ps_w1"])
            R("scalar", lambda e: e.activation(out=t_f1, in_=ps_w1, func=AF.Abs),
              reads=["ps_w1"], writes=["t_f1"])
            R("scalar", lambda e: e.activation(out=t_f1, in_=t_f1, func=AF.Exp, scale=-1.0),
              reads=["t_f1"], writes=["t_f1"])
            R("scalar", lambda e: e.activation(out=t_f1, in_=t_f1, func=AF.Ln, bias=1.0),
              reads=["t_f1"], writes=["t_f1"])
            R("vector", lambda e: e.tensor_scalar(out=t_gk, in0=ps_w1, scalar1=0.0,
                                                  scalar2=None, op0=ALU.min),
              reads=["ps_w1"], writes=["t_gk"])
            R("vector", lambda e: e.tensor_sub(t_gk, t_gk, t_f1),
              reads=["t_gk", "t_f1"], writes=["t_gk"])
            R("vector", lambda e: e.tensor_copy(t_ghi, t_gk), reads=["t_gk"], writes=["t_ghi"])
            R("vector", lambda e: e.tensor_sub(t_glo, t_gk, t_ghi),
              reads=["t_gk", "t_ghi"], writes=["t_glo"])
            for nh in range(2):
                sl = slice(nh * 512, (nh + 1) * 512)
                mm(ps_w1[:, sl], c_trib, t_ghi[:, sl], True, False, ["c_trib", "t_ghi"], ["ps_w1"])
                mm(ps_w1[:, sl], c_trib, t_glo[:, sl], False, True, ["c_trib", "t_glo"], ["ps_w1"])
                mm(ps_w2[:, sl], c_trib, t_ghi[:, sl], True, False, ["c_trib", "t_ghi"], ["ps_w2"])
                mm(ps_w2[:, sl], c_trib, t_glo[:, sl], False, False, ["c_trib", "t_glo"], ["ps_w2"])
                mm(ps_w2[:, sl], c_ones[0:1, :], s_bphi[:, sl], False, False,
                   ["c_ones", "s_bphi"], ["ps_w2"])
                mm(ps_w2[:, sl], c_ones[0:1, :], s_bplo[:, sl], False, True,
                   ["c_ones", "s_bplo"], ["ps_w2"])
            mm(ps_h1[:, 0:512], c_trir, t_ghi[:, 0:512], True, False, ["c_trir", "t_ghi"], ["ps_h1"])
            mm(ps_h1[:, 0:512], c_trir, t_glo[:, 0:512], False, True, ["c_trir", "t_glo"], ["ps_h1"])
            mm(ps_h2[:, 0:512], c_trir, t_ghi[:, 512:1024], True, False, ["c_trir", "t_ghi"], ["ps_h2"])
            mm(ps_h2[:, 0:512], c_trir, t_glo[:, 512:1024], False, True, ["c_trir", "t_glo"], ["ps_h2"])
            R("scalar", lambda e: e.activation(out=t_lam, in_=ps_w1, func=AF.Exp, bias=c_ln16),
              reads=["ps_w1", "c_ln16"], writes=["t_lam"])
            R("scalar", lambda e: e.activation(out=t_lami, in_=ps_w1, func=AF.Exp, scale=-1.0),
              reads=["ps_w1"], writes=["t_lami"])
            R("scalar", lambda e: e.activation(out=t_lglob, in_=ps_w2, func=AF.Exp, bias=c_ln16),
              reads=["ps_w2", "c_ln16"], writes=["t_lglob"])
            R("scalar", lambda e: e.activation(out=t_lrev[:, 0:512], in_=ps_h1, func=AF.Exp),
              reads=["ps_h1"], writes=["t_lrev"])
            R("scalar", lambda e: e.activation(out=t_lrev[:, 512:1024], in_=ps_h2, func=AF.Exp),
              reads=["ps_h2"], writes=["t_lrev"])
            for j in range(8):
                mm(ps_h1[:, j:j + 1], t_ghi[:, j * C:(j + 1) * C], c_ones[:, 0:1],
                   j == 0, False, ["t_ghi", "c_ones"], ["ps_h1"])
                mm(ps_h1[:, j:j + 1], t_glo[:, j * C:(j + 1) * C], c_ones[:, 0:1],
                   False, j == 7, ["t_glo", "c_ones"], ["ps_h1"])
            R("scalar", lambda e: e.activation(out=t_dc, in_=ps_h1[:, 0:8], func=AF.Exp),
              reads=["ps_h1"], writes=["t_dc"])
            for nh in range(2):
                sl = slice(nh * 512, (nh + 1) * 512)
                mm(ps_w2[0:1, sl], c_ones[:, 0:1], t_ghi[:, sl], True, False,
                   ["c_ones", "t_ghi"], ["ps_w2"])
                mm(ps_w2[0:1, sl], c_ones[:, 0:1], t_glo[:, sl], False, True,
                   ["c_ones", "t_glo"], ["ps_w2"])
            R("vector", lambda e: e.tensor_add(s_bprow, s_bprow, ps_w2[0:1, :]),
              reads=["s_bprow", "ps_w2"], writes=["s_bprow"])
            R("vector", lambda e: e.tensor_copy(s_bphi, s_bprow),
              reads=["s_bprow"], writes=["s_bphi"])
            R("vector", lambda e: e.tensor_sub(s_bplo, s_bprow, s_bphi),
              reads=["s_bprow", "s_bphi"], writes=["s_bplo"])
            for j in range(8):
                for nh in range(2):
                    sl = slice(nh * 512, (nh + 1) * 512)
                    mm(ps_w1[:, sl], t_xnT[:, j * C:(j + 1) * C],
                       w_q[:, j * D + nh * 512: j * D + (nh + 1) * 512],
                       j == 0, j == 7, ["t_xnT", "w_q"], ["ps_w1"])
            R("vector", lambda e: e.tensor_mul(t_qh, ps_w1, t_lam),
              reads=["ps_w1", "t_lam"], writes=["t_qh"])
            R("vector", lambda e: e.tensor_mul(t_qg, ps_w1, t_lglob),
              reads=["ps_w1", "t_lglob"], writes=["t_qg"])
            for j in range(8):
                for nh in range(2):
                    sl = slice(nh * 512, (nh + 1) * 512)
                    mm(ps_w2[:, sl], t_xnT[:, j * C:(j + 1) * C],
                       w_k[:, j * D + nh * 512: j * D + (nh + 1) * 512],
                       j == 0, j == 7, ["t_xnT", "w_k"], ["ps_w2"])
            R("vector", lambda e: e.tensor_mul(t_kt, ps_w2, t_lami),
              reads=["ps_w2", "t_lami"], writes=["t_kt"])
            R("vector", lambda e: e.tensor_mul(t_kh, ps_w2, t_lrev),
              reads=["ps_w2", "t_lrev"], writes=["t_kh"])
            for half, pw in ((0, ps_w1), (1, ps_w2)):
                for j in range(8):
                    for nh in range(2):
                        mm(pw[:, nh * 512:(nh + 1) * 512], t_xnT[:, j * C:(j + 1) * C],
                           w_v[:, j * 2 * D + half * D + nh * 512: j * 2 * D + half * D + (nh + 1) * 512],
                           j == 0, j == 7, ["t_xnT", "w_v"],
                           ["ps_w1" if half == 0 else "ps_w2"])
            R("scalar", lambda e: e.copy(t_v[:, 0:D], ps_w1), reads=["ps_w1"], writes=["t_v"])
            R("scalar", lambda e: e.copy(t_v[:, D:2 * D], ps_w2), reads=["ps_w2"], writes=["t_v"])
            for j in range(8):
                tr(ps_tb1[:, j * C:(j + 1) * C], t_qh[:, j * C:(j + 1) * C], c_identb,
                   ["t_qh", "c_identb"], ["ps_tb1"])
            R("scalar", lambda e: e.copy(t_qhT, ps_tb1), reads=["ps_tb1"], writes=["t_qhT"])
            for j in range(8):
                tr(ps_tb2[:, j * C:(j + 1) * C], t_kt[:, j * C:(j + 1) * C], c_identb,
                   ["t_kt", "c_identb"], ["ps_tb2"])
            R("scalar", lambda e: e.copy(t_ktT, ps_tb2), reads=["ps_tb2"], writes=["t_ktT"])
            for j in range(8):
                tr(ps_tb1[:, j * C:(j + 1) * C], t_qg[:, j * C:(j + 1) * C], c_identb,
                   ["t_qg", "c_identb"], ["ps_tb1"])
            R("scalar", lambda e: e.copy(t_qgT, ps_tb1), reads=["ps_tb1"], writes=["t_qgT"])
            R("sync", lambda e, cc=c: e.dma_start(out=ap["qgT_d"][cc], in_=t_qgT),
              reads=["t_qgT"], writes=["d:qgT"], queue="qs")
            for h in range(4):
                for ji, j in enumerate((2 * h, 2 * h + 1)):
                    mm(ps_h1[:, 0:C], t_ktT[:, j * C:(j + 1) * C],
                       t_qhT[:, j * C:(j + 1) * C], ji == 0, ji == 1,
                       ["t_ktT", "t_qhT"], ["ps_h1"])
                R("vector", lambda e, hh=h: e.tensor_mul(t_pt[:, hh * C:(hh + 1) * C],
                                                         ps_h1[:, 0:C], c_mask),
                  reads=["ps_h1", "c_mask"], writes=["t_pt"])
                mm(ps_h2, t_pt[:, h * C:(h + 1) * C], t_v[:, h * DV:(h + 1) * DV],
                   True, False, ["t_pt", "t_v"], ["ps_h2"])
                for ji, j in enumerate((2 * h, 2 * h + 1)):
                    mm(ps_h2, t_qhT[:, j * C:(j + 1) * C],
                       s_Sb[:, j * DV:(j + 1) * DV], False, ji == 1,
                       ["t_qhT", "s_Sb"], ["ps_h2"])
                R("scalar", lambda e, hh=h: e.copy(t_o16[:, hh * DV:(hh + 1) * DV], ps_h2),
                  reads=["ps_h2"], writes=["t_o16"])
                for ih in range(2):
                    sg = 2 * h + ih
                    mm(ps_h1[:, 0:DV], t_kh[:, h * DK + ih * C: h * DK + (ih + 1) * C],
                       t_v[:, h * DV:(h + 1) * DV], True, True, ["t_kh", "t_v"], ["ps_h1"])
                    R("vector", lambda e, s=sg: e.scalar_tensor_tensor(
                        out=s_S[:, s * DV:(s + 1) * DV], in0=s_S[:, s * DV:(s + 1) * DV],
                        scalar=t_dc[:, s:s + 1], in1=ps_h1[:, 0:DV],
                        op0=ALU.mult, op1=ALU.add),
                      reads=["s_S", "t_dc", "ps_h1"], writes=["s_S"])
                    R("scalar", lambda e, s=sg: e.copy(s_Sb[:, s * DV:(s + 1) * DV],
                                                       s_S[:, s * DV:(s + 1) * DV]),
                      reads=["s_S"], writes=["s_Sb"])
            R("sync", lambda e, cc=c: e.dma_start(out=ap["o16_d"][cc], in_=t_o16),
              reads=["t_o16"], writes=["d:o16"], queue="qs")

        R("sync", lambda e: e.dma_start(out=ap["T_d"], in_=s_Sb),
          reads=["s_Sb"], writes=["d:T"], queue="qs")
        R("sync", lambda e: e.dma_start(out=ap["btot_d"], in_=s_bprow),
          reads=["s_bprow"], writes=["d:btot"], queue="qs")
        p.barrier()

        block = es.enter_context(nc.Block())
        p.finalize(es, block)
    return nc



def build_phase_b():
    nc = bass.Bass("TRN2", target_bir_lowering=False, debug=False)
    ap = {}
    for n, shp, dt in [("o16_d", [NCH, C, 2 * D], F16), ("qgT_d", [NCH, C, D], BF16),
                       ("base_d", [NCH, C, D], F32), ("xnT_d", [NCH, C, D], BF16),
                       ("T_all", [NCORES * C, 8 * DV], BF16), ("btot_all", [8, D], F32),
                       ("x0", [1, D], F32),
                       ("wg", [D, 2 * D], F32), ("wo", [2 * D, D], F32),
                       ("alay", [D, 512], F32), ("bwT", [512, D], F32),
                       ("wrT", [D, 8], F32), ("gnw", [128, 4], F32),
                       ("cmatx", [8, 8], BF16), ("wmrow", [1, 8], BF16),
                       ("conesb", [C, C], BF16),
                       ("cidentb", [C, C], BF16), ("cidentf", [C, C], F32),
                       ("cesel", [8, 512], BF16)]:
        ap[n] = nc.dram_tensor(n, shp, dt, kind="ExternalInput").ap()
    ap["y"] = nc.dram_tensor("y", [TOK, D], F32, kind="ExternalOutput").ap()

    es = ExitStack()
    with es:
        sb = lambda n, s, t: es.enter_context(nc.sbuf_tensor(n, s, t))[:]
        ps = lambda n, s, t: es.enter_context(nc.psum_tensor(n, s, t))[:]
        w_g = sb("w_g", [128, 8 * 2 * D], BF16)
        w_wo = sb("w_wo", [128, 16 * D], BF16)
        w_alay = sb("w_alay", [128, 8 * 512], BF16)
        w_bwT = sb("w_bwT", [128, 4 * D], BF16)
        w_wrT = sb("w_wrT", [128, 8 * 8], F32)
        s_gnw = sb("s_gnw", [128, 4], F32)
        c_identb = sb("c_identb", [C, C], BF16)
        c_identf = sb("c_identf", [C, C], F32)
        c_esel = sb("c_esel", [8, 512], BF16)
        c_cmat = sb("c_cmat", [8, 8], BF16)
        c_ones = sb("c_ones", [C, C], BF16)
        s_wmrow = sb("s_wmrow", [1, 8], BF16)
        s_x0 = sb("s_x0", [1, D], F32)
        s_x0b = sb("s_x0b", [1, D], F32)
        g_st = sb("g_st", [1, 8], F32)
        g_junk = sb("g_junk", [1, D], F32)
        s_x0T = sb("s_x0T", [128, 8], F32)
        g_lg8 = sb("g_lg8", [8, 1], F32)
        g_lg = sb("g_lg", [1, 8], F32)
        g_d = sb("g_d", [1, 8], F32)
        g_ind = sb("g_ind", [1, 8], F32)
        g_l2 = sb("g_l2", [1, 8], F32)
        g_keep = sb("g_keep", [1, 8], F32)
        g_e = sb("g_e", [1, 8], F32)
        g_gt = sb("g_gt", [1, 8], F32)
        g_gc8 = sb("g_gc8", [8, 1], F32)
        g_gcb = sb("g_gcb", [8, 4], BF16)
        s_gc = sb("s_gc", [128, 4], F32)
        b_bt = sb("b_bt", [8, D], F32)
        b_bthi = sb("b_bthi", [8, D], BF16)
        b_btlo = sb("b_btlo", [8, D], BF16)
        b_w = sb("b_w", [128, 64], F32)
        b_Tr = sb("b_Tr", [128, 8 * DV], BF16)
        s_Sin = sb("s_Sin", [128, 8 * DV], F32)
        s_Sinb = sb("s_Sinb", [128, 8 * DV], BF16)
        b_xnT = sb("b_xnT", [C, D], BF16)
        b_o16 = sb("b_o16", [C, 2 * D], F16)
        b_qgT = sb("b_qgT", [C, D], BF16)
        b_base = sb("b_base", [C, D], F32)
        b_ob = sb("b_ob", [C, 2 * D], BF16)
        b_rs = sb("b_rs", [C, 4], F32)
        b_sd = sb("b_sd", [C, 4], F32)
        b_rst = sb("b_rst", [C, 4], F32)
        b_junk = sb("b_junk", [C, DV], F32)
        b_sg = sb("b_sg", [C, 2 * D], BF16)
        b_t1 = sb("b_t1", [C, 2 * D], BF16)
        b_og = sb("b_og", [C, 2 * D], BF16)
        b_ogT = sb("b_ogT", [C, 2 * D], BF16)
        b_mid = sb("b_mid", [C, 512], BF16)
        b_midT = sb("b_midT", [C, 512], BF16)
        b_out = sb("b_out", [C, D], F32)
        ps_w1 = ps("ps_w1", [128, 1024], F32)
        ps_w2 = ps("ps_w2", [128, 1024], F32)
        ps_h1 = ps("ps_h1", [128, 512], F32)
        ps_h2 = ps("ps_h2", [128, 512], F32)
        ps_tb1 = ps("ps_tb1", [128, 1024], BF16)
        ps_tb2 = ps("ps_tb2", [128, 1024], BF16)

        p = Prog(nc)
        R = p.rec

        def mm(out, lhsT, rhs, start, stop, rd, wr):
            R("tensor", lambda e, o=out, l=lhsT, r=rhs, s1=start, s2=stop:
              e.matmul(o, l, r, start=s1, stop=s2), reads=rd, writes=wr)

        def tr(out, in_, ident, rd, wr):
            R("tensor", lambda e, o=out, i=in_, d=ident: e.transpose(o, i, d),
              reads=rd, writes=wr)

        def ldg(dst, src, rd, wr):
            R("gpsimd", lambda e, d=dst, s=src: e.dma_start(out=d, in_=s),
              reads=rd, writes=wr, queue="qg")

        for kt in range(8):
            ldg(w_g[:, kt * 2 * D:(kt + 1) * 2 * D],
                ap["wg"][kt * 128:(kt + 1) * 128, :], ["d:wg"], ["w_g"])
        for kt in range(16):
            ldg(w_wo[:, kt * D:(kt + 1) * D],
                ap["wo"][kt * 128:(kt + 1) * 128, :], ["d:wo"], ["w_wo"])
        ldg(w_alay.rearrange("p (kt n) -> p kt n", kt=8),
            ap["alay"].rearrange("(kt p) n -> p kt n", p=128), ["d:alay"], ["w_alay"])
        ldg(w_bwT.rearrange("p (kt n) -> p kt n", kt=4),
            ap["bwT"].rearrange("(kt p) n -> p kt n", p=128), ["d:bwT"], ["w_bwT"])
        R("sync", lambda e: e.dma_start(
            out=w_wrT.rearrange("p (kt n) -> p kt n", kt=8),
            in_=ap["wrT"].rearrange("(kt p) n -> p kt n", p=128)),
          reads=["d:wrT"], writes=["w_wrT"], queue="qs")
        ldg(c_identb, ap["cidentb"], ["d:cidentb"], ["c_identb"])
        ldg(c_esel, ap["cesel"], ["d:cesel"], ["c_esel"])
        ldg(c_cmat, ap["cmatx"], ["d:cmatx"], ["c_cmat"])
        ldg(c_ones, ap["conesb"], ["d:conesb"], ["c_ones"])
        ldg(s_wmrow, ap["wmrow"], ["d:wmrow"], ["s_wmrow"])
        R("sync", lambda e: e.dma_start(out=s_gnw, in_=ap["gnw"]),
          reads=["d:gnw"], writes=["s_gnw"], queue="qs")
        R("sync", lambda e: e.dma_start(out=c_identf, in_=ap["cidentf"]),
          reads=["d:cidentf"], writes=["c_identf"], queue="qs")
        R("sync", lambda e: e.dma_start(out=s_x0, in_=ap["x0"]),
          reads=["d:x0"], writes=["s_x0"], queue="qs")
        R("sync", lambda e: e.dma_start(out=b_bt, in_=ap["btot_all"]),
          reads=["d:btot"], writes=["b_bt"], queue="qs")

        R("vector", lambda e: e.reduce_sum(g_st[:, 0:1], s_x0, axis=AX.X),
          reads=["s_x0"], writes=["g_st"])
        R("scalar", lambda e: e.activation(out=g_junk, in_=s_x0, func=AF.Square,
                                           accum_out=g_st[:, 1:2]),
          reads=["s_x0"], writes=["g_junk", "g_st"])
        R("vector", lambda e: e.tensor_scalar(out=g_st[:, 2:3], in0=g_st[:, 0:1],
                                              scalar1=1.0 / D, scalar2=None, op0=ALU.mult),
          reads=["g_st"], writes=["g_st"])
        R("vector", lambda e: e.tensor_scalar(out=g_st[:, 3:4], in0=g_st[:, 1:2],
                                              scalar1=1.0 / D, scalar2=None, op0=ALU.mult),
          reads=["g_st"], writes=["g_st"])
        R("vector", lambda e: e.tensor_mul(g_st[:, 4:5], g_st[:, 2:3], g_st[:, 2:3]),
          reads=["g_st"], writes=["g_st"])
        R("vector", lambda e: e.tensor_sub(g_st[:, 4:5], g_st[:, 3:4], g_st[:, 4:5]),
          reads=["g_st"], writes=["g_st"])
        R("vector", lambda e: e.tensor_scalar(out=g_st[:, 4:5], in0=g_st[:, 4:5],
                                              scalar1=1.0, scalar2=EPS, op0=ALU.mult,
                                              op1=ALU.add),
          reads=["g_st"], writes=["g_st"])
        R("scalar", lambda e: e.activation(out=g_st[:, 5:6], in_=g_st[:, 4:5],
                                           func=AF.Sqrt),
          reads=["g_st"], writes=["g_st"])
        R("vector", lambda e: e.reciprocal(g_st[:, 6:7], g_st[:, 5:6]),
          reads=["g_st"], writes=["g_st"])
        R("vector", lambda e: e.tensor_scalar(out=s_x0b, in0=s_x0, scalar1=g_st[:, 2:3],
                                              scalar2=g_st[:, 6:7], op0=ALU.subtract,
                                              op1=ALU.mult),
          reads=["s_x0", "g_st"], writes=["s_x0b"])
        for j in range(8):
            tr(ps_h2[:, j:j + 1], s_x0b[:, j * C:(j + 1) * C], c_identf[0:1, 0:1],
               ["s_x0b", "c_identf"], ["ps_h2"])
        R("scalar", lambda e: e.copy(s_x0T, ps_h2[:, 0:8]), reads=["ps_h2"], writes=["s_x0T"])
        for j in range(8):
            mm(ps_h1[0:8, 0:1], w_wrT[:, j * 8:(j + 1) * 8], s_x0T[:, j:j + 1],
               j == 0, j == 7, ["w_wrT", "s_x0T"], ["ps_h1"])
        R("scalar", lambda e: e.copy(g_lg8, ps_h1[0:8, 0:1]), reads=["ps_h1"], writes=["g_lg8"])
        tr(ps_h2[0:1, 0:8], g_lg8, c_identf[0:8, 0:8], ["g_lg8", "c_identf"], ["ps_h2"])
        R("scalar", lambda e: e.copy(g_lg, ps_h2[0:1, 0:8]), reads=["ps_h2"], writes=["g_lg"])
        R("vector", lambda e: e.reduce_max(g_st[:, 7:8], g_lg, axis=AX.X),
          reads=["g_lg"], writes=["g_st"])
        R("vector", lambda e: e.tensor_scalar(out=g_d, in0=g_lg, scalar1=g_st[:, 7:8],
                                              scalar2=None, op0=ALU.subtract),
          reads=["g_lg", "g_st"], writes=["g_d"])
        R("vector", lambda e: e.tensor_scalar(out=g_ind, in0=g_d, scalar1=0.0,
                                              scalar2=None, op0=ALU.is_ge),
          reads=["g_d"], writes=["g_ind"])
        R("vector", lambda e: e.scalar_tensor_tensor(out=g_l2, in0=g_ind, scalar=-1e30,
                                                     in1=g_lg, op0=ALU.mult, op1=ALU.add),
          reads=["g_ind", "g_lg"], writes=["g_l2"])
        R("vector", lambda e: e.reduce_max(g_st[:, 0:1], g_l2, axis=AX.X),
          reads=["g_l2"], writes=["g_st"])
        R("vector", lambda e: e.tensor_scalar(out=g_keep, in0=g_lg, scalar1=g_st[:, 0:1],
                                              scalar2=None, op0=ALU.is_ge),
          reads=["g_lg", "g_st"], writes=["g_keep"])
        R("scalar", lambda e: e.activation(out=g_e, in_=g_d, func=AF.Exp),
          reads=["g_d"], writes=["g_e"])
        R("vector", lambda e: e.tensor_mul(g_e, g_e, g_keep),
          reads=["g_e", "g_keep"], writes=["g_e"])
        R("vector", lambda e: e.reduce_sum(g_st[:, 1:2], g_e, axis=AX.X),
          reads=["g_e"], writes=["g_st"])
        R("vector", lambda e: e.reciprocal(g_st[:, 2:3], g_st[:, 1:2]),
          reads=["g_st"], writes=["g_st"])
        R("vector", lambda e: e.tensor_scalar(out=g_gt, in0=g_e, scalar1=g_st[:, 2:3],
                                              scalar2=None, op0=ALU.mult),
          reads=["g_e", "g_st"], writes=["g_gt"])
        tr(ps_h2[0:8, 8:9], g_gt, c_identf[0:1, 0:1], ["g_gt", "c_identf"], ["ps_h2"])
        R("scalar", lambda e: e.copy(g_gc8, ps_h2[0:8, 8:9]), reads=["ps_h2"], writes=["g_gc8"])
        R("vector", lambda e: e.tensor_copy(g_gcb[:, 0:1], g_gc8),
          reads=["g_gc8"], writes=["g_gcb"])
        R("vector", lambda e: e.tensor_sub(g_gcb[:, 2:3], g_gc8, g_gcb[:, 0:1]),
          reads=["g_gc8", "g_gcb"], writes=["g_gcb"])
        for tau in range(4):
            for hl in range(2):
                mm(ps_h2[:, tau:tau + 1], c_esel[:, tau * 128:(tau + 1) * 128],
                   g_gcb[:, 2 * hl:2 * hl + 1], (tau == 0 and hl == 0), (tau == 3 and hl == 1),
                   ["c_esel", "g_gcb"], ["ps_h2"])
        R("scalar", lambda e: e.copy(s_gc, ps_h2[:, 0:4]), reads=["ps_h2"], writes=["s_gc"])
        for tau in range(4):
            R("vector", lambda e, t=tau: e.tensor_scalar(
                out=w_bwT[:, t * D:(t + 1) * D], in0=w_bwT[:, t * D:(t + 1) * D],
                scalar1=s_gc[:, t:t + 1], scalar2=None, op0=ALU.mult),
              reads=["w_bwT", "s_gc"], writes=["w_bwT"])
        for k in range(16):
            R("vector", lambda e, kk=k: e.tensor_scalar(
                out=w_wo[:, kk * D:(kk + 1) * D], in0=w_wo[:, kk * D:(kk + 1) * D],
                scalar1=s_gnw[:, kk % 4:kk % 4 + 1], scalar2=None, op0=ALU.mult),
              reads=["w_wo", "s_gnw"], writes=["w_wo"])

        R("vector", lambda e: e.tensor_copy(b_bthi, b_bt), reads=["b_bt"], writes=["b_bthi"])
        R("vector", lambda e: e.tensor_sub(b_btlo, b_bt, b_bthi),
          reads=["b_bt", "b_bthi"], writes=["b_btlo"])
        for tau in range(8):
            mm(ps_h1[:, tau * 8:(tau + 1) * 8], b_bthi[:, tau * 128:(tau + 1) * 128],
               c_cmat, tau == 0, False, ["b_bthi", "c_cmat"], ["ps_h1"])
            mm(ps_h1[:, tau * 8:(tau + 1) * 8], b_btlo[:, tau * 128:(tau + 1) * 128],
               c_cmat, False, False, ["b_btlo", "c_cmat"], ["ps_h1"])
            mm(ps_h1[:, tau * 8:(tau + 1) * 8], c_ones[0:1, :], s_wmrow,
               False, tau == 7, ["c_ones", "s_wmrow"], ["ps_h1"])
        R("scalar", lambda e: e.activation(out=b_w, in_=ps_h1[:, 0:64], func=AF.Exp),
          reads=["ps_h1"], writes=["b_w"])
        R("vector", lambda e: e.memset(s_Sin, 0.0), writes=["s_Sin"])
        for r in range(NCORES):
            R("sync", lambda e, rr=r: e.dma_start(out=b_Tr, in_=ap["T_all"][rr * C:(rr + 1) * C, :]),
              reads=["d:Tall"], writes=["b_Tr"], queue="qs")
            for sg in range(8):
                R("vector", lambda e, s=sg, rr=r: e.scalar_tensor_tensor(
                    out=s_Sin[:, s * DV:(s + 1) * DV], in0=b_Tr[:, s * DV:(s + 1) * DV],
                    scalar=b_w[:, s * 8 + rr:s * 8 + rr + 1], in1=s_Sin[:, s * DV:(s + 1) * DV],
                    op0=ALU.mult, op1=ALU.add),
                  reads=["b_Tr", "b_w", "s_Sin"], writes=["s_Sin"])
        R("scalar", lambda e: e.copy(s_Sinb, s_Sin), reads=["s_Sin"], writes=["s_Sinb"])

        for c in range(NCH):
            R("sync", lambda e, cc=c: e.dma_start(out=b_xnT, in_=ap["xnT_d"][cc]),
              reads=["d:xnT"], writes=["b_xnT"], queue="qs")
            R("sync", lambda e, cc=c: e.dma_start(out=b_o16, in_=ap["o16_d"][cc]),
              reads=["d:o16"], writes=["b_o16"], queue="qs")
            R("sync", lambda e, cc=c: e.dma_start(out=b_qgT, in_=ap["qgT_d"][cc]),
              reads=["d:qgT"], writes=["b_qgT"], queue="qs")
            R("sync", lambda e, cc=c: e.dma_start(out=b_base, in_=ap["base_d"][cc]),
              reads=["d:base"], writes=["b_base"], queue="qs")
            hbank = [ps_h1, ps_h2, ps_w1[:, 0:512], ps_w1[:, 512:1024]]
            hbank_n = ["ps_h1", "ps_h2", "ps_w1", "ps_w1"]
            for h in range(4):
                for ji, j in enumerate((2 * h, 2 * h + 1)):
                    mm(hbank[h], b_qgT[:, j * C:(j + 1) * C], s_Sinb[:, j * DV:(j + 1) * DV],
                       ji == 0, ji == 1, ["b_qgT", "s_Sinb"], [hbank_n[h]])
            for h in range(4):
                R("vector", lambda e, hh=h: e.tensor_tensor(
                    b_ob[:, hh * DV:(hh + 1) * DV], b_o16[:, hh * DV:(hh + 1) * DV],
                    hbank[hh], ALU.add),
                  reads=["b_o16", hbank_n[h]], writes=["b_ob"])
                R("scalar", lambda e, hh=h: e.activation(
                    out=b_junk, in_=b_ob[:, hh * DV:(hh + 1) * DV], func=AF.Square,
                    accum_out=b_rs[:, hh:hh + 1]),
                  reads=["b_ob"], writes=["b_junk", "b_rs"])
            R("vector", lambda e: e.tensor_scalar(out=b_rs, in0=b_rs, scalar1=1.0 / DV,
                                                  scalar2=EPS, op0=ALU.mult, op1=ALU.add),
              reads=["b_rs"], writes=["b_rs"])
            R("scalar", lambda e: e.activation(out=b_sd, in_=b_rs, func=AF.Sqrt),
              reads=["b_rs"], writes=["b_sd"])
            R("vector", lambda e: e.reciprocal(b_rst, b_sd), reads=["b_sd"], writes=["b_rst"])
            for half, pw, pwn in ((0, ps_w1, "ps_w1"), (1, ps_w2, "ps_w2")):
                for j in range(8):
                    for nh in range(2):
                        mm(pw[:, nh * 512:(nh + 1) * 512], b_xnT[:, j * C:(j + 1) * C],
                           w_g[:, j * 2 * D + half * D + nh * 512: j * 2 * D + half * D + (nh + 1) * 512],
                           j == 0, j == 7, ["b_xnT", "w_g"], [pwn])
            R("scalar", lambda e: e.activation(out=b_sg[:, 0:D], in_=ps_w1, func=AF.Silu),
              reads=["ps_w1"], writes=["b_sg"])
            R("scalar", lambda e: e.activation(out=b_sg[:, D:2 * D], in_=ps_w2, func=AF.Silu),
              reads=["ps_w2"], writes=["b_sg"])
            R("vector", lambda e: e.tensor_mul(b_t1, b_ob, b_sg),
              reads=["b_ob", "b_sg"], writes=["b_t1"])
            for h in range(4):
                R("vector", lambda e, hh=h: e.tensor_scalar(
                    out=b_og[:, hh * DV:(hh + 1) * DV], in0=b_t1[:, hh * DV:(hh + 1) * DV],
                    scalar1=b_rst[:, hh:hh + 1], scalar2=None, op0=ALU.mult),
                  reads=["b_t1", "b_rst"], writes=["b_og"])
            for j in range(8):
                tr(ps_tb1[:, j * C:(j + 1) * C], b_og[:, j * C:(j + 1) * C], c_identb,
                   ["b_og", "c_identb"], ["ps_tb1"])
            R("scalar", lambda e: e.copy(b_ogT[:, 0:D], ps_tb1), reads=["ps_tb1"], writes=["b_ogT"])
            for j in range(8):
                tr(ps_tb2[:, j * C:(j + 1) * C], b_og[:, D + j * C: D + (j + 1) * C], c_identb,
                   ["b_og", "c_identb"], ["ps_tb2"])
            R("scalar", lambda e: e.copy(b_ogT[:, D:2 * D], ps_tb2), reads=["ps_tb2"], writes=["b_ogT"])
            for nh in range(2):
                sl = slice(nh * 512, (nh + 1) * 512)
                for k in range(16):
                    mm(ps_w1[:, sl], b_ogT[:, k * C:(k + 1) * C],
                       w_wo[:, k * D + nh * 512: k * D + (nh + 1) * 512],
                       k == 0, False, ["b_ogT", "w_wo"], ["ps_w1"])
            for j in range(8):
                mm(ps_h1[:, 0:512], b_xnT[:, j * C:(j + 1) * C], w_alay[:, j * 512:(j + 1) * 512],
                   j == 0, j == 7, ["b_xnT", "w_alay"], ["ps_h1"])
            R("vector", lambda e: e.tensor_copy(b_mid, ps_h1), reads=["ps_h1"], writes=["b_mid"])
            for j in range(4):
                tr(ps_tb1[:, j * C:(j + 1) * C], b_mid[:, j * C:(j + 1) * C], c_identb,
                   ["b_mid", "c_identb"], ["ps_tb1"])
            R("scalar", lambda e: e.copy(b_midT, ps_tb1[:, 0:512]), reads=["ps_tb1"], writes=["b_midT"])
            for nh in range(2):
                sl = slice(nh * 512, (nh + 1) * 512)
                for j in range(4):
                    mm(ps_w1[:, sl], b_midT[:, j * C:(j + 1) * C],
                       w_bwT[:, j * D + nh * 512: j * D + (nh + 1) * 512],
                       False, j == 3, ["b_midT", "w_bwT"], ["ps_w1"])
            R("vector", lambda e: e.tensor_add(b_out, b_base, ps_w1),
              reads=["b_base", "ps_w1"], writes=["b_out"])
            R("sync", lambda e, cc=c: e.dma_start(out=ap["y"][cc * C:(cc + 1) * C, :], in_=b_out),
              reads=["b_out"], writes=["d:y"], queue="qs")

        p.barrier()
        block = es.enter_context(nc.Block())
        p.finalize(es, block)
    return nc



def _prep_shared(inputs):
    con = _tile_consts()
    w = {}
    w["wq"] = np.ascontiguousarray(inputs["L0_attn_Wq"], np.float32)
    w["wk"] = np.ascontiguousarray(inputs["L0_attn_Wk"], np.float32)
    w["wv"] = np.ascontiguousarray(inputs["L0_attn_Wv"], np.float32)
    w["wgk1"] = np.ascontiguousarray(inputs["L0_attn_Wgk1"], np.float32)
    w["wgk2"] = np.ascontiguousarray(inputs["L0_attn_Wgk2"], np.float32)
    w["bgk2"] = np.ascontiguousarray(inputs["L0_attn_bgk2"], np.float32).reshape(1, D)
    w["wg"] = np.ascontiguousarray(inputs["L0_attn_Wg"], np.float32)
    w["wo"] = np.ascontiguousarray(inputs["L0_attn_Wo"], np.float32)
    w["alay"] = np.ascontiguousarray(
        np.asarray(inputs["L0_ffn_A"], np.float32).reshape(512, D).T)
    w["bwT"] = np.ascontiguousarray(
        np.asarray(inputs["L0_ffn_Bw"], np.float32).transpose(0, 2, 1).reshape(512, D))
    w["wrT"] = np.ascontiguousarray(np.asarray(inputs["L0_ffn_Wr"], np.float32).T)
    w["gnw"] = np.ascontiguousarray(
        np.asarray(inputs["L0_attn_g_norm_weight"], np.float32).reshape(4, 128).T)
    x = np.asarray(inputs["L0_attn_x"], np.float32).reshape(NCORES * TOK, D)
    w["x0"] = np.ascontiguousarray(x[0:1, :])
    return con, w, x


def kernel(**inputs):
    con, w, x = _prep_shared(inputs)

    nc_a = build_phase_a()
    in_a = []
    for cidx in range(NCORES):
        m = dict(
            x=np.ascontiguousarray(x[cidx * TOK:(cidx + 1) * TOK]),
            wq=w["wq"], wk=w["wk"], wv=w["wv"], wgk1=w["wgk1"], wgk2=w["wgk2"],
            bgk2=w["bgk2"],
            ctrib=con["trib"], ctrir=con["trir"], cidentb=con["identb"],
            conesb=con["onesb"], cmask=con["mask"],
            cln16=np.full((C, 1), -LN16, np.float32),
        )
        in_a.append(m)
    res_a = run_bass_kernel_spmd(nc_a, in_a, core_ids=list(range(NCORES)))
    ra = res_a.results

    T_all = np.concatenate([np.asarray(ra[cidx]["T_d"]) for cidx in range(NCORES)], axis=0)
    btot_all = np.zeros((8, D), np.float32)
    for cidx in range(NCORES):
        btot_all[cidx] = np.asarray(ra[cidx]["btot_d"])[0]

    nc_b = build_phase_b()
    in_b = []
    for cidx in range(NCORES):
        m = dict(
            o16_d=np.asarray(ra[cidx]["o16_d"]),
            qgT_d=np.asarray(ra[cidx]["qgT_d"]),
            base_d=np.asarray(ra[cidx]["base_d"]),
            xnT_d=np.asarray(ra[cidx]["xnT_d"]),
            T_all=T_all, btot_all=btot_all,
            x0=w["x0"], wg=w["wg"], wo=w["wo"], alay=w["alay"], bwT=w["bwT"],
            wrT=w["wrT"], gnw=w["gnw"],
            cmatx=_cmat(cidx),
            wmrow=np.where(np.arange(8) >= cidx, -1e4, 0.0).astype(np.float32)
                    .astype(BF).reshape(1, 8),
            conesb=con["onesb"],
            cidentb=con["identb"], cidentf=con["identf"], cesel=con["esel"],
        )
        in_b.append(m)
    res_b = run_bass_kernel_spmd(nc_b, in_b, core_ids=list(range(NCORES)))
    y = np.concatenate([np.asarray(res_b.results[cidx]["y"]) for cidx in range(NCORES)],
                       axis=0)
    return y.reshape(1, NCORES * TOK, D)


# version 05
# speedup vs baseline: 1.0075x; 1.0075x over previous
import numpy as np
import ml_dtypes
from contextlib import ExitStack

import concourse.bass as bass
import concourse.mybir as mybir
from concourse.bass_utils import run_bass_kernel_spmd

F32 = mybir.dt.float32
BF16 = mybir.dt.bfloat16
F16 = mybir.dt.float16
AF = mybir.ActivationFunctionType
ALU = mybir.AluOpType
AX = mybir.AxisListType
BF = ml_dtypes.bfloat16

NCORES = 8
TOK = 2048
D = 1024
C = 128
NCH = TOK // C
HEADS = 4
DK = 256
DV = 512
EPS = 1e-5
LN16 = float(np.log(16.0))

ENGS = ("tensor", "vector", "scalar", "gpsimd", "sync")


class Prog:

    NSLOT = 4

    def __init__(self, nc):
        self.nc = nc
        self.ops = {e: [] for e in ENGS}
        self.nops = {e: 0 for e in ENGS}
        self.qcount = {}
        self.last_write = {}
        self.readers = {}
        self.last_psum = {}
        self.waited = {e: {} for e in ENGS}
        self.need_inc = {e: set() for e in ENGS}

    def _src_key(self, tok):
        return ("E", tok[1]) if tok[0] == "E" else ("Q",) + tuple(tok[1])

    def rec(self, eng, emit, reads=(), writes=(), queue=None, tag=""):
        deps = {}

        def add(tok):
            k = self._src_key(tok)
            v = tok[2]
            if k not in deps or deps[k] < v:
                deps[k] = v

        for r in reads:
            t = self.last_write.get(r)
            if t is not None:
                add(t)
            if r.startswith("ps_"):
                t2 = self.last_psum.get(r)
                if t2 is not None:
                    add(t2)
        for w in writes:
            t = self.last_write.get(w)
            if t is not None:
                add(t)
            for kk, vv in self.readers.get(w, {}).items():
                add(("E", kk[1], vv) if kk[0] == "E" else ("Q", (kk[1], kk[2]), vv))
            if w.startswith("ps_"):
                t2 = self.last_psum.get(w)
                if t2 is not None:
                    add(t2)

        if queue is not None:
            idx = None
            n = self.qcount.get(queue, 0) + 1
            self.qcount[queue] = n
            slot = (n - 1) % self.NSLOT
            gen = (n - 1) // self.NSLOT + 1
            my = ("Q", (queue, slot), gen)
            if gen > 1:
                add(("Q", (queue, slot), gen - 1))
        else:
            self.nops[eng] += 1
            idx = self.nops[eng]
            my = ("E", eng, idx)
        myk = self._src_key(my)

        waits = []
        for k, v in deps.items():
            if k[0] == "E" and k[1] == "tensor" and eng == "tensor":
                continue
            got = self.waited[eng].get(k, 0)
            if v <= got:
                continue
            waits.append((k, v))
            self.waited[eng][k] = v
            if k[0] == "E":
                self.need_inc[k[1]].add(v)

        for r in reads:
            self.readers.setdefault(r, {})[myk] = max(
                self.readers.get(r, {}).get(myk, 0), my[2]
            )
            if r.startswith("ps_"):
                self.last_psum[r] = my
        for w in writes:
            self.last_write[w] = my
            self.readers[w] = {}
            if w.startswith("ps_"):
                self.last_psum[w] = my

        self.ops[eng].append(
            dict(emit=emit, waits=waits, queue=queue, idx=idx, tag=tag)
        )
        return my

    def _queue_toks(self):
        toks = []
        for q, n in self.qcount.items():
            for slot in range(min(n, self.NSLOT)):
                gen = (n - 1 - slot) // self.NSLOT + 1
                toks.append(("Q", (q, slot), gen))
        return toks

    def barrier(self):
        toks = []
        for e in ENGS:
            if self.nops[e]:
                toks.append(("E", e, self.nops[e]))
        toks += self._queue_toks()
        for e in ENGS:
            for t in toks:
                k = self._src_key(t)
                if k == ("E", e):
                    continue
                v = t[2]
                if self.waited[e].get(k, 0) < v:
                    self.waited[e][k] = v
                    if k[0] == "E":
                        self.need_inc[k[1]].add(v)
                    self.ops[e].append(dict(emit=None, waits=[(k, v)], queue=None,
                                            idx=None, tag="barrier"))

    def finalize(self, es, block):
        sems = {}
        for e in ENGS:
            sems[("E", e)] = es.enter_context(self.nc.semaphore(f"sem_{e}"))
        for q in self.qcount:
            for slot in range(self.NSLOT):
                sems[("Q", q, slot)] = es.enter_context(
                    self.nc.semaphore(f"sem_{q}_{slot}"))
        inc_prefix = {}
        for e in ENGS:
            pref = {}
            cnt = 0
            marks = self.need_inc[e]
            for i in range(1, self.nops[e] + 1):
                if i in marks:
                    cnt += 1
                pref[i] = cnt
            inc_prefix[e] = pref

        def val_of(k, v):
            if k[0] == "Q":
                return 16 * v
            return inc_prefix[k[1]].get(v, 0)

        qseen = {}

        def body(e):
            def run(eng):
                for op in self.ops[e]:
                    for (k, v) in op["waits"]:
                        eng.wait_ge(sems[k], val_of(k, v))
                    if op["emit"] is None:
                        continue
                    inst = op["emit"](eng)
                    if op["queue"] is not None:
                        q = op["queue"]
                        n = qseen.get(q, 0)
                        qseen[q] = n + 1
                        inst.then_inc(sems[("Q", q, n % self.NSLOT)], 16)
                    elif op["idx"] in self.need_inc[e]:
                        inst.then_inc(sems[("E", e)], 1)
            return run

        for e in ENGS:
            if self.ops[e]:
                getattr(block, e)(body(e))


def _tile_consts():
    i = np.arange(C)
    trib = ((i[:, None] <= i[None, :]).astype(np.float32) / 16.0).astype(BF)
    trir = ((i[:, None] > i[None, :]).astype(np.float32) / 16.0).astype(BF)
    mask = (i[:, None] <= i[None, :]).astype(np.float32)
    identb = np.eye(C, dtype=np.float32).astype(BF)
    identf = np.eye(C, dtype=np.float32)
    onesb = np.ones((C, C), dtype=np.float32).astype(BF)
    esel = np.zeros((8, 512), dtype=np.float32)
    for p in range(512):
        esel[p // 64, p] = 1.0
    return dict(trib=trib, trir=trir, mask=mask, identb=identb, identf=identf,
                onesb=onesb, esel=esel.astype(BF))


def _cmat(core):
    m_ = np.zeros((8, 8), dtype=np.float32)
    for r in range(8):
        for m in range(8):
            if r < m < core:
                m_[m, r] = 1.0
    for r in range(8):
        if r >= core:
            m_[:, r] = 0.0
    return m_.astype(BF)


def _wmask(core):
    m = np.zeros((8,), np.float32)
    m[:core] = 1.0
    return m



def build_phase_a():
    nc = bass.Bass("TRN2", target_bir_lowering=False, debug=False)
    ap = {}
    ap["x"] = nc.dram_tensor("x", [TOK, D], F32, kind="ExternalInput").ap()
    for n, shp in [("wq", [D, D]), ("wk", [D, D]), ("wv", [D, 2 * D]),
                   ("wgk1", [D, 16]), ("wgk2", [16, D]), ("bgk2", [1, D])]:
        ap[n] = nc.dram_tensor(n, shp, F32, kind="ExternalInput").ap()
    for n, shp in [("ctrib", [C, C]), ("ctrir", [C, C]), ("cidentb", [C, C]),
                   ("conesb", [C, C])]:
        ap[n] = nc.dram_tensor(n, shp, BF16, kind="ExternalInput").ap()
    ap["cmask"] = nc.dram_tensor("cmask", [C, C], F32, kind="ExternalInput").ap()
    ap["cln16"] = nc.dram_tensor("cln16", [C, 1], F32, kind="ExternalInput").ap()
    ap["o16_d"] = nc.dram_tensor("o16_d", [NCH, C, 2 * D], F16, kind="ExternalOutput").ap()
    ap["qgT_d"] = nc.dram_tensor("qgT_d", [NCH, C, D], BF16, kind="ExternalOutput").ap()
    ap["base_d"] = nc.dram_tensor("base_d", [NCH, C, D], F32, kind="ExternalOutput").ap()
    ap["xnT_d"] = nc.dram_tensor("xnT_d", [NCH, C, D], BF16, kind="ExternalOutput").ap()
    ap["T_d"] = nc.dram_tensor("T_d", [C, 8 * DV], BF16, kind="ExternalOutput").ap()
    ap["btot_d"] = nc.dram_tensor("btot_d", [1, D], F32, kind="ExternalOutput").ap()

    es = ExitStack()
    with es:
        def sb(n, s, t):
            return es.enter_context(nc.sbuf_tensor(n, s, t))[:]

        def psm(n, s, t):
            return es.enter_context(nc.psum_tensor(n, s, t))[:]

        w_q = sb("w_q", [128, 8 * D], BF16)
        w_k = sb("w_k", [128, 8 * D], BF16)
        w_v = sb("w_v", [128, 8 * 2 * D], BF16)
        w_gk1 = sb("w_gk1", [128, 8 * 16], BF16)
        w_gk2 = sb("w_gk2", [16, D], BF16)
        s_bgk2 = sb("s_bgk2", [1, D], BF16)
        c_trib = sb("c_trib", [C, C], BF16)
        c_trir = sb("c_trir", [C, C], BF16)
        c_identb = sb("c_identb", [C, C], BF16)
        c_ones = sb("c_ones", [C, C], BF16)
        c_mask = sb("c_mask", [C, C], F32)
        c_ln16 = sb("c_ln16", [C, 1], F32)
        s_S = sb("s_S", [128, 8 * DV], F32)
        s_Sb = sb("s_Sb", [128, 8 * DV], BF16)
        s_bprow = sb("s_bprow", [1, D], F32)
        s_bphi = sb("s_bphi", [1, D], BF16)
        s_bplo = sb("s_bplo", [1, D], BF16)

        TS = []
        for par in range(2):
            t = {}
            su = f"_{par}"
            t["x"] = sb("t_x" + su, [C, D], F32)
            t["st"] = sb("t_st" + su, [C, 8], F32)
            t["xn"] = sb("t_xn" + su, [C, D], F32)
            t["xnb"] = sb("t_xnb" + su, [C, D], BF16)
            t["xnT"] = sb("t_xnT" + su, [C, D], BF16)
            t["f1"] = sb("t_f1" + su, [C, D], F32)
            t["ghi"] = sb("t_ghi" + su, [C, D], BF16)
            t["glo"] = sb("t_glo" + su, [C, D], BF16)
            t["z1e"] = sb("t_z1e" + su, [C, 16], BF16)
            t["z1b"] = sb("t_z1b" + su, [16, C], BF16)
            t["lam"] = sb("t_lam" + su, [C, D], BF16)
            t["lami"] = sb("t_lami" + su, [C, D], BF16)
            t["lrev"] = sb("t_lrev" + su, [C, D], BF16)
            t["lglob"] = sb("t_lglob" + su, [C, D], BF16)
            t["qh"] = sb("t_qh" + su, [C, D], BF16)
            t["qg"] = sb("t_qg" + su, [C, D], BF16)
            t["kt"] = sb("t_kt" + su, [C, D], BF16)
            t["kh"] = sb("t_kh" + su, [C, D], BF16)
            t["v"] = sb("t_v" + su, [C, 2 * D], BF16)
            t["qhT"] = sb("t_qhT" + su, [C, D], BF16)
            t["ktT"] = sb("t_ktT" + su, [C, D], BF16)
            t["qgT"] = sb("t_qgT" + su, [C, D], BF16)
            t["pt"] = sb("t_pt" + su, [C, 4 * C], BF16)
            t["o16"] = sb("t_o16" + su, [C, 2 * D], F16)
            t["dc"] = sb("t_dc" + su, [C, 8], F32)
            t["pw"] = psm("pw" + su, [128, 1024], F32)
            t["ph"] = psm("ph" + su, [128, 512], F32)
            t["ptb"] = psm("ptb" + su, [128, 1024], BF16)
            TS.append(t)

        p = Prog(nc)
        R = p.rec

        def ld(engq, dst, src, rd, wr):
            R(engq[0], lambda e, d=dst, s=src: e.dma_start(out=d, in_=s),
              reads=rd, writes=wr, queue=engq[1])

        gq = ("gpsimd", "qg")
        sq = ("sync", "qs")
        for kt in range(8):
            ld(gq, w_q[:, kt * D:(kt + 1) * D],
               ap["wq"][kt * 128:(kt + 1) * 128, :], ["d:wq"], ["w_q"])
            ld(gq, w_k[:, kt * D:(kt + 1) * D],
               ap["wk"][kt * 128:(kt + 1) * 128, :], ["d:wk"], ["w_k"])
            ld(gq, w_v[:, kt * 2 * D:(kt + 1) * 2 * D],
               ap["wv"][kt * 128:(kt + 1) * 128, :], ["d:wv"], ["w_v"])
        ld(gq, w_gk1.rearrange("p (kt n) -> p kt n", kt=8),
           ap["wgk1"].rearrange("(kt p) n -> p kt n", p=128), ["d:wgk1"], ["w_gk1"])
        ld(gq, w_gk2, ap["wgk2"], ["d:wgk2"], ["w_gk2"])
        ld(gq, s_bgk2, ap["bgk2"], ["d:bgk2"], ["s_bgk2"])
        ld(gq, c_trib, ap["ctrib"], ["d:ctrib"], ["c_trib"])
        ld(gq, c_trir, ap["ctrir"], ["d:ctrir"], ["c_trir"])
        ld(gq, c_identb, ap["cidentb"], ["d:cidentb"], ["c_identb"])
        ld(gq, c_ones, ap["conesb"], ["d:conesb"], ["c_ones"])
        ld(sq, c_mask, ap["cmask"], ["d:cmask"], ["c_mask"])
        ld(sq, c_ln16, ap["cln16"], ["d:cln16"], ["c_ln16"])
        R("vector", lambda e: e.memset(s_S, 0.0), writes=["s_S"])
        R("vector", lambda e: e.memset(s_Sb, 0.0), writes=["s_Sb"])
        R("vector", lambda e: e.memset(s_bprow, 0.0), writes=["s_bprow"])
        R("vector", lambda e: e.memset(s_bphi, 0.0), writes=["s_bphi"])
        R("vector", lambda e: e.memset(s_bplo, 0.0), writes=["s_bplo"])

        def mm(out, lhsT, rhs, start, stop, rd, wr):
            R("tensor", lambda e, o=out, l=lhsT, r=rhs, s1=start, s2=stop:
              e.matmul(o, l, r, start=s1, stop=s2), reads=rd, writes=wr)

        def tr(out, in_, ident, rd, wr):
            R("tensor", lambda e, o=out, i=in_, d=ident: e.transpose(o, i, d),
              reads=rd, writes=wr)

        for c in range(NCH):
            par = c % 2
            t = TS[par]
            s = f"_{par}"
            pw, ph, ptb = t["pw"], t["ph"], t["ptb"]
            PW, PH, PTB = "ps_pw" + s, "ps_ph" + s, "ps_ptb" + s
            xc = ap["x"][c * C:(c + 1) * C, :]
            R("sync", lambda e, sl=xc, d=t["x"]: e.dma_start(out=d, in_=sl),
              reads=["d:x"], writes=["t_x" + s], queue="qs")
            R("vector", lambda e, t=t: e.reduce_sum(t["st"][:, 0:1], t["x"], axis=AX.X),
              reads=["t_x" + s], writes=["t_st" + s])
            R("scalar", lambda e, t=t: e.activation(out=t["f1"], in_=t["x"], func=AF.Square,
                                                    accum_out=t["st"][:, 1:2]),
              reads=["t_x" + s], writes=["t_f1" + s, "t_st" + s])
            R("vector", lambda e, t=t: e.tensor_scalar(out=t["st"][:, 2:3], in0=t["st"][:, 0:1],
                                                       scalar1=1.0 / D, scalar2=None, op0=ALU.mult),
              reads=["t_st" + s], writes=["t_st" + s])
            R("vector", lambda e, t=t: e.tensor_scalar(out=t["st"][:, 3:4], in0=t["st"][:, 1:2],
                                                       scalar1=1.0 / D, scalar2=None, op0=ALU.mult),
              reads=["t_st" + s], writes=["t_st" + s])
            R("vector", lambda e, t=t: e.tensor_mul(t["st"][:, 4:5], t["st"][:, 2:3], t["st"][:, 2:3]),
              reads=["t_st" + s], writes=["t_st" + s])
            R("vector", lambda e, t=t: e.tensor_sub(t["st"][:, 4:5], t["st"][:, 3:4], t["st"][:, 4:5]),
              reads=["t_st" + s], writes=["t_st" + s])
            R("vector", lambda e, t=t: e.tensor_scalar(out=t["st"][:, 4:5], in0=t["st"][:, 4:5],
                                                       scalar1=1.0, scalar2=EPS, op0=ALU.mult,
                                                       op1=ALU.add),
              reads=["t_st" + s], writes=["t_st" + s])
            R("scalar", lambda e, t=t: e.activation(out=t["st"][:, 5:6], in_=t["st"][:, 4:5],
                                                    func=AF.Sqrt),
              reads=["t_st" + s], writes=["t_st" + s])
            R("vector", lambda e, t=t: e.reciprocal(t["st"][:, 6:7], t["st"][:, 5:6]),
              reads=["t_st" + s], writes=["t_st" + s])
            R("vector", lambda e, t=t: e.tensor_scalar(out=t["xn"], in0=t["x"],
                                                       scalar1=t["st"][:, 2:3],
                                                       scalar2=t["st"][:, 6:7],
                                                       op0=ALU.subtract, op1=ALU.mult),
              reads=["t_x" + s, "t_st" + s], writes=["t_xn" + s])
            R("vector", lambda e, t=t: e.tensor_copy(t["xnb"], t["xn"]),
              reads=["t_xn" + s], writes=["t_xnb" + s])
            R("vector", lambda e, t=t: e.tensor_add(t["x"], t["x"], t["xn"]),
              reads=["t_x" + s, "t_xn" + s], writes=["t_x" + s])
            R("sync", lambda e, cc=c, t=t: e.dma_start(out=ap["base_d"][cc], in_=t["x"]),
              reads=["t_x" + s], writes=["d:base"], queue="qs")
            for j in range(8):
                tr(ptb[:, j * C:(j + 1) * C], t["xnb"][:, j * C:(j + 1) * C], c_identb,
                   ["t_xnb" + s, "c_identb"], [PTB])
            R("scalar", lambda e, t=t: e.copy(t["xnT"], t["ptb"]), reads=[PTB], writes=["t_xnT" + s])
            R("sync", lambda e, cc=c, t=t: e.dma_start(out=ap["xnT_d"][cc], in_=t["xnT"]),
              reads=["t_xnT" + s], writes=["d:xnT"], queue="qs")
            for j in range(8):
                mm(ph[:, 0:16], t["xnT"][:, j * C:(j + 1) * C],
                   w_gk1[:, j * 16:(j + 1) * 16], j == 0, j == 7,
                   ["t_xnT" + s, "w_gk1"], [PH])
            R("vector", lambda e, t=t: e.tensor_copy(t["z1e"], t["ph"][:, 0:16]),
              reads=[PH], writes=["t_z1e" + s])
            tr(ptb[0:16, 0:C], t["z1e"], c_identb, ["t_z1e" + s, "c_identb"], [PTB])
            R("scalar", lambda e, t=t: e.copy(t["z1b"], t["ptb"][0:16, 0:C]),
              reads=[PTB], writes=["t_z1b" + s])
            for nh in range(2):
                sl = slice(nh * 512, (nh + 1) * 512)
                mm(pw[:, sl], t["z1b"], w_gk2[:, sl], True, False,
                   ["t_z1b" + s, "w_gk2"], [PW])
                mm(pw[:, sl], c_ones[0:1, :], s_bgk2[:, sl], False, True,
                   ["c_ones", "s_bgk2"], [PW])
            R("scalar", lambda e, t=t: e.activation(out=t["f1"], in_=t["pw"], func=AF.Abs),
              reads=[PW], writes=["t_f1" + s])
            R("scalar", lambda e, t=t: e.activation(out=t["f1"], in_=t["f1"], func=AF.Exp,
                                                    scale=-1.0),
              reads=["t_f1" + s], writes=["t_f1" + s])
            R("scalar", lambda e, t=t: e.activation(out=t["f1"], in_=t["f1"], func=AF.Ln,
                                                    bias=1.0),
              reads=["t_f1" + s], writes=["t_f1" + s])
            R("vector", lambda e, t=t: e.tensor_scalar(out=t["xn"], in0=t["pw"], scalar1=0.0,
                                                       scalar2=None, op0=ALU.min),
              reads=[PW], writes=["t_xn" + s])
            R("vector", lambda e, t=t: e.tensor_sub(t["xn"], t["xn"], t["f1"]),
              reads=["t_xn" + s, "t_f1" + s], writes=["t_xn" + s])
            R("vector", lambda e, t=t: e.tensor_copy(t["ghi"], t["xn"]),
              reads=["t_xn" + s], writes=["t_ghi" + s])
            R("vector", lambda e, t=t: e.tensor_sub(t["glo"], t["xn"], t["ghi"]),
              reads=["t_xn" + s, "t_ghi" + s], writes=["t_glo" + s])
            for nh in range(2):
                sl = slice(nh * 512, (nh + 1) * 512)
                mm(pw[:, sl], c_trib, t["ghi"][:, sl], True, False,
                   ["c_trib", "t_ghi" + s], [PW])
                mm(pw[:, sl], c_trib, t["glo"][:, sl], False, True,
                   ["c_trib", "t_glo" + s], [PW])
            R("scalar", lambda e, t=t: e.activation(out=t["lam"], in_=t["pw"], func=AF.Exp,
                                                    bias=c_ln16),
              reads=[PW, "c_ln16"], writes=["t_lam" + s])
            R("scalar", lambda e, t=t: e.activation(out=t["lami"], in_=t["pw"], func=AF.Exp,
                                                    scale=-1.0),
              reads=[PW], writes=["t_lami" + s])
            for nh in range(2):
                sl = slice(nh * 512, (nh + 1) * 512)
                mm(pw[:, sl], c_trib, t["ghi"][:, sl], True, False,
                   ["c_trib", "t_ghi" + s], [PW])
                mm(pw[:, sl], c_trib, t["glo"][:, sl], False, False,
                   ["c_trib", "t_glo" + s], [PW])
                mm(pw[:, sl], c_ones[0:1, :], s_bphi[:, sl], False, False,
                   ["c_ones", "s_bphi"], [PW])
                mm(pw[:, sl], c_ones[0:1, :], s_bplo[:, sl], False, True,
                   ["c_ones", "s_bplo"], [PW])
            R("scalar", lambda e, t=t: e.activation(out=t["lglob"], in_=t["pw"], func=AF.Exp,
                                                    bias=c_ln16),
              reads=[PW, "c_ln16"], writes=["t_lglob" + s])
            for nh in range(2):
                sl = slice(nh * 512, (nh + 1) * 512)
                mm(pw[:, sl], c_trir, t["ghi"][:, sl], True, False,
                   ["c_trir", "t_ghi" + s], [PW])
                mm(pw[:, sl], c_trir, t["glo"][:, sl], False, True,
                   ["c_trir", "t_glo" + s], [PW])
            R("scalar", lambda e, t=t: e.activation(out=t["lrev"], in_=t["pw"], func=AF.Exp),
              reads=[PW], writes=["t_lrev" + s])
            for j in range(8):
                mm(ph[:, j:j + 1], t["ghi"][:, j * C:(j + 1) * C], c_ones[:, 0:1],
                   j == 0, False, ["t_ghi" + s, "c_ones"], [PH])
                mm(ph[:, j:j + 1], t["glo"][:, j * C:(j + 1) * C], c_ones[:, 0:1],
                   False, j == 7, ["t_glo" + s, "c_ones"], [PH])
            R("scalar", lambda e, t=t: e.activation(out=t["dc"], in_=t["ph"][:, 0:8], func=AF.Exp),
              reads=[PH], writes=["t_dc" + s])
            for nh in range(2):
                sl = slice(nh * 512, (nh + 1) * 512)
                mm(pw[0:1, sl], c_ones[:, 0:1], t["ghi"][:, sl], True, False,
                   ["c_ones", "t_ghi" + s], [PW])
                mm(pw[0:1, sl], c_ones[:, 0:1], t["glo"][:, sl], False, True,
                   ["c_ones", "t_glo" + s], [PW])
            R("vector", lambda e, t=t: e.tensor_add(s_bprow, s_bprow, t["pw"][0:1, :]),
              reads=["s_bprow", PW], writes=["s_bprow"])
            R("vector", lambda e: e.tensor_copy(s_bphi, s_bprow),
              reads=["s_bprow"], writes=["s_bphi"])
            R("vector", lambda e: e.tensor_sub(s_bplo, s_bprow, s_bphi),
              reads=["s_bprow", "s_bphi"], writes=["s_bplo"])
            for j in range(8):
                for nh in range(2):
                    sl = slice(nh * 512, (nh + 1) * 512)
                    mm(pw[:, sl], t["xnT"][:, j * C:(j + 1) * C],
                       w_q[:, j * D + nh * 512: j * D + (nh + 1) * 512],
                       j == 0, j == 7, ["t_xnT" + s, "w_q"], [PW])
            R("vector", lambda e, t=t: e.tensor_mul(t["qh"], t["pw"], t["lam"]),
              reads=[PW, "t_lam" + s], writes=["t_qh" + s])
            R("vector", lambda e, t=t: e.tensor_mul(t["qg"], t["pw"], t["lglob"]),
              reads=[PW, "t_lglob" + s], writes=["t_qg" + s])
            for j in range(8):
                for nh in range(2):
                    sl = slice(nh * 512, (nh + 1) * 512)
                    mm(pw[:, sl], t["xnT"][:, j * C:(j + 1) * C],
                       w_k[:, j * D + nh * 512: j * D + (nh + 1) * 512],
                       j == 0, j == 7, ["t_xnT" + s, "w_k"], [PW])
            R("vector", lambda e, t=t: e.tensor_mul(t["kt"], t["pw"], t["lami"]),
              reads=[PW, "t_lami" + s], writes=["t_kt" + s])
            R("vector", lambda e, t=t: e.tensor_mul(t["kh"], t["pw"], t["lrev"]),
              reads=[PW, "t_lrev" + s], writes=["t_kh" + s])
            for half in range(2):
                for j in range(8):
                    for nh in range(2):
                        mm(pw[:, nh * 512:(nh + 1) * 512], t["xnT"][:, j * C:(j + 1) * C],
                           w_v[:, j * 2 * D + half * D + nh * 512:
                               j * 2 * D + half * D + (nh + 1) * 512],
                           j == 0, j == 7, ["t_xnT" + s, "w_v"], [PW])
                R("scalar", lambda e, t=t, h=half: e.copy(t["v"][:, h * D:(h + 1) * D], t["pw"]),
                  reads=[PW], writes=["t_v" + s])
            for j in range(8):
                tr(ptb[:, j * C:(j + 1) * C], t["qh"][:, j * C:(j + 1) * C], c_identb,
                   ["t_qh" + s, "c_identb"], [PTB])
            R("scalar", lambda e, t=t: e.copy(t["qhT"], t["ptb"]), reads=[PTB], writes=["t_qhT" + s])
            for j in range(8):
                tr(ptb[:, j * C:(j + 1) * C], t["kt"][:, j * C:(j + 1) * C], c_identb,
                   ["t_kt" + s, "c_identb"], [PTB])
            R("scalar", lambda e, t=t: e.copy(t["ktT"], t["ptb"]), reads=[PTB], writes=["t_ktT" + s])
            for j in range(8):
                tr(ptb[:, j * C:(j + 1) * C], t["qg"][:, j * C:(j + 1) * C], c_identb,
                   ["t_qg" + s, "c_identb"], [PTB])
            R("scalar", lambda e, t=t: e.copy(t["qgT"], t["ptb"]), reads=[PTB], writes=["t_qgT" + s])
            R("sync", lambda e, cc=c, t=t: e.dma_start(out=ap["qgT_d"][cc], in_=t["qgT"]),
              reads=["t_qgT" + s], writes=["d:qgT"], queue="qs")
            for h in range(4):
                for ji, j in enumerate((2 * h, 2 * h + 1)):
                    mm(ph[:, 0:C], t["ktT"][:, j * C:(j + 1) * C],
                       t["qhT"][:, j * C:(j + 1) * C], ji == 0, ji == 1,
                       ["t_ktT" + s, "t_qhT" + s], [PH])
                R("vector", lambda e, t=t, hh=h: e.tensor_mul(
                    t["pt"][:, hh * C:(hh + 1) * C], t["ph"][:, 0:C], c_mask),
                  reads=[PH, "c_mask"], writes=["t_pt" + s])
                mm(pw[:, 0:DV], t["pt"][:, h * C:(h + 1) * C], t["v"][:, h * DV:(h + 1) * DV],
                   True, False, ["t_pt" + s, "t_v" + s], [PW])
                for ji, j in enumerate((2 * h, 2 * h + 1)):
                    mm(pw[:, 0:DV], t["qhT"][:, j * C:(j + 1) * C],
                       s_Sb[:, j * DV:(j + 1) * DV], False, ji == 1,
                       ["t_qhT" + s, "s_Sb"], [PW])
                R("scalar", lambda e, t=t, hh=h: e.copy(t["o16"][:, hh * DV:(hh + 1) * DV],
                                                        t["pw"][:, 0:DV]),
                  reads=[PW], writes=["t_o16" + s])
                for ih in range(2):
                    sg = 2 * h + ih
                    mm(ph[:, 0:DV], t["kh"][:, h * DK + ih * C: h * DK + (ih + 1) * C],
                       t["v"][:, h * DV:(h + 1) * DV], True, True,
                       ["t_kh" + s, "t_v" + s], [PH])
                    R("vector", lambda e, t=t, g=sg: e.scalar_tensor_tensor(
                        out=s_S[:, g * DV:(g + 1) * DV], in0=s_S[:, g * DV:(g + 1) * DV],
                        scalar=t["dc"][:, g:g + 1], in1=t["ph"][:, 0:DV],
                        op0=ALU.mult, op1=ALU.add),
                      reads=["s_S", "t_dc" + s, PH], writes=["s_S"])
                    R("scalar", lambda e, g=sg: e.copy(s_Sb[:, g * DV:(g + 1) * DV],
                                                       s_S[:, g * DV:(g + 1) * DV]),
                      reads=["s_S"], writes=["s_Sb"])
            R("sync", lambda e, cc=c, t=t: e.dma_start(out=ap["o16_d"][cc], in_=t["o16"]),
              reads=["t_o16" + s], writes=["d:o16"], queue="qs")

        R("sync", lambda e: e.dma_start(out=ap["T_d"], in_=s_Sb),
          reads=["s_Sb"], writes=["d:T"], queue="qs")
        R("sync", lambda e: e.dma_start(out=ap["btot_d"], in_=s_bprow),
          reads=["s_bprow"], writes=["d:btot"], queue="qs")
        p.barrier()

        block = es.enter_context(nc.Block())
        p.finalize(es, block)
    return nc



def build_phase_b():
    nc = bass.Bass("TRN2", target_bir_lowering=False, debug=False)
    ap = {}
    for n, shp, dt in [("o16_d", [NCH, C, 2 * D], F16), ("qgT_d", [NCH, C, D], BF16),
                       ("base_d", [NCH, C, D], F32), ("xnT_d", [NCH, C, D], BF16),
                       ("T_all", [NCORES * C, 8 * DV], BF16), ("btot_all", [8, D], F32),
                       ("x0", [1, D], F32),
                       ("wg", [D, 2 * D], F32), ("wo", [2 * D, D], F32),
                       ("alay", [D, 512], F32), ("bwT", [512, D], F32),
                       ("wrT", [D, 8], F32), ("gnw", [128, 4], F32),
                       ("cmatx", [8, 8], BF16), ("wmrow", [1, 8], BF16),
                       ("conesb", [C, C], BF16),
                       ("cidentb", [C, C], BF16), ("cidentf", [C, C], F32),
                       ("cesel", [8, 512], BF16)]:
        ap[n] = nc.dram_tensor(n, shp, dt, kind="ExternalInput").ap()
    ap["y"] = nc.dram_tensor("y", [TOK, D], F32, kind="ExternalOutput").ap()

    es = ExitStack()
    with es:
        def sb(n, s, t):
            return es.enter_context(nc.sbuf_tensor(n, s, t))[:]

        def psm(n, s, t):
            return es.enter_context(nc.psum_tensor(n, s, t))[:]

        w_g = sb("w_g", [128, 8 * 2 * D], BF16)
        w_wo = sb("w_wo", [128, 16 * D], BF16)
        w_alay = sb("w_alay", [128, 8 * 512], BF16)
        w_bwT = sb("w_bwT", [128, 4 * D], BF16)
        w_wrT = sb("w_wrT", [128, 8 * 8], F32)
        s_gnw = sb("s_gnw", [128, 4], F32)
        c_identb = sb("c_identb", [C, C], BF16)
        c_identf = sb("c_identf", [C, C], F32)
        c_esel = sb("c_esel", [8, 512], BF16)
        c_cmat = sb("c_cmat", [8, 8], BF16)
        c_ones = sb("c_ones", [C, C], BF16)
        s_wmrow = sb("s_wmrow", [1, 8], BF16)
        s_x0 = sb("s_x0", [1, D], F32)
        s_x0b = sb("s_x0b", [1, D], F32)
        g_st = sb("g_st", [1, 8], F32)
        g_junk = sb("g_junk", [1, D], F32)
        s_x0T = sb("s_x0T", [128, 8], F32)
        g_lg8 = sb("g_lg8", [8, 1], F32)
        g_lg = sb("g_lg", [1, 8], F32)
        g_d = sb("g_d", [1, 8], F32)
        g_ind = sb("g_ind", [1, 8], F32)
        g_l2 = sb("g_l2", [1, 8], F32)
        g_keep = sb("g_keep", [1, 8], F32)
        g_e = sb("g_e", [1, 8], F32)
        g_gt = sb("g_gt", [1, 8], F32)
        g_gc8 = sb("g_gc8", [8, 1], F32)
        g_gcb = sb("g_gcb", [8, 4], BF16)
        s_gc = sb("s_gc", [128, 4], F32)
        b_bt = sb("b_bt", [8, D], F32)
        b_bthi = sb("b_bthi", [8, D], BF16)
        b_btlo = sb("b_btlo", [8, D], BF16)
        b_w = sb("b_w", [128, 64], F32)
        b_Tr = [sb("b_Tr_0", [128, 8 * DV], BF16), sb("b_Tr_1", [128, 8 * DV], BF16)]
        s_Sin = sb("s_Sin", [128, 8 * DV], F32)
        s_Sinb = sb("s_Sinb", [128, 8 * DV], BF16)

        TS = []
        for par in range(2):
            t = {}
            su = f"_{par}"
            t["xnT"] = sb("b_xnT" + su, [C, D], BF16)
            t["o16"] = sb("b_o16" + su, [C, 2 * D], F16)
            t["qgT"] = sb("b_qgT" + su, [C, D], BF16)
            t["base"] = sb("b_base" + su, [C, D], F32)
            t["ob"] = sb("b_ob" + su, [C, 2 * D], BF16)
            t["rs"] = sb("b_rs" + su, [C, 4], F32)
            t["sd"] = sb("b_sd" + su, [C, 4], F32)
            t["rst"] = sb("b_rst" + su, [C, 4], F32)
            t["junk"] = sb("b_junk" + su, [C, DV], F32)
            t["sg"] = sb("b_sg" + su, [C, 2 * D], BF16)
            t["t1"] = sb("b_t1" + su, [C, 2 * D], BF16)
            t["ogT"] = sb("b_ogT" + su, [C, 2 * D], BF16)
            t["mid"] = sb("b_mid" + su, [C, 512], BF16)
            t["midT"] = sb("b_midT" + su, [C, 512], BF16)
            t["pw"] = psm("pwB" + su, [128, 1024], F32)
            t["ph"] = psm("phB" + su, [128, 512], F32)
            t["ptb"] = psm("ptbB" + su, [128, 1024], BF16)
            TS.append(t)

        p = Prog(nc)
        R = p.rec

        def mm(out, lhsT, rhs, start, stop, rd, wr):
            R("tensor", lambda e, o=out, l=lhsT, r=rhs, s1=start, s2=stop:
              e.matmul(o, l, r, start=s1, stop=s2), reads=rd, writes=wr)

        def tr(out, in_, ident, rd, wr):
            R("tensor", lambda e, o=out, i=in_, d=ident: e.transpose(o, i, d),
              reads=rd, writes=wr)

        def ldg(dst, src, rd, wr):
            R("gpsimd", lambda e, d=dst, s=src: e.dma_start(out=d, in_=s),
              reads=rd, writes=wr, queue="qg")

        for kt in range(8):
            ldg(w_g[:, kt * 2 * D:(kt + 1) * 2 * D],
                ap["wg"][kt * 128:(kt + 1) * 128, :], ["d:wg"], ["w_g"])
        for kt in range(16):
            ldg(w_wo[:, kt * D:(kt + 1) * D],
                ap["wo"][kt * 128:(kt + 1) * 128, :], ["d:wo"], ["w_wo"])
        ldg(w_alay.rearrange("p (kt n) -> p kt n", kt=8),
            ap["alay"].rearrange("(kt p) n -> p kt n", p=128), ["d:alay"], ["w_alay"])
        ldg(w_bwT.rearrange("p (kt n) -> p kt n", kt=4),
            ap["bwT"].rearrange("(kt p) n -> p kt n", p=128), ["d:bwT"], ["w_bwT"])
        ldg(c_identb, ap["cidentb"], ["d:cidentb"], ["c_identb"])
        ldg(c_esel, ap["cesel"], ["d:cesel"], ["c_esel"])
        ldg(c_cmat, ap["cmatx"], ["d:cmatx"], ["c_cmat"])
        ldg(c_ones, ap["conesb"], ["d:conesb"], ["c_ones"])
        ldg(s_wmrow, ap["wmrow"], ["d:wmrow"], ["s_wmrow"])
        R("sync", lambda e: e.dma_start(
            out=w_wrT.rearrange("p (kt n) -> p kt n", kt=8),
            in_=ap["wrT"].rearrange("(kt p) n -> p kt n", p=128)),
          reads=["d:wrT"], writes=["w_wrT"], queue="qs")
        R("sync", lambda e: e.dma_start(out=s_gnw, in_=ap["gnw"]),
          reads=["d:gnw"], writes=["s_gnw"], queue="qs")
        R("sync", lambda e: e.dma_start(out=c_identf, in_=ap["cidentf"]),
          reads=["d:cidentf"], writes=["c_identf"], queue="qs")
        R("sync", lambda e: e.dma_start(out=s_x0, in_=ap["x0"]),
          reads=["d:x0"], writes=["s_x0"], queue="qs")
        R("sync", lambda e: e.dma_start(out=b_bt, in_=ap["btot_all"]),
          reads=["d:btot"], writes=["b_bt"], queue="qs")

        pw0, ph0, ptb0 = TS[0]["pw"], TS[0]["ph"], TS[0]["ptb"]
        PW0, PH0, PTB0 = "ps_pwB_0", "ps_phB_0", "ps_ptbB_0"
        R("vector", lambda e: e.reduce_sum(g_st[:, 0:1], s_x0, axis=AX.X),
          reads=["s_x0"], writes=["g_st"])
        R("scalar", lambda e: e.activation(out=g_junk, in_=s_x0, func=AF.Square,
                                           accum_out=g_st[:, 1:2]),
          reads=["s_x0"], writes=["g_junk", "g_st"])
        R("vector", lambda e: e.tensor_scalar(out=g_st[:, 2:3], in0=g_st[:, 0:1],
                                              scalar1=1.0 / D, scalar2=None, op0=ALU.mult),
          reads=["g_st"], writes=["g_st"])
        R("vector", lambda e: e.tensor_scalar(out=g_st[:, 3:4], in0=g_st[:, 1:2],
                                              scalar1=1.0 / D, scalar2=None, op0=ALU.mult),
          reads=["g_st"], writes=["g_st"])
        R("vector", lambda e: e.tensor_mul(g_st[:, 4:5], g_st[:, 2:3], g_st[:, 2:3]),
          reads=["g_st"], writes=["g_st"])
        R("vector", lambda e: e.tensor_sub(g_st[:, 4:5], g_st[:, 3:4], g_st[:, 4:5]),
          reads=["g_st"], writes=["g_st"])
        R("vector", lambda e: e.tensor_scalar(out=g_st[:, 4:5], in0=g_st[:, 4:5],
                                              scalar1=1.0, scalar2=EPS, op0=ALU.mult,
                                              op1=ALU.add),
          reads=["g_st"], writes=["g_st"])
        R("scalar", lambda e: e.activation(out=g_st[:, 5:6], in_=g_st[:, 4:5], func=AF.Sqrt),
          reads=["g_st"], writes=["g_st"])
        R("vector", lambda e: e.reciprocal(g_st[:, 6:7], g_st[:, 5:6]),
          reads=["g_st"], writes=["g_st"])
        R("vector", lambda e: e.tensor_scalar(out=s_x0b, in0=s_x0, scalar1=g_st[:, 2:3],
                                              scalar2=g_st[:, 6:7], op0=ALU.subtract,
                                              op1=ALU.mult),
          reads=["s_x0", "g_st"], writes=["s_x0b"])
        for j in range(8):
            tr(ph0[:, j:j + 1], s_x0b[:, j * C:(j + 1) * C], c_identf[0:1, 0:1],
               ["s_x0b", "c_identf"], [PH0])
        R("scalar", lambda e: e.copy(s_x0T, ph0[:, 0:8]), reads=[PH0], writes=["s_x0T"])
        for j in range(8):
            mm(pw0[0:8, 0:1], w_wrT[:, j * 8:(j + 1) * 8], s_x0T[:, j:j + 1],
               j == 0, j == 7, ["w_wrT", "s_x0T"], [PW0])
        R("scalar", lambda e: e.copy(g_lg8, pw0[0:8, 0:1]), reads=[PW0], writes=["g_lg8"])
        tr(ph0[0:1, 0:8], g_lg8, c_identf[0:8, 0:8], ["g_lg8", "c_identf"], [PH0])
        R("scalar", lambda e: e.copy(g_lg, ph0[0:1, 0:8]), reads=[PH0], writes=["g_lg"])
        R("vector", lambda e: e.reduce_max(g_st[:, 7:8], g_lg, axis=AX.X),
          reads=["g_lg"], writes=["g_st"])
        R("vector", lambda e: e.tensor_scalar(out=g_d, in0=g_lg, scalar1=g_st[:, 7:8],
                                              scalar2=None, op0=ALU.subtract),
          reads=["g_lg", "g_st"], writes=["g_d"])
        R("vector", lambda e: e.tensor_scalar(out=g_ind, in0=g_d, scalar1=0.0,
                                              scalar2=None, op0=ALU.is_ge),
          reads=["g_d"], writes=["g_ind"])
        R("vector", lambda e: e.scalar_tensor_tensor(out=g_l2, in0=g_ind, scalar=-1e30,
                                                     in1=g_lg, op0=ALU.mult, op1=ALU.add),
          reads=["g_ind", "g_lg"], writes=["g_l2"])
        R("vector", lambda e: e.reduce_max(g_st[:, 0:1], g_l2, axis=AX.X),
          reads=["g_l2"], writes=["g_st"])
        R("vector", lambda e: e.tensor_scalar(out=g_keep, in0=g_lg, scalar1=g_st[:, 0:1],
                                              scalar2=None, op0=ALU.is_ge),
          reads=["g_lg", "g_st"], writes=["g_keep"])
        R("scalar", lambda e: e.activation(out=g_e, in_=g_d, func=AF.Exp),
          reads=["g_d"], writes=["g_e"])
        R("vector", lambda e: e.tensor_mul(g_e, g_e, g_keep),
          reads=["g_e", "g_keep"], writes=["g_e"])
        R("vector", lambda e: e.reduce_sum(g_st[:, 1:2], g_e, axis=AX.X),
          reads=["g_e"], writes=["g_st"])
        R("vector", lambda e: e.reciprocal(g_st[:, 2:3], g_st[:, 1:2]),
          reads=["g_st"], writes=["g_st"])
        R("vector", lambda e: e.tensor_scalar(out=g_gt, in0=g_e, scalar1=g_st[:, 2:3],
                                              scalar2=None, op0=ALU.mult),
          reads=["g_e", "g_st"], writes=["g_gt"])
        tr(ph0[0:8, 8:9], g_gt, c_identf[0:1, 0:1], ["g_gt", "c_identf"], [PH0])
        R("scalar", lambda e: e.copy(g_gc8, ph0[0:8, 8:9]), reads=[PH0], writes=["g_gc8"])
        R("vector", lambda e: e.tensor_copy(g_gcb[:, 0:1], g_gc8),
          reads=["g_gc8"], writes=["g_gcb"])
        R("vector", lambda e: e.tensor_sub(g_gcb[:, 2:3], g_gc8, g_gcb[:, 0:1]),
          reads=["g_gc8", "g_gcb"], writes=["g_gcb"])
        for tau in range(4):
            for hl in range(2):
                mm(ph0[:, tau:tau + 1], c_esel[:, tau * 128:(tau + 1) * 128],
                   g_gcb[:, 2 * hl:2 * hl + 1], (tau == 0 and hl == 0),
                   (tau == 3 and hl == 1), ["c_esel", "g_gcb"], [PH0])
        R("scalar", lambda e: e.copy(s_gc, ph0[:, 0:4]), reads=[PH0], writes=["s_gc"])
        for tau in range(4):
            R("vector", lambda e, u=tau: e.tensor_scalar(
                out=w_bwT[:, u * D:(u + 1) * D], in0=w_bwT[:, u * D:(u + 1) * D],
                scalar1=s_gc[:, u:u + 1], scalar2=None, op0=ALU.mult),
              reads=["w_bwT", "s_gc"], writes=["w_bwT"])
        for k in range(16):
            R("vector", lambda e, kk=k: e.tensor_scalar(
                out=w_wo[:, kk * D:(kk + 1) * D], in0=w_wo[:, kk * D:(kk + 1) * D],
                scalar1=s_gnw[:, kk % 4:kk % 4 + 1], scalar2=None, op0=ALU.mult),
              reads=["w_wo", "s_gnw"], writes=["w_wo"])

        R("vector", lambda e: e.tensor_copy(b_bthi, b_bt), reads=["b_bt"], writes=["b_bthi"])
        R("vector", lambda e: e.tensor_sub(b_btlo, b_bt, b_bthi),
          reads=["b_bt", "b_bthi"], writes=["b_btlo"])
        for tau in range(8):
            mm(ph0[:, tau * 8:(tau + 1) * 8], b_bthi[:, tau * 128:(tau + 1) * 128],
               c_cmat, tau == 0, False, ["b_bthi", "c_cmat"], [PH0])
            mm(ph0[:, tau * 8:(tau + 1) * 8], b_btlo[:, tau * 128:(tau + 1) * 128],
               c_cmat, False, False, ["b_btlo", "c_cmat"], [PH0])
            mm(ph0[:, tau * 8:(tau + 1) * 8], c_ones[0:1, :], s_wmrow,
               False, tau == 7, ["c_ones", "s_wmrow"], [PH0])
        R("scalar", lambda e: e.activation(out=b_w, in_=ph0[:, 0:64], func=AF.Exp),
          reads=[PH0], writes=["b_w"])
        R("vector", lambda e: e.memset(s_Sin, 0.0), writes=["s_Sin"])
        for r in range(NCORES):
            tb = b_Tr[r % 2]
            tbn = f"b_Tr_{r % 2}"
            R("sync", lambda e, rr=r, d=tb: e.dma_start(
                out=d, in_=ap["T_all"][rr * C:(rr + 1) * C, :]),
              reads=["d:Tall"], writes=[tbn], queue="qs")
            for sg in range(8):
                R("vector", lambda e, g=sg, rr=r, d=tb: e.scalar_tensor_tensor(
                    out=s_Sin[:, g * DV:(g + 1) * DV], in0=d[:, g * DV:(g + 1) * DV],
                    scalar=b_w[:, g * 8 + rr:g * 8 + rr + 1],
                    in1=s_Sin[:, g * DV:(g + 1) * DV],
                    op0=ALU.mult, op1=ALU.add),
                  reads=[tbn, "b_w", "s_Sin"], writes=["s_Sin"])
        R("scalar", lambda e: e.copy(s_Sinb, s_Sin), reads=["s_Sin"], writes=["s_Sinb"])

        for c in range(NCH):
            par = c % 2
            t = TS[par]
            su = f"_{par}"
            pw, ph, ptb = t["pw"], t["ph"], t["ptb"]
            PW, PH, PTB = "ps_pwB" + su, "ps_phB" + su, "ps_ptbB" + su
            for nm, key, dt2 in [("xnT", "xnT_d", None), ("o16", "o16_d", None),
                                 ("qgT", "qgT_d", None), ("base", "base_d", None)]:
                R("sync", lambda e, cc=c, d=t[nm], k=key: e.dma_start(out=d, in_=ap[k][cc]),
                  reads=["d:" + key], writes=["b_" + nm + su], queue="qs")
            for half in range(2):
                for j in range(8):
                    for nh in range(2):
                        mm(pw[:, nh * 512:(nh + 1) * 512], t["xnT"][:, j * C:(j + 1) * C],
                           w_g[:, j * 2 * D + half * D + nh * 512:
                               j * 2 * D + half * D + (nh + 1) * 512],
                           j == 0, j == 7, ["b_xnT" + su, "w_g"], [PW])
                R("scalar", lambda e, t=t, h=half: e.activation(
                    out=t["sg"][:, h * D:(h + 1) * D], in_=t["pw"], func=AF.Silu),
                  reads=[PW], writes=["b_sg" + su])
            for j in range(8):
                mm(ph[:, 0:512], t["xnT"][:, j * C:(j + 1) * C],
                   w_alay[:, j * 512:(j + 1) * 512], j == 0, j == 7,
                   ["b_xnT" + su, "w_alay"], [PH])
            R("vector", lambda e, t=t: e.tensor_copy(t["mid"], t["ph"][:, 0:512]),
              reads=[PH], writes=["b_mid" + su])
            for j in range(4):
                tr(ptb[:, j * C:(j + 1) * C], t["mid"][:, j * C:(j + 1) * C], c_identb,
                   ["b_mid" + su, "c_identb"], [PTB])
            R("scalar", lambda e, t=t: e.copy(t["midT"], t["ptb"][:, 0:512]),
              reads=[PTB], writes=["b_midT" + su])
            for hp in range(2):
                for hi in range(2):
                    h = 2 * hp + hi
                    for ji, j in enumerate((2 * h, 2 * h + 1)):
                        mm(pw[:, hi * DV:(hi + 1) * DV], t["qgT"][:, j * C:(j + 1) * C],
                           s_Sinb[:, j * DV:(j + 1) * DV], ji == 0, ji == 1,
                           ["b_qgT" + su, "s_Sinb"], [PW])
                for hi in range(2):
                    h = 2 * hp + hi
                    R("vector", lambda e, t=t, hh=h, i=hi: e.tensor_tensor(
                        t["ob"][:, hh * DV:(hh + 1) * DV], t["o16"][:, hh * DV:(hh + 1) * DV],
                        t["pw"][:, i * DV:(i + 1) * DV], ALU.add),
                      reads=["b_o16" + su, PW], writes=["b_ob" + su])
            for h in range(4):
                R("scalar", lambda e, t=t, hh=h: e.activation(
                    out=t["junk"], in_=t["ob"][:, hh * DV:(hh + 1) * DV], func=AF.Square,
                    accum_out=t["rs"][:, hh:hh + 1]),
                  reads=["b_ob" + su], writes=["b_junk" + su, "b_rs" + su])
            R("vector", lambda e, t=t: e.tensor_scalar(out=t["rs"], in0=t["rs"],
                                                       scalar1=1.0 / DV, scalar2=EPS,
                                                       op0=ALU.mult, op1=ALU.add),
              reads=["b_rs" + su], writes=["b_rs" + su])
            R("scalar", lambda e, t=t: e.activation(out=t["sd"], in_=t["rs"], func=AF.Sqrt),
              reads=["b_rs" + su], writes=["b_sd" + su])
            R("vector", lambda e, t=t: e.reciprocal(t["rst"], t["sd"]),
              reads=["b_sd" + su], writes=["b_rst" + su])
            R("vector", lambda e, t=t: e.tensor_mul(t["t1"], t["ob"], t["sg"]),
              reads=["b_ob" + su, "b_sg" + su], writes=["b_t1" + su])
            for h in range(4):
                R("vector", lambda e, t=t, hh=h: e.tensor_scalar(
                    out=t["t1"][:, hh * DV:(hh + 1) * DV], in0=t["t1"][:, hh * DV:(hh + 1) * DV],
                    scalar1=t["rst"][:, hh:hh + 1], scalar2=None, op0=ALU.mult),
                  reads=["b_t1" + su, "b_rst" + su], writes=["b_t1" + su])
            for half in range(2):
                for j in range(8):
                    tr(ptb[:, j * C:(j + 1) * C], t["t1"][:, half * D + j * C: half * D + (j + 1) * C],
                       c_identb, ["b_t1" + su, "c_identb"], [PTB])
                R("scalar", lambda e, t=t, h=half: e.copy(t["ogT"][:, h * D:(h + 1) * D], t["ptb"]),
                  reads=[PTB], writes=["b_ogT" + su])
            for nh in range(2):
                sl = slice(nh * 512, (nh + 1) * 512)
                for k in range(16):
                    mm(pw[:, sl], t["ogT"][:, k * C:(k + 1) * C],
                       w_wo[:, k * D + nh * 512: k * D + (nh + 1) * 512],
                       k == 0, False, ["b_ogT" + su, "w_wo"], [PW])
                for j in range(4):
                    mm(pw[:, sl], t["midT"][:, j * C:(j + 1) * C],
                       w_bwT[:, j * D + nh * 512: j * D + (nh + 1) * 512],
                       False, j == 3, ["b_midT" + su, "w_bwT"], [PW])
            R("vector", lambda e, t=t: e.tensor_add(t["base"], t["base"], t["pw"]),
              reads=["b_base" + su, PW], writes=["b_base" + su])
            R("sync", lambda e, cc=c, t=t: e.dma_start(
                out=ap["y"][cc * C:(cc + 1) * C, :], in_=t["base"]),
              reads=["b_base" + su], writes=["d:y"], queue="qs")

        p.barrier()
        block = es.enter_context(nc.Block())
        p.finalize(es, block)
    return nc


def _prep_shared(inputs):
    con = _tile_consts()
    w = {}
    w["wq"] = np.ascontiguousarray(inputs["L0_attn_Wq"], np.float32)
    w["wk"] = np.ascontiguousarray(inputs["L0_attn_Wk"], np.float32)
    w["wv"] = np.ascontiguousarray(inputs["L0_attn_Wv"], np.float32)
    w["wgk1"] = np.ascontiguousarray(inputs["L0_attn_Wgk1"], np.float32)
    w["wgk2"] = np.ascontiguousarray(inputs["L0_attn_Wgk2"], np.float32)
    w["bgk2"] = np.ascontiguousarray(inputs["L0_attn_bgk2"], np.float32).reshape(1, D)
    w["wg"] = np.ascontiguousarray(inputs["L0_attn_Wg"], np.float32)
    w["wo"] = np.ascontiguousarray(inputs["L0_attn_Wo"], np.float32)
    w["alay"] = np.ascontiguousarray(
        np.asarray(inputs["L0_ffn_A"], np.float32).reshape(512, D).T)
    w["bwT"] = np.ascontiguousarray(
        np.asarray(inputs["L0_ffn_Bw"], np.float32).transpose(0, 2, 1).reshape(512, D))
    w["wrT"] = np.ascontiguousarray(np.asarray(inputs["L0_ffn_Wr"], np.float32).T)
    w["gnw"] = np.ascontiguousarray(
        np.asarray(inputs["L0_attn_g_norm_weight"], np.float32).reshape(4, 128).T)
    x = np.asarray(inputs["L0_attn_x"], np.float32).reshape(NCORES * TOK, D)
    w["x0"] = np.ascontiguousarray(x[0:1, :])
    return con, w, x


def kernel(**inputs):
    con, w, x = _prep_shared(inputs)

    nc_a = build_phase_a()
    in_a = []
    for cidx in range(NCORES):
        m = dict(
            x=np.ascontiguousarray(x[cidx * TOK:(cidx + 1) * TOK]),
            wq=w["wq"], wk=w["wk"], wv=w["wv"], wgk1=w["wgk1"], wgk2=w["wgk2"],
            bgk2=w["bgk2"],
            ctrib=con["trib"], ctrir=con["trir"], cidentb=con["identb"],
            conesb=con["onesb"], cmask=con["mask"],
            cln16=np.full((C, 1), -LN16, np.float32),
        )
        in_a.append(m)
    res_a = run_bass_kernel_spmd(nc_a, in_a, core_ids=list(range(NCORES)))
    ra = res_a.results

    T_all = np.concatenate([np.asarray(ra[cidx]["T_d"]) for cidx in range(NCORES)], axis=0)
    btot_all = np.zeros((8, D), np.float32)
    for cidx in range(NCORES):
        btot_all[cidx] = np.asarray(ra[cidx]["btot_d"])[0]

    nc_b = build_phase_b()
    in_b = []
    for cidx in range(NCORES):
        m = dict(
            o16_d=np.asarray(ra[cidx]["o16_d"]),
            qgT_d=np.asarray(ra[cidx]["qgT_d"]),
            base_d=np.asarray(ra[cidx]["base_d"]),
            xnT_d=np.asarray(ra[cidx]["xnT_d"]),
            T_all=T_all, btot_all=btot_all,
            x0=w["x0"], wg=w["wg"], wo=w["wo"], alay=w["alay"], bwT=w["bwT"],
            wrT=w["wrT"], gnw=w["gnw"],
            cmatx=_cmat(cidx),
            wmrow=np.where(np.arange(8) >= cidx, -1e4, 0.0).astype(np.float32)
                    .astype(BF).reshape(1, 8),
            conesb=con["onesb"],
            cidentb=con["identb"], cidentf=con["identf"], cesel=con["esel"],
        )
        in_b.append(m)
    res_b = run_bass_kernel_spmd(nc_b, in_b, core_ids=list(range(NCORES)))
    y = np.concatenate([np.asarray(res_b.results[cidx]["y"]) for cidx in range(NCORES)],
                       axis=0)
    return y.reshape(1, NCORES * TOK, D)


# version 06
# speedup vs baseline: 1.0515x; 1.0437x over previous
import numpy as np
import ml_dtypes
from contextlib import ExitStack

import concourse.bass as bass
import concourse.mybir as mybir
from concourse.bass_utils import run_bass_kernel_spmd

F32 = mybir.dt.float32
BF16 = mybir.dt.bfloat16
F16 = mybir.dt.float16
AF = mybir.ActivationFunctionType
ALU = mybir.AluOpType
AX = mybir.AxisListType
BF = ml_dtypes.bfloat16

NCORES = 8
TOK = 2048
D = 1024
C = 128
NCH = TOK // C
HEADS = 4
DK = 256
DV = 512
EPS = 1e-5
LN16 = float(np.log(16.0))

ENGS = ("tensor", "vector", "scalar", "gpsimd", "sync")


class Prog:

    NSLOT = 4

    def __init__(self, nc):
        self.nc = nc
        self.ops = {e: [] for e in ENGS}
        self.nops = {e: 0 for e in ENGS}
        self.qcount = {}
        self.last_write = {}
        self.readers = {}
        self.last_psum = {}
        self.waited = {e: {} for e in ENGS}
        self.need_inc = {e: set() for e in ENGS}

    def _src_key(self, tok):
        return ("E", tok[1]) if tok[0] == "E" else ("Q",) + tuple(tok[1])

    def rec(self, eng, emit, reads=(), writes=(), queue=None, tag=""):
        deps = {}

        def add(tok):
            k = self._src_key(tok)
            v = tok[2]
            if k not in deps or deps[k] < v:
                deps[k] = v

        for r in reads:
            t = self.last_write.get(r)
            if t is not None:
                add(t)
            if r.startswith("ps_"):
                t2 = self.last_psum.get(r)
                if t2 is not None:
                    add(t2)
        for w in writes:
            t = self.last_write.get(w)
            if t is not None:
                add(t)
            for kk, vv in self.readers.get(w, {}).items():
                add(("E", kk[1], vv) if kk[0] == "E" else ("Q", (kk[1], kk[2]), vv))
            if w.startswith("ps_"):
                t2 = self.last_psum.get(w)
                if t2 is not None:
                    add(t2)

        if queue is not None:
            idx = None
            n = self.qcount.get(queue, 0) + 1
            self.qcount[queue] = n
            slot = (n - 1) % self.NSLOT
            gen = (n - 1) // self.NSLOT + 1
            my = ("Q", (queue, slot), gen)
            if gen > 1:
                add(("Q", (queue, slot), gen - 1))
        else:
            self.nops[eng] += 1
            idx = self.nops[eng]
            my = ("E", eng, idx)
        myk = self._src_key(my)

        waits = []
        for k, v in deps.items():
            if k[0] == "E" and k[1] == "tensor" and eng == "tensor":
                continue
            got = self.waited[eng].get(k, 0)
            if v <= got:
                continue
            waits.append((k, v))
            self.waited[eng][k] = v
            if k[0] == "E":
                self.need_inc[k[1]].add(v)

        for r in reads:
            self.readers.setdefault(r, {})[myk] = max(
                self.readers.get(r, {}).get(myk, 0), my[2]
            )
            if r.startswith("ps_"):
                self.last_psum[r] = my
        for w in writes:
            self.last_write[w] = my
            self.readers[w] = {}
            if w.startswith("ps_"):
                self.last_psum[w] = my

        self.ops[eng].append(
            dict(emit=emit, waits=waits, queue=queue, idx=idx, tag=tag)
        )
        return my

    def _queue_toks(self):
        toks = []
        for q, n in self.qcount.items():
            for slot in range(min(n, self.NSLOT)):
                gen = (n - 1 - slot) // self.NSLOT + 1
                toks.append(("Q", (q, slot), gen))
        return toks

    def barrier(self):
        toks = []
        for e in ENGS:
            if self.nops[e]:
                toks.append(("E", e, self.nops[e]))
        toks += self._queue_toks()
        for e in ENGS:
            for t in toks:
                k = self._src_key(t)
                if k == ("E", e):
                    continue
                v = t[2]
                if self.waited[e].get(k, 0) < v:
                    self.waited[e][k] = v
                    if k[0] == "E":
                        self.need_inc[k[1]].add(v)
                    self.ops[e].append(dict(emit=None, waits=[(k, v)], queue=None,
                                            idx=None, tag="barrier"))

    def finalize(self, es, block):
        sems = {}
        for e in ENGS:
            sems[("E", e)] = es.enter_context(self.nc.semaphore(f"sem_{e}"))
        for q in self.qcount:
            for slot in range(self.NSLOT):
                sems[("Q", q, slot)] = es.enter_context(
                    self.nc.semaphore(f"sem_{q}_{slot}"))
        inc_prefix = {}
        for e in ENGS:
            pref = {}
            cnt = 0
            marks = self.need_inc[e]
            for i in range(1, self.nops[e] + 1):
                if i in marks:
                    cnt += 1
                pref[i] = cnt
            inc_prefix[e] = pref

        def val_of(k, v):
            if k[0] == "Q":
                return 16 * v
            return inc_prefix[k[1]].get(v, 0)

        qseen = {}

        def body(e):
            def run(eng):
                for op in self.ops[e]:
                    for (k, v) in op["waits"]:
                        eng.wait_ge(sems[k], val_of(k, v))
                    if op["emit"] is None:
                        continue
                    inst = op["emit"](eng)
                    if op["queue"] is not None:
                        q = op["queue"]
                        n = qseen.get(q, 0)
                        qseen[q] = n + 1
                        inst.then_inc(sems[("Q", q, n % self.NSLOT)], 16)
                    elif op["idx"] in self.need_inc[e]:
                        inst.then_inc(sems[("E", e)], 1)
            return run

        for e in ENGS:
            if self.ops[e]:
                getattr(block, e)(body(e))


def _tile_consts():
    i = np.arange(C)
    trib = ((i[:, None] <= i[None, :]).astype(np.float32) / 16.0).astype(BF)
    trir = ((i[:, None] > i[None, :]).astype(np.float32) / 16.0).astype(BF)
    mask = (i[:, None] <= i[None, :]).astype(np.float32)
    identb = np.eye(C, dtype=np.float32).astype(BF)
    identf = np.eye(C, dtype=np.float32)
    onesb = np.ones((C, C), dtype=np.float32).astype(BF)
    esel = np.zeros((8, 512), dtype=np.float32)
    for p in range(512):
        esel[p // 64, p] = 1.0
    return dict(trib=trib, trir=trir, mask=mask, identb=identb, identf=identf,
                onesb=onesb, esel=esel.astype(BF))


def _cmat(core):
    m_ = np.zeros((8, 8), dtype=np.float32)
    for r in range(8):
        for m in range(8):
            if r < m < core:
                m_[m, r] = 1.0
    for r in range(8):
        if r >= core:
            m_[:, r] = 0.0
    return m_.astype(BF)


def _wmask(core):
    m = np.zeros((8,), np.float32)
    m[:core] = 1.0
    return m



def build_phase_a():
    nc = bass.Bass("TRN2", target_bir_lowering=False, debug=False)
    ap = {}
    ap["x"] = nc.dram_tensor("x", [TOK, D], F32, kind="ExternalInput").ap()
    for n, shp in [("wq", [D, D]), ("wk", [D, D]), ("wv", [D, 2 * D]),
                   ("wgk1", [D, 16]), ("wgk2", [16, D]), ("bgk2", [1, D])]:
        ap[n] = nc.dram_tensor(n, shp, F32, kind="ExternalInput").ap()
    for n, shp in [("ctrib", [C, C]), ("ctrir", [C, C]), ("cidentb", [C, C]),
                   ("conesb", [C, C])]:
        ap[n] = nc.dram_tensor(n, shp, BF16, kind="ExternalInput").ap()
    ap["cmask"] = nc.dram_tensor("cmask", [C, C], F32, kind="ExternalInput").ap()
    ap["cln16"] = nc.dram_tensor("cln16", [C, 1], F32, kind="ExternalInput").ap()
    ap["o16_d"] = nc.dram_tensor("o16_d", [NCH, C, 2 * D], F16, kind="ExternalOutput").ap()
    ap["qgT_d"] = nc.dram_tensor("qgT_d", [NCH, C, D], BF16, kind="ExternalOutput").ap()
    ap["base_d"] = nc.dram_tensor("base_d", [NCH, C, D], F32, kind="ExternalOutput").ap()
    ap["xnT_d"] = nc.dram_tensor("xnT_d", [NCH, C, D], BF16, kind="ExternalOutput").ap()
    ap["T_d"] = nc.dram_tensor("T_d", [C, 8 * DV], BF16, kind="ExternalOutput").ap()
    ap["btot_d"] = nc.dram_tensor("btot_d", [1, D], F32, kind="ExternalOutput").ap()

    es = ExitStack()
    with es:
        def sb(n, s, t):
            return es.enter_context(nc.sbuf_tensor(n, s, t))[:]

        def psm(n, s, t):
            return es.enter_context(nc.psum_tensor(n, s, t))[:]

        w_q = sb("w_q", [128, 8 * D], BF16)
        w_k = sb("w_k", [128, 8 * D], BF16)
        w_v = sb("w_v", [128, 8 * 2 * D], BF16)
        w_gk1 = sb("w_gk1", [128, 8 * 16], BF16)
        w_gk2 = sb("w_gk2", [16, D], BF16)
        s_bgk2 = sb("s_bgk2", [1, D], BF16)
        c_trib = sb("c_trib", [C, C], BF16)
        c_trir = sb("c_trir", [C, C], BF16)
        c_identb = sb("c_identb", [C, C], BF16)
        c_ones = sb("c_ones", [C, C], BF16)
        c_mask = sb("c_mask", [C, C], F32)
        c_ln16 = sb("c_ln16", [C, 1], F32)
        s_S = sb("s_S", [128, 8 * DV], F32)
        s_Sb = sb("s_Sb", [128, 8 * DV], BF16)
        s_bprow = sb("s_bprow", [1, D], F32)
        s_bphi = sb("s_bphi", [1, D], BF16)
        s_bplo = sb("s_bplo", [1, D], BF16)

        spec = {
            "x": ([C, D], F32, 2), "st": ([C, 8], F32, 2), "xn": ([C, D], F32, 2),
            "xnb": ([C, D], BF16, 2), "xnT": ([C, D], BF16, 3), "f1": ([C, D], F32, 1),
            "ghi": ([C, D], BF16, 2), "glo": ([C, D], BF16, 2),
            "z1e": ([C, 16], BF16, 2), "z1b": ([16, C], BF16, 2),
            "lam": ([C, D], BF16, 2), "lami": ([C, D], BF16, 2),
            "lrev": ([C, D], BF16, 2), "lglob": ([C, D], BF16, 2),
            "qh": ([C, D], BF16, 2), "qg": ([C, D], BF16, 2),
            "kt": ([C, D], BF16, 2), "kh": ([C, D], BF16, 3),
            "v": ([C, 2 * D], BF16, 3),
            "qhT": ([C, D], BF16, 2), "ktT": ([C, D], BF16, 2), "qgT": ([C, D], BF16, 2),
            "pt": ([C, C], BF16, 2), "o16": ([C, 2 * D], F16, 2),
            "dc": ([C, 8], F32, 4),
        }
        tiles = {}
        for nm, (shp, dt2, nb) in spec.items():
            tiles[nm] = [sb(f"t_{nm}_{b}", shp, dt2) for b in range(nb)]

        def T(nm, i):
            nb = spec[nm][2]
            return tiles[nm][i % nb], f"t_{nm}_{i % nb}"

        ptb1 = psm("ptb1", [128, 1024], BF16)
        ptb4 = psm("ptb4", [128, 1024], BF16)
        pc = psm("pc", [128, 1024], F32)
        pp = psm("pp", [128, 1024], F32)
        psc = psm("psc", [128, 512], F32)
        psc2 = psm("psc2", [128, 512], F32)
        PTB1, PTB4, PC, PP, PSC, PSC2 = ("ps_ptb1", "ps_ptb4", "ps_pc", "ps_pp",
                                         "ps_psc", "ps_psc2")

        p = Prog(nc)
        R = p.rec

        def ld(engq, dst, src, rd, wr):
            R(engq[0], lambda e, d=dst, s=src: e.dma_start(out=d, in_=s),
              reads=rd, writes=wr, queue=engq[1])

        gq = ("gpsimd", "qg")
        sq = ("sync", "qs")
        for kt in range(8):
            ld(gq, w_q[:, kt * D:(kt + 1) * D],
               ap["wq"][kt * 128:(kt + 1) * 128, :], ["d:wq"], ["w_q"])
            ld(gq, w_k[:, kt * D:(kt + 1) * D],
               ap["wk"][kt * 128:(kt + 1) * 128, :], ["d:wk"], ["w_k"])
            ld(gq, w_v[:, kt * 2 * D:(kt + 1) * 2 * D],
               ap["wv"][kt * 128:(kt + 1) * 128, :], ["d:wv"], ["w_v"])
        ld(gq, w_gk1.rearrange("p (kt n) -> p kt n", kt=8),
           ap["wgk1"].rearrange("(kt p) n -> p kt n", p=128), ["d:wgk1"], ["w_gk1"])
        ld(gq, w_gk2, ap["wgk2"], ["d:wgk2"], ["w_gk2"])
        ld(gq, s_bgk2, ap["bgk2"], ["d:bgk2"], ["s_bgk2"])
        ld(gq, c_trib, ap["ctrib"], ["d:ctrib"], ["c_trib"])
        ld(gq, c_trir, ap["ctrir"], ["d:ctrir"], ["c_trir"])
        ld(gq, c_identb, ap["cidentb"], ["d:cidentb"], ["c_identb"])
        ld(gq, c_ones, ap["conesb"], ["d:conesb"], ["c_ones"])
        ld(sq, c_mask, ap["cmask"], ["d:cmask"], ["c_mask"])
        ld(sq, c_ln16, ap["cln16"], ["d:cln16"], ["c_ln16"])
        R("vector", lambda e: e.memset(s_S, 0.0), writes=["s_S"])
        R("vector", lambda e: e.memset(s_Sb, 0.0), writes=["s_Sb"])
        R("vector", lambda e: e.memset(s_bprow, 0.0), writes=["s_bprow"])
        R("vector", lambda e: e.memset(s_bphi, 0.0), writes=["s_bphi"])
        R("vector", lambda e: e.memset(s_bplo, 0.0), writes=["s_bplo"])

        def mm(out, lhsT, rhs, start, stop, rd, wr):
            R("tensor", lambda e, o=out, l=lhsT, r=rhs, s1=start, s2=stop:
              e.matmul(o, l, r, start=s1, stop=s2), reads=rd, writes=wr)

        def tr(out, in_, ident, rd, wr):
            R("tensor", lambda e, o=out, i=in_, d=ident: e.transpose(o, i, d),
              reads=rd, writes=wr)

        def S0(i):
            d, dn = T("x", i)
            R("sync", lambda e, dd=d, sl=ap["x"][i * C:(i + 1) * C, :]:
              e.dma_start(out=dd, in_=sl), reads=["d:x"], writes=[dn], queue="qs")

        def S1a(i):
            x, xn_ = T("x", i)
            st, stn = T("st", i)
            xn, xnn = T("xn", i)
            xnb, xnbn = T("xnb", i)
            xnT, xnTn = T("xnT", i)
            f1, f1n = T("f1", i)
            R("vector", lambda e: e.reduce_sum(st[:, 0:1], x, axis=AX.X),
              reads=[xn_], writes=[stn])
            R("scalar", lambda e: e.activation(out=f1, in_=x, func=AF.Square,
                                               accum_out=st[:, 1:2]),
              reads=[xn_], writes=[f1n, stn])
            R("vector", lambda e: e.tensor_scalar(out=st[:, 2:3], in0=st[:, 0:1],
                                                  scalar1=1.0 / D, scalar2=None, op0=ALU.mult),
              reads=[stn], writes=[stn])
            R("vector", lambda e: e.tensor_scalar(out=st[:, 3:4], in0=st[:, 1:2],
                                                  scalar1=1.0 / D, scalar2=None, op0=ALU.mult),
              reads=[stn], writes=[stn])
            R("vector", lambda e: e.tensor_mul(st[:, 4:5], st[:, 2:3], st[:, 2:3]),
              reads=[stn], writes=[stn])
            R("vector", lambda e: e.tensor_sub(st[:, 4:5], st[:, 3:4], st[:, 4:5]),
              reads=[stn], writes=[stn])
            R("vector", lambda e: e.tensor_scalar(out=st[:, 4:5], in0=st[:, 4:5],
                                                  scalar1=1.0, scalar2=EPS, op0=ALU.mult,
                                                  op1=ALU.add),
              reads=[stn], writes=[stn])
            R("scalar", lambda e: e.activation(out=st[:, 5:6], in_=st[:, 4:5], func=AF.Sqrt),
              reads=[stn], writes=[stn])
            R("vector", lambda e: e.reciprocal(st[:, 6:7], st[:, 5:6]),
              reads=[stn], writes=[stn])
            R("vector", lambda e: e.tensor_scalar(out=xn, in0=x, scalar1=st[:, 2:3],
                                                  scalar2=st[:, 6:7], op0=ALU.subtract,
                                                  op1=ALU.mult),
              reads=[xn_, stn], writes=[xnn])
            R("vector", lambda e: e.tensor_copy(xnb, xn), reads=[xnn], writes=[xnbn])
            R("vector", lambda e: e.tensor_add(x, x, xn), reads=[xn_, xnn], writes=[xn_])
            R("sync", lambda e, cc=i: e.dma_start(out=ap["base_d"][cc], in_=x),
              reads=[xn_], writes=["d:base"], queue="qs")
            for j in range(8):
                tr(ptb1[:, j * C:(j + 1) * C], xnb[:, j * C:(j + 1) * C], c_identb,
                   [xnbn, "c_identb"], [PTB1])
            R("scalar", lambda e: e.copy(xnT, ptb1), reads=[PTB1], writes=[xnTn])
            R("sync", lambda e, cc=i: e.dma_start(out=ap["xnT_d"][cc], in_=xnT),
              reads=[xnTn], writes=["d:xnT"], queue="qs")

        def S1b(i):
            xnT, xnTn = T("xnT", i)
            xn, xnn = T("xn", i)
            f1, f1n = T("f1", i)
            ghi, ghin = T("ghi", i)
            glo, glon = T("glo", i)
            z1e, z1en = T("z1e", i)
            z1b, z1bn = T("z1b", i)
            for j in range(8):
                mm(pc[:, 0:16], xnT[:, j * C:(j + 1) * C],
                   w_gk1[:, j * 16:(j + 1) * 16], j == 0, j == 7, [xnTn, "w_gk1"], [PC])
            R("vector", lambda e: e.tensor_copy(z1e, pc[:, 0:16]), reads=[PC], writes=[z1en])
            tr(ptb1[0:16, 0:C], z1e, c_identb, [z1en, "c_identb"], [PTB1])
            R("scalar", lambda e: e.copy(z1b, ptb1[0:16, 0:C]), reads=[PTB1], writes=[z1bn])
            for nh in range(2):
                sl = slice(nh * 512, (nh + 1) * 512)
                mm(pc[:, sl], z1b, w_gk2[:, sl], True, False, [z1bn, "w_gk2"], [PC])
                mm(pc[:, sl], c_ones[0:1, :], s_bgk2[:, sl], False, True,
                   ["c_ones", "s_bgk2"], [PC])
            R("scalar", lambda e: e.activation(out=f1, in_=pc, func=AF.Abs),
              reads=[PC], writes=[f1n])
            R("scalar", lambda e: e.activation(out=f1, in_=f1, func=AF.Exp, scale=-1.0),
              reads=[f1n], writes=[f1n])
            R("scalar", lambda e: e.activation(out=f1, in_=f1, func=AF.Ln, bias=1.0),
              reads=[f1n], writes=[f1n])
            R("vector", lambda e: e.tensor_scalar(out=xn, in0=pc, scalar1=0.0,
                                                  scalar2=None, op0=ALU.min),
              reads=[PC], writes=[xnn])
            R("vector", lambda e: e.tensor_sub(xn, xn, f1), reads=[xnn, f1n], writes=[xnn])
            R("vector", lambda e: e.tensor_copy(ghi, xn), reads=[xnn], writes=[ghin])
            R("vector", lambda e: e.tensor_sub(glo, xn, ghi), reads=[xnn, ghin], writes=[glon])

        def S2a(i):
            ghi, ghin = T("ghi", i)
            glo, glon = T("glo", i)
            lam, lamn = T("lam", i)
            lami, lamin = T("lami", i)
            for nh in range(2):
                sl = slice(nh * 512, (nh + 1) * 512)
                mm(pc[:, sl], c_trib, ghi[:, sl], True, False, ["c_trib", ghin], [PC])
                mm(pc[:, sl], c_trib, glo[:, sl], False, True, ["c_trib", glon], [PC])
            R("scalar", lambda e: e.activation(out=lam, in_=pc, func=AF.Exp, bias=c_ln16),
              reads=[PC, "c_ln16"], writes=[lamn])
            R("scalar", lambda e: e.activation(out=lami, in_=pc, func=AF.Exp, scale=-1.0),
              reads=[PC], writes=[lamin])

        def S2b(i):
            ghi, ghin = T("ghi", i)
            glo, glon = T("glo", i)
            lglob, lglobn = T("lglob", i)
            lrev, lrevn = T("lrev", i)
            for nh in range(2):
                sl = slice(nh * 512, (nh + 1) * 512)
                mm(pc[:, sl], c_trib, ghi[:, sl], True, False, ["c_trib", ghin], [PC])
                mm(pc[:, sl], c_trib, glo[:, sl], False, False, ["c_trib", glon], [PC])
                mm(pc[:, sl], c_ones[0:1, :], s_bphi[:, sl], False, False,
                   ["c_ones", "s_bphi"], [PC])
                mm(pc[:, sl], c_ones[0:1, :], s_bplo[:, sl], False, True,
                   ["c_ones", "s_bplo"], [PC])
            R("scalar", lambda e: e.activation(out=lglob, in_=pc, func=AF.Exp, bias=c_ln16),
              reads=[PC, "c_ln16"], writes=[lglobn])
            for nh in range(2):
                sl = slice(nh * 512, (nh + 1) * 512)
                mm(pc[:, sl], c_trir, ghi[:, sl], True, False, ["c_trir", ghin], [PC])
                mm(pc[:, sl], c_trir, glo[:, sl], False, True, ["c_trir", glon], [PC])
            R("scalar", lambda e: e.activation(out=lrev, in_=pc, func=AF.Exp),
              reads=[PC], writes=[lrevn])

        def S2c(i):
            ghi, ghin = T("ghi", i)
            glo, glon = T("glo", i)
            dc, dcn = T("dc", i)
            for j in range(8):
                mm(pc[:, j:j + 1], ghi[:, j * C:(j + 1) * C], c_ones[:, 0:1],
                   j == 0, False, [ghin, "c_ones"], [PC])
                mm(pc[:, j:j + 1], glo[:, j * C:(j + 1) * C], c_ones[:, 0:1],
                   False, j == 7, [glon, "c_ones"], [PC])
            R("scalar", lambda e: e.activation(out=dc, in_=pc[:, 0:8], func=AF.Exp),
              reads=[PC], writes=[dcn])
            for nh in range(2):
                sl = slice(nh * 512, (nh + 1) * 512)
                mm(pc[0:1, sl], c_ones[:, 0:1], ghi[:, sl], True, False,
                   ["c_ones", ghin], [PC])
                mm(pc[0:1, sl], c_ones[:, 0:1], glo[:, sl], False, True,
                   ["c_ones", glon], [PC])
            R("vector", lambda e: e.tensor_add(s_bprow, s_bprow, pc[0:1, :]),
              reads=["s_bprow", PC], writes=["s_bprow"])
            R("vector", lambda e: e.tensor_copy(s_bphi, s_bprow),
              reads=["s_bprow"], writes=["s_bphi"])
            R("vector", lambda e: e.tensor_sub(s_bplo, s_bprow, s_bphi),
              reads=["s_bprow", "s_bphi"], writes=["s_bplo"])

        def S3q(i):
            xnT, xnTn = T("xnT", i)
            lam, lamn = T("lam", i)
            lglob, lglobn = T("lglob", i)
            qh, qhn = T("qh", i)
            qg, qgn = T("qg", i)
            for j in range(8):
                for nh in range(2):
                    sl = slice(nh * 512, (nh + 1) * 512)
                    mm(pp[:, sl], xnT[:, j * C:(j + 1) * C],
                       w_q[:, j * D + nh * 512: j * D + (nh + 1) * 512],
                       j == 0, j == 7, [xnTn, "w_q"], [PP])
            R("vector", lambda e: e.tensor_mul(qh, pp, lam), reads=[PP, lamn], writes=[qhn])
            R("vector", lambda e: e.tensor_mul(qg, pp, lglob), reads=[PP, lglobn], writes=[qgn])

        def S3k(i):
            xnT, xnTn = T("xnT", i)
            lami, lamin = T("lami", i)
            lrev, lrevn = T("lrev", i)
            kt, ktn = T("kt", i)
            kh, khn = T("kh", i)
            for j in range(8):
                for nh in range(2):
                    sl = slice(nh * 512, (nh + 1) * 512)
                    mm(pp[:, sl], xnT[:, j * C:(j + 1) * C],
                       w_k[:, j * D + nh * 512: j * D + (nh + 1) * 512],
                       j == 0, j == 7, [xnTn, "w_k"], [PP])
            R("vector", lambda e: e.tensor_mul(kt, pp, lami), reads=[PP, lamin], writes=[ktn])
            R("vector", lambda e: e.tensor_mul(kh, pp, lrev), reads=[PP, lrevn], writes=[khn])

        def S3v(i, half):
            xnT, xnTn = T("xnT", i)
            v, vn = T("v", i)
            for j in range(8):
                for nh in range(2):
                    mm(pp[:, nh * 512:(nh + 1) * 512], xnT[:, j * C:(j + 1) * C],
                       w_v[:, j * 2 * D + half * D + nh * 512:
                           j * 2 * D + half * D + (nh + 1) * 512],
                       j == 0, j == 7, [xnTn, "w_v"], [PP])
            R("scalar", lambda e, h=half: e.copy(v[:, h * D:(h + 1) * D], pp),
              reads=[PP], writes=[vn])

        def S4(i):
            for src, dst in (("qh", "qhT"), ("kt", "ktT"), ("qg", "qgT")):
                sT, sn = T(src, i)
                dT, dn = T(dst, i)
                for j in range(8):
                    tr(ptb4[:, j * C:(j + 1) * C], sT[:, j * C:(j + 1) * C], c_identb,
                       [sn, "c_identb"], [PTB4])
                R("scalar", lambda e, d=dT, ps=ptb4: e.copy(d, ps), reads=[PTB4], writes=[dn])
            qgT, qgTn = T("qgT", i)
            R("sync", lambda e, cc=i: e.dma_start(out=ap["qgT_d"][cc], in_=qgT),
              reads=[qgTn], writes=["d:qgT"], queue="qs")

        def S5h(i, h):
            qhT, qhTn = T("qhT", i)
            ktT, ktTn = T("ktT", i)
            kh, khn = T("kh", i)
            v, vn = T("v", i)
            pt, ptn = T("pt", i)
            o16, o16n = T("o16", i)
            dc, dcn = T("dc", i)
            for ji, j in enumerate((2 * h, 2 * h + 1)):
                mm(psc[:, 0:C], ktT[:, j * C:(j + 1) * C], qhT[:, j * C:(j + 1) * C],
                   ji == 0, ji == 1, [ktTn, qhTn], [PSC])
            R("vector", lambda e: e.tensor_mul(pt, psc[:, 0:C], c_mask),
              reads=[PSC, "c_mask"], writes=[ptn])
            mm(psc[:, 0:DV], pt, v[:, h * DV:(h + 1) * DV], True, False,
               [ptn, vn], [PSC])
            for ji, j in enumerate((2 * h, 2 * h + 1)):
                mm(psc[:, 0:DV], qhT[:, j * C:(j + 1) * C], s_Sb[:, j * DV:(j + 1) * DV],
                   False, ji == 1, [qhTn, "s_Sb"], [PSC])
            R("scalar", lambda e: e.copy(o16[:, h * DV:(h + 1) * DV], psc[:, 0:DV]),
              reads=[PSC], writes=[o16n])
            for ih in range(2):
                sg = 2 * h + ih
                pbank, pbn = (psc2, PSC2) if ih == 0 else (psc, PSC)
                mm(pbank[:, 0:DV], kh[:, h * DK + ih * C: h * DK + (ih + 1) * C],
                   v[:, h * DV:(h + 1) * DV], True, True, [khn, vn], [pbn])
                R("vector", lambda e, g=sg, pb=pbank: e.scalar_tensor_tensor(
                    out=s_S[:, g * DV:(g + 1) * DV], in0=s_S[:, g * DV:(g + 1) * DV],
                    scalar=dc[:, g:g + 1], in1=pb[:, 0:DV],
                    op0=ALU.mult, op1=ALU.add),
                  reads=["s_S", dcn, pbn], writes=["s_S"])
                R("scalar", lambda e, g=sg: e.copy(s_Sb[:, g * DV:(g + 1) * DV],
                                                   s_S[:, g * DV:(g + 1) * DV]),
                  reads=["s_S"], writes=["s_Sb"])
            if h == 3:
                R("sync", lambda e, cc=i: e.dma_start(out=ap["o16_d"][cc], in_=o16),
                  reads=[o16n], writes=["d:o16"], queue="qs")

        S0(0)
        for i in range(NCH + 5):
            if 0 <= i + 1 < NCH:
                S0(i + 1)
            if 0 <= i - 4 < NCH:
                S5h(i - 4, 0)
            if 0 <= i - 2 < NCH:
                S3q(i - 2)
            if 0 <= i - 4 < NCH:
                S5h(i - 4, 1)
            if 0 <= i - 2 < NCH:
                S3k(i - 2)
            if 0 <= i - 4 < NCH:
                S5h(i - 4, 2)
            if 0 <= i - 2 < NCH:
                S3v(i - 2, 0)
            if 0 <= i - 4 < NCH:
                S5h(i - 4, 3)
            if 0 <= i - 2 < NCH:
                S3v(i - 2, 1)
            if 0 <= i - 1 < NCH:
                S2a(i - 1)
            if 0 <= i - 3 < NCH:
                S4(i - 3)
            if 0 <= i - 1 < NCH:
                S2b(i - 1)
            if 0 <= i < NCH:
                S1a(i)
            if 0 <= i - 1 < NCH:
                S2c(i - 1)
            if 0 <= i < NCH:
                S1b(i)

        R("sync", lambda e: e.dma_start(out=ap["T_d"], in_=s_Sb),
          reads=["s_Sb"], writes=["d:T"], queue="qs")
        R("sync", lambda e: e.dma_start(out=ap["btot_d"], in_=s_bprow),
          reads=["s_bprow"], writes=["d:btot"], queue="qs")
        p.barrier()

        block = es.enter_context(nc.Block())
        p.finalize(es, block)
    return nc


def build_phase_b():
    nc = bass.Bass("TRN2", target_bir_lowering=False, debug=False)
    ap = {}
    for n, shp, dt in [("o16_d", [NCH, C, 2 * D], F16), ("qgT_d", [NCH, C, D], BF16),
                       ("base_d", [NCH, C, D], F32), ("xnT_d", [NCH, C, D], BF16),
                       ("T_all", [NCORES * C, 8 * DV], BF16), ("btot_all", [8, D], F32),
                       ("x0", [1, D], F32),
                       ("wg", [D, 2 * D], F32), ("wo", [2 * D, D], F32),
                       ("alay", [D, 512], F32), ("bwT", [512, D], F32),
                       ("wrT", [D, 8], F32), ("gnw", [128, 4], F32),
                       ("cmatx", [8, 8], BF16), ("wmrow", [1, 8], BF16),
                       ("conesb", [C, C], BF16),
                       ("cidentb", [C, C], BF16), ("cidentf", [C, C], F32),
                       ("cesel", [8, 512], BF16)]:
        ap[n] = nc.dram_tensor(n, shp, dt, kind="ExternalInput").ap()
    ap["y"] = nc.dram_tensor("y", [TOK, D], F32, kind="ExternalOutput").ap()

    es = ExitStack()
    with es:
        def sb(n, s, t):
            return es.enter_context(nc.sbuf_tensor(n, s, t))[:]

        def psm(n, s, t):
            return es.enter_context(nc.psum_tensor(n, s, t))[:]

        w_g = sb("w_g", [128, 8 * 2 * D], BF16)
        w_wo = sb("w_wo", [128, 16 * D], BF16)
        w_alay = sb("w_alay", [128, 8 * 512], BF16)
        w_bwT = sb("w_bwT", [128, 4 * D], BF16)
        w_wrT = sb("w_wrT", [128, 8 * 8], F32)
        s_gnw = sb("s_gnw", [128, 4], F32)
        c_identb = sb("c_identb", [C, C], BF16)
        c_identf = sb("c_identf", [C, C], F32)
        c_esel = sb("c_esel", [8, 512], BF16)
        c_cmat = sb("c_cmat", [8, 8], BF16)
        c_ones = sb("c_ones", [C, C], BF16)
        s_wmrow = sb("s_wmrow", [1, 8], BF16)
        s_x0 = sb("s_x0", [1, D], F32)
        s_x0b = sb("s_x0b", [1, D], F32)
        g_st = sb("g_st", [1, 8], F32)
        g_junk = sb("g_junk", [1, D], F32)
        s_x0T = sb("s_x0T", [128, 8], F32)
        g_lg8 = sb("g_lg8", [8, 1], F32)
        g_lg = sb("g_lg", [1, 8], F32)
        g_d = sb("g_d", [1, 8], F32)
        g_ind = sb("g_ind", [1, 8], F32)
        g_l2 = sb("g_l2", [1, 8], F32)
        g_keep = sb("g_keep", [1, 8], F32)
        g_e = sb("g_e", [1, 8], F32)
        g_gt = sb("g_gt", [1, 8], F32)
        g_gc8 = sb("g_gc8", [8, 1], F32)
        g_gcb = sb("g_gcb", [8, 4], BF16)
        s_gc = sb("s_gc", [128, 4], F32)
        b_bt = sb("b_bt", [8, D], F32)
        b_bthi = sb("b_bthi", [8, D], BF16)
        b_btlo = sb("b_btlo", [8, D], BF16)
        b_w = sb("b_w", [128, 64], F32)
        b_Tr = [sb("b_Tr_0", [128, 8 * DV], BF16)]
        s_Sin = sb("s_Sin", [128, 8 * DV], F32)
        s_Sinb = sb("s_Sinb", [128, 8 * DV], BF16)

        spec = {
            "xnT": ([C, D], BF16, 3), "o16": ([C, 2 * D], F16, 3),
            "qgT": ([C, D], BF16, 3), "base": ([C, D], F32, 4),
            "ob": ([C, 2 * D], BF16, 2), "rs": ([C, 4], F32, 2),
            "sd": ([C, 4], F32, 2), "rst": ([C, 4], F32, 2),
            "junk": ([C, DV], F32, 1), "sg": ([C, 2 * D], BF16, 2),
            "ogT": ([C, 2 * D], BF16, 2),
            "mid": ([C, 512], BF16, 2), "midT": ([C, 512], BF16, 2),
        }
        tiles = {}
        for nm, (shp, dt2, nb) in spec.items():
            tiles[nm] = [sb(f"b_{nm}_{b}", shp, dt2) for b in range(nb)]

        def T(nm, i):
            nb = spec[nm][2]
            return tiles[nm][i % nb], f"b_{nm}_{i % nb}"

        pgA = psm("pgA", [128, 1024], F32)
        pd = psm("pd", [128, 1024], F32)
        pm = psm("pm", [128, 512], F32)
        pwo = psm("pwo", [128, 1024], F32)
        ptbB = psm("ptbB", [128, 1024], BF16)
        PGA, PD, PM, PWO, PTBB = "ps_pgA", "ps_pd", "ps_pm", "ps_pwo", "ps_ptbB"

        p = Prog(nc)
        R = p.rec

        def mm(out, lhsT, rhs, start, stop, rd, wr):
            R("tensor", lambda e, o=out, l=lhsT, r=rhs, s1=start, s2=stop:
              e.matmul(o, l, r, start=s1, stop=s2), reads=rd, writes=wr)

        def tr(out, in_, ident, rd, wr):
            R("tensor", lambda e, o=out, i=in_, d=ident: e.transpose(o, i, d),
              reads=rd, writes=wr)

        def ldg(dst, src, rd, wr):
            R("gpsimd", lambda e, d=dst, s=src: e.dma_start(out=d, in_=s),
              reads=rd, writes=wr, queue="qg")

        for kt in range(8):
            ldg(w_g[:, kt * 2 * D:(kt + 1) * 2 * D],
                ap["wg"][kt * 128:(kt + 1) * 128, :], ["d:wg"], ["w_g"])
        for kt in range(16):
            ldg(w_wo[:, kt * D:(kt + 1) * D],
                ap["wo"][kt * 128:(kt + 1) * 128, :], ["d:wo"], ["w_wo"])
        ldg(w_alay.rearrange("p (kt n) -> p kt n", kt=8),
            ap["alay"].rearrange("(kt p) n -> p kt n", p=128), ["d:alay"], ["w_alay"])
        ldg(w_bwT.rearrange("p (kt n) -> p kt n", kt=4),
            ap["bwT"].rearrange("(kt p) n -> p kt n", p=128), ["d:bwT"], ["w_bwT"])
        ldg(c_identb, ap["cidentb"], ["d:cidentb"], ["c_identb"])
        ldg(c_esel, ap["cesel"], ["d:cesel"], ["c_esel"])
        ldg(c_cmat, ap["cmatx"], ["d:cmatx"], ["c_cmat"])
        ldg(c_ones, ap["conesb"], ["d:conesb"], ["c_ones"])
        ldg(s_wmrow, ap["wmrow"], ["d:wmrow"], ["s_wmrow"])
        R("sync", lambda e: e.dma_start(
            out=w_wrT.rearrange("p (kt n) -> p kt n", kt=8),
            in_=ap["wrT"].rearrange("(kt p) n -> p kt n", p=128)),
          reads=["d:wrT"], writes=["w_wrT"], queue="qs")
        R("sync", lambda e: e.dma_start(out=s_gnw, in_=ap["gnw"]),
          reads=["d:gnw"], writes=["s_gnw"], queue="qs")
        R("sync", lambda e: e.dma_start(out=c_identf, in_=ap["cidentf"]),
          reads=["d:cidentf"], writes=["c_identf"], queue="qs")
        R("sync", lambda e: e.dma_start(out=s_x0, in_=ap["x0"]),
          reads=["d:x0"], writes=["s_x0"], queue="qs")
        R("sync", lambda e: e.dma_start(out=b_bt, in_=ap["btot_all"]),
          reads=["d:btot"], writes=["b_bt"], queue="qs")

        R("vector", lambda e: e.reduce_sum(g_st[:, 0:1], s_x0, axis=AX.X),
          reads=["s_x0"], writes=["g_st"])
        R("scalar", lambda e: e.activation(out=g_junk, in_=s_x0, func=AF.Square,
                                           accum_out=g_st[:, 1:2]),
          reads=["s_x0"], writes=["g_junk", "g_st"])
        R("vector", lambda e: e.tensor_scalar(out=g_st[:, 2:3], in0=g_st[:, 0:1],
                                              scalar1=1.0 / D, scalar2=None, op0=ALU.mult),
          reads=["g_st"], writes=["g_st"])
        R("vector", lambda e: e.tensor_scalar(out=g_st[:, 3:4], in0=g_st[:, 1:2],
                                              scalar1=1.0 / D, scalar2=None, op0=ALU.mult),
          reads=["g_st"], writes=["g_st"])
        R("vector", lambda e: e.tensor_mul(g_st[:, 4:5], g_st[:, 2:3], g_st[:, 2:3]),
          reads=["g_st"], writes=["g_st"])
        R("vector", lambda e: e.tensor_sub(g_st[:, 4:5], g_st[:, 3:4], g_st[:, 4:5]),
          reads=["g_st"], writes=["g_st"])
        R("vector", lambda e: e.tensor_scalar(out=g_st[:, 4:5], in0=g_st[:, 4:5],
                                              scalar1=1.0, scalar2=EPS, op0=ALU.mult,
                                              op1=ALU.add),
          reads=["g_st"], writes=["g_st"])
        R("scalar", lambda e: e.activation(out=g_st[:, 5:6], in_=g_st[:, 4:5], func=AF.Sqrt),
          reads=["g_st"], writes=["g_st"])
        R("vector", lambda e: e.reciprocal(g_st[:, 6:7], g_st[:, 5:6]),
          reads=["g_st"], writes=["g_st"])
        R("vector", lambda e: e.tensor_scalar(out=s_x0b, in0=s_x0, scalar1=g_st[:, 2:3],
                                              scalar2=g_st[:, 6:7], op0=ALU.subtract,
                                              op1=ALU.mult),
          reads=["s_x0", "g_st"], writes=["s_x0b"])
        for j in range(8):
            tr(pm[:, j:j + 1], s_x0b[:, j * C:(j + 1) * C], c_identf[0:1, 0:1],
               ["s_x0b", "c_identf"], [PM])
        R("scalar", lambda e: e.copy(s_x0T, pm[:, 0:8]), reads=[PM], writes=["s_x0T"])
        for j in range(8):
            mm(pd[0:8, 0:1], w_wrT[:, j * 8:(j + 1) * 8], s_x0T[:, j:j + 1],
               j == 0, j == 7, ["w_wrT", "s_x0T"], [PD])
        R("scalar", lambda e: e.copy(g_lg8, pd[0:8, 0:1]), reads=[PD], writes=["g_lg8"])
        tr(pm[0:1, 0:8], g_lg8, c_identf[0:8, 0:8], ["g_lg8", "c_identf"], [PM])
        R("scalar", lambda e: e.copy(g_lg, pm[0:1, 0:8]), reads=[PM], writes=["g_lg"])
        R("vector", lambda e: e.reduce_max(g_st[:, 7:8], g_lg, axis=AX.X),
          reads=["g_lg"], writes=["g_st"])
        R("vector", lambda e: e.tensor_scalar(out=g_d, in0=g_lg, scalar1=g_st[:, 7:8],
                                              scalar2=None, op0=ALU.subtract),
          reads=["g_lg", "g_st"], writes=["g_d"])
        R("vector", lambda e: e.tensor_scalar(out=g_ind, in0=g_d, scalar1=0.0,
                                              scalar2=None, op0=ALU.is_ge),
          reads=["g_d"], writes=["g_ind"])
        R("vector", lambda e: e.scalar_tensor_tensor(out=g_l2, in0=g_ind, scalar=-1e30,
                                                     in1=g_lg, op0=ALU.mult, op1=ALU.add),
          reads=["g_ind", "g_lg"], writes=["g_l2"])
        R("vector", lambda e: e.reduce_max(g_st[:, 0:1], g_l2, axis=AX.X),
          reads=["g_l2"], writes=["g_st"])
        R("vector", lambda e: e.tensor_scalar(out=g_keep, in0=g_lg, scalar1=g_st[:, 0:1],
                                              scalar2=None, op0=ALU.is_ge),
          reads=["g_lg", "g_st"], writes=["g_keep"])
        R("scalar", lambda e: e.activation(out=g_e, in_=g_d, func=AF.Exp),
          reads=["g_d"], writes=["g_e"])
        R("vector", lambda e: e.tensor_mul(g_e, g_e, g_keep),
          reads=["g_e", "g_keep"], writes=["g_e"])
        R("vector", lambda e: e.reduce_sum(g_st[:, 1:2], g_e, axis=AX.X),
          reads=["g_e"], writes=["g_st"])
        R("vector", lambda e: e.reciprocal(g_st[:, 2:3], g_st[:, 1:2]),
          reads=["g_st"], writes=["g_st"])
        R("vector", lambda e: e.tensor_scalar(out=g_gt, in0=g_e, scalar1=g_st[:, 2:3],
                                              scalar2=None, op0=ALU.mult),
          reads=["g_e", "g_st"], writes=["g_gt"])
        tr(pm[0:8, 8:9], g_gt, c_identf[0:1, 0:1], ["g_gt", "c_identf"], [PM])
        R("scalar", lambda e: e.copy(g_gc8, pm[0:8, 8:9]), reads=[PM], writes=["g_gc8"])
        R("vector", lambda e: e.tensor_copy(g_gcb[:, 0:1], g_gc8),
          reads=["g_gc8"], writes=["g_gcb"])
        R("vector", lambda e: e.tensor_sub(g_gcb[:, 2:3], g_gc8, g_gcb[:, 0:1]),
          reads=["g_gc8", "g_gcb"], writes=["g_gcb"])
        for tau in range(4):
            for hl in range(2):
                mm(pm[:, tau:tau + 1], c_esel[:, tau * 128:(tau + 1) * 128],
                   g_gcb[:, 2 * hl:2 * hl + 1], (tau == 0 and hl == 0),
                   (tau == 3 and hl == 1), ["c_esel", "g_gcb"], [PM])
        R("scalar", lambda e: e.copy(s_gc, pm[:, 0:4]), reads=[PM], writes=["s_gc"])
        for tau in range(4):
            R("vector", lambda e, u=tau: e.tensor_scalar(
                out=w_bwT[:, u * D:(u + 1) * D], in0=w_bwT[:, u * D:(u + 1) * D],
                scalar1=s_gc[:, u:u + 1], scalar2=None, op0=ALU.mult),
              reads=["w_bwT", "s_gc"], writes=["w_bwT"])
        for k in range(16):
            R("vector", lambda e, kk=k: e.tensor_scalar(
                out=w_wo[:, kk * D:(kk + 1) * D], in0=w_wo[:, kk * D:(kk + 1) * D],
                scalar1=s_gnw[:, kk % 4:kk % 4 + 1], scalar2=None, op0=ALU.mult),
              reads=["w_wo", "s_gnw"], writes=["w_wo"])

        R("vector", lambda e: e.tensor_copy(b_bthi, b_bt), reads=["b_bt"], writes=["b_bthi"])
        R("vector", lambda e: e.tensor_sub(b_btlo, b_bt, b_bthi),
          reads=["b_bt", "b_bthi"], writes=["b_btlo"])
        for tau in range(8):
            mm(pm[:, tau * 8:(tau + 1) * 8], b_bthi[:, tau * 128:(tau + 1) * 128],
               c_cmat, tau == 0, False, ["b_bthi", "c_cmat"], [PM])
            mm(pm[:, tau * 8:(tau + 1) * 8], b_btlo[:, tau * 128:(tau + 1) * 128],
               c_cmat, False, False, ["b_btlo", "c_cmat"], [PM])
            mm(pm[:, tau * 8:(tau + 1) * 8], c_ones[0:1, :], s_wmrow,
               False, tau == 7, ["c_ones", "s_wmrow"], [PM])
        R("scalar", lambda e: e.activation(out=b_w, in_=pm[:, 0:64], func=AF.Exp),
          reads=[PM], writes=["b_w"])
        R("vector", lambda e: e.memset(s_Sin, 0.0), writes=["s_Sin"])
        for r in range(NCORES):
            tb = b_Tr[0]
            tbn = "b_Tr_0"
            R("sync", lambda e, rr=r, d=tb: e.dma_start(
                out=d, in_=ap["T_all"][rr * C:(rr + 1) * C, :]),
              reads=["d:Tall"], writes=[tbn], queue="qs")
            for sg_ in range(8):
                R("vector", lambda e, g=sg_, rr=r, d=tb: e.scalar_tensor_tensor(
                    out=s_Sin[:, g * DV:(g + 1) * DV], in0=d[:, g * DV:(g + 1) * DV],
                    scalar=b_w[:, g * 8 + rr:g * 8 + rr + 1],
                    in1=s_Sin[:, g * DV:(g + 1) * DV],
                    op0=ALU.mult, op1=ALU.add),
                  reads=[tbn, "b_w", "s_Sin"], writes=["s_Sin"])
        R("scalar", lambda e: e.copy(s_Sinb, s_Sin), reads=["s_Sin"], writes=["s_Sinb"])

        def B0(i):
            for nm, key in [("xnT", "xnT_d"), ("o16", "o16_d"),
                            ("qgT", "qgT_d"), ("base", "base_d")]:
                d, dn = T(nm, i)
                R("sync", lambda e, cc=i, dd=d, k=key: e.dma_start(out=dd, in_=ap[k][cc]),
                  reads=["d:" + key], writes=[dn], queue="qs")

        def B1(i):
            xnT, xnTn = T("xnT", i)
            o16, o16n = T("o16", i)
            qgT, qgTn = T("qgT", i)
            ob, obn = T("ob", i)
            rs, rsn = T("rs", i)
            sd, sdn = T("sd", i)
            rst, rstn = T("rst", i)
            junk, junkn = T("junk", i)
            sg, sgn = T("sg", i)
            t1, t1n = ob, obn
            mid, midn = T("mid", i)
            for j in range(8):
                for nh in range(2):
                    mm(pgA[:, nh * 512:(nh + 1) * 512], xnT[:, j * C:(j + 1) * C],
                       w_g[:, j * 2 * D + nh * 512: j * 2 * D + (nh + 1) * 512],
                       j == 0, j == 7, [xnTn, "w_g"], [PGA])
            for hi in range(2):
                for ji, j in enumerate((2 * hi, 2 * hi + 1)):
                    mm(pd[:, hi * DV:(hi + 1) * DV], qgT[:, j * C:(j + 1) * C],
                       s_Sinb[:, j * DV:(j + 1) * DV], ji == 0, ji == 1,
                       [qgTn, "s_Sinb"], [PD])
            R("scalar", lambda e: e.activation(out=sg[:, 0:D], in_=pgA, func=AF.Silu),
              reads=[PGA], writes=[sgn])
            for j in range(8):
                for nh in range(2):
                    mm(pgA[:, nh * 512:(nh + 1) * 512], xnT[:, j * C:(j + 1) * C],
                       w_g[:, j * 2 * D + D + nh * 512: j * 2 * D + D + (nh + 1) * 512],
                       j == 0, j == 7, [xnTn, "w_g"], [PGA])
            for hi in range(2):
                R("vector", lambda e, h=hi: e.tensor_tensor(
                    ob[:, h * DV:(h + 1) * DV], o16[:, h * DV:(h + 1) * DV],
                    pd[:, h * DV:(h + 1) * DV], ALU.add),
                  reads=[o16n, PD], writes=[obn])
            for hi in range(2):
                h = 2 + hi
                for ji, j in enumerate((2 * h, 2 * h + 1)):
                    mm(pd[:, hi * DV:(hi + 1) * DV], qgT[:, j * C:(j + 1) * C],
                       s_Sinb[:, j * DV:(j + 1) * DV], ji == 0, ji == 1,
                       [qgTn, "s_Sinb"], [PD])
            R("scalar", lambda e: e.activation(out=sg[:, D:2 * D], in_=pgA, func=AF.Silu),
              reads=[PGA], writes=[sgn])
            for j in range(8):
                mm(pm[:, 0:512], xnT[:, j * C:(j + 1) * C],
                   w_alay[:, j * 512:(j + 1) * 512], j == 0, j == 7,
                   [xnTn, "w_alay"], [PM])
            for hi in range(2):
                h = 2 + hi
                R("vector", lambda e, h2=h, i2=hi: e.tensor_tensor(
                    ob[:, h2 * DV:(h2 + 1) * DV], o16[:, h2 * DV:(h2 + 1) * DV],
                    pd[:, i2 * DV:(i2 + 1) * DV], ALU.add),
                  reads=[o16n, PD], writes=[obn])
            R("vector", lambda e: e.tensor_copy(mid, pm[:, 0:512]), reads=[PM], writes=[midn])
            for h in range(4):
                R("scalar", lambda e, hh=h: e.activation(
                    out=junk, in_=ob[:, hh * DV:(hh + 1) * DV], func=AF.Square,
                    accum_out=rs[:, hh:hh + 1]),
                  reads=[obn], writes=[junkn, rsn])
            R("vector", lambda e: e.tensor_scalar(out=rs, in0=rs, scalar1=1.0 / DV,
                                                  scalar2=EPS, op0=ALU.mult, op1=ALU.add),
              reads=[rsn], writes=[rsn])
            R("scalar", lambda e: e.activation(out=sd, in_=rs, func=AF.Sqrt),
              reads=[rsn], writes=[sdn])
            R("vector", lambda e: e.reciprocal(rst, sd), reads=[sdn], writes=[rstn])
            R("vector", lambda e: e.tensor_mul(t1, ob, sg),
              reads=[obn, sgn], writes=[t1n])
            for h in range(4):
                R("vector", lambda e, hh=h: e.tensor_scalar(
                    out=t1[:, hh * DV:(hh + 1) * DV], in0=t1[:, hh * DV:(hh + 1) * DV],
                    scalar1=rst[:, hh:hh + 1], scalar2=None, op0=ALU.mult),
                  reads=[t1n, rstn], writes=[t1n])

        def B2(i):
            t1, t1n = T("ob", i)
            ogT, ogTn = T("ogT", i)
            mid, midn = T("mid", i)
            midT, midTn = T("midT", i)
            base, basen = T("base", i)
            for half in range(2):
                for j in range(8):
                    tr(ptbB[:, j * C:(j + 1) * C],
                       t1[:, half * D + j * C: half * D + (j + 1) * C],
                       c_identb, [t1n, "c_identb"], [PTBB])
                R("scalar", lambda e, h=half: e.copy(ogT[:, h * D:(h + 1) * D], ptbB),
                  reads=[PTBB], writes=[ogTn])
            for j in range(4):
                tr(ptbB[:, j * C:(j + 1) * C], mid[:, j * C:(j + 1) * C], c_identb,
                   [midn, "c_identb"], [PTBB])
            R("scalar", lambda e: e.copy(midT, ptbB[:, 0:512]), reads=[PTBB], writes=[midTn])
            for nh in range(2):
                sl = slice(nh * 512, (nh + 1) * 512)
                for k in range(16):
                    mm(pwo[:, sl], ogT[:, k * C:(k + 1) * C],
                       w_wo[:, k * D + nh * 512: k * D + (nh + 1) * 512],
                       k == 0, False, [ogTn, "w_wo"], [PWO])
                for j in range(4):
                    mm(pwo[:, sl], midT[:, j * C:(j + 1) * C],
                       w_bwT[:, j * D + nh * 512: j * D + (nh + 1) * 512],
                       False, j == 3, [midTn, "w_bwT"], [PWO])
            R("vector", lambda e: e.tensor_add(base, base, pwo),
              reads=[basen, PWO], writes=[basen])
            R("sync", lambda e, cc=i: e.dma_start(
                out=ap["y"][cc * C:(cc + 1) * C, :], in_=base),
              reads=[basen], writes=["d:y"], queue="qs")

        B0(0)
        B0(1)
        for i in range(NCH + 1):
            if 0 <= i + 2 < NCH:
                B0(i + 2)
            if 0 <= i - 1 < NCH:
                B2(i - 1)
            if 0 <= i < NCH:
                B1(i)

        p.barrier()
        block = es.enter_context(nc.Block())
        p.finalize(es, block)
    return nc


def _prep_shared(inputs):
    con = _tile_consts()
    w = {}
    w["wq"] = np.ascontiguousarray(inputs["L0_attn_Wq"], np.float32)
    w["wk"] = np.ascontiguousarray(inputs["L0_attn_Wk"], np.float32)
    w["wv"] = np.ascontiguousarray(inputs["L0_attn_Wv"], np.float32)
    w["wgk1"] = np.ascontiguousarray(inputs["L0_attn_Wgk1"], np.float32)
    w["wgk2"] = np.ascontiguousarray(inputs["L0_attn_Wgk2"], np.float32)
    w["bgk2"] = np.ascontiguousarray(inputs["L0_attn_bgk2"], np.float32).reshape(1, D)
    w["wg"] = np.ascontiguousarray(inputs["L0_attn_Wg"], np.float32)
    w["wo"] = np.ascontiguousarray(inputs["L0_attn_Wo"], np.float32)
    w["alay"] = np.ascontiguousarray(
        np.asarray(inputs["L0_ffn_A"], np.float32).reshape(512, D).T)
    w["bwT"] = np.ascontiguousarray(
        np.asarray(inputs["L0_ffn_Bw"], np.float32).transpose(0, 2, 1).reshape(512, D))
    w["wrT"] = np.ascontiguousarray(np.asarray(inputs["L0_ffn_Wr"], np.float32).T)
    w["gnw"] = np.ascontiguousarray(
        np.asarray(inputs["L0_attn_g_norm_weight"], np.float32).reshape(4, 128).T)
    x = np.asarray(inputs["L0_attn_x"], np.float32).reshape(NCORES * TOK, D)
    w["x0"] = np.ascontiguousarray(x[0:1, :])
    return con, w, x


def kernel(**inputs):
    con, w, x = _prep_shared(inputs)

    nc_a = build_phase_a()
    in_a = []
    for cidx in range(NCORES):
        m = dict(
            x=np.ascontiguousarray(x[cidx * TOK:(cidx + 1) * TOK]),
            wq=w["wq"], wk=w["wk"], wv=w["wv"], wgk1=w["wgk1"], wgk2=w["wgk2"],
            bgk2=w["bgk2"],
            ctrib=con["trib"], ctrir=con["trir"], cidentb=con["identb"],
            conesb=con["onesb"], cmask=con["mask"],
            cln16=np.full((C, 1), -LN16, np.float32),
        )
        in_a.append(m)
    res_a = run_bass_kernel_spmd(nc_a, in_a, core_ids=list(range(NCORES)))
    ra = res_a.results

    T_all = np.concatenate([np.asarray(ra[cidx]["T_d"]) for cidx in range(NCORES)], axis=0)
    btot_all = np.zeros((8, D), np.float32)
    for cidx in range(NCORES):
        btot_all[cidx] = np.asarray(ra[cidx]["btot_d"])[0]

    nc_b = build_phase_b()
    in_b = []
    for cidx in range(NCORES):
        m = dict(
            o16_d=np.asarray(ra[cidx]["o16_d"]),
            qgT_d=np.asarray(ra[cidx]["qgT_d"]),
            base_d=np.asarray(ra[cidx]["base_d"]),
            xnT_d=np.asarray(ra[cidx]["xnT_d"]),
            T_all=T_all, btot_all=btot_all,
            x0=w["x0"], wg=w["wg"], wo=w["wo"], alay=w["alay"], bwT=w["bwT"],
            wrT=w["wrT"], gnw=w["gnw"],
            cmatx=_cmat(cidx),
            wmrow=np.where(np.arange(8) >= cidx, -1e4, 0.0).astype(np.float32)
                    .astype(BF).reshape(1, 8),
            conesb=con["onesb"],
            cidentb=con["identb"], cidentf=con["identf"], cesel=con["esel"],
        )
        in_b.append(m)
    res_b = run_bass_kernel_spmd(nc_b, in_b, core_ids=list(range(NCORES)))
    y = np.concatenate([np.asarray(res_b.results[cidx]["y"]) for cidx in range(NCORES)],
                       axis=0)
    return y.reshape(1, NCORES * TOK, D)


# version 07
# speedup vs baseline: 1.3955x; 1.3271x over previous
import numpy as np
import ml_dtypes
from contextlib import ExitStack

import concourse.bass as bass
import concourse.mybir as mybir
from concourse.bass_utils import run_bass_kernel_spmd

F32 = mybir.dt.float32
BF16 = mybir.dt.bfloat16
F16 = mybir.dt.float16
AF = mybir.ActivationFunctionType
ALU = mybir.AluOpType
AX = mybir.AxisListType
I32 = mybir.dt.int32
BF = ml_dtypes.bfloat16

NCORES = 8
TOK = 2048
D = 1024
C = 128
NCH = TOK // C
HEADS = 4
DK = 256
DV = 512
EPS = 1e-5
LN16 = float(np.log(16.0))

ENGS = ("tensor", "vector", "scalar", "gpsimd", "sync")


class Prog:

    NSLOT = 4

    def __init__(self, nc):
        self.nc = nc
        self.ops = {e: [] for e in ENGS}
        self.nops = {e: 0 for e in ENGS}
        self.qcount = {}
        self.last_write = {}
        self.readers = {}
        self.last_psum = {}
        self.waited = {e: {} for e in ENGS}
        self.need_inc = {e: set() for e in ENGS}

    def _src_key(self, tok):
        return ("E", tok[1]) if tok[0] == "E" else ("Q",) + tuple(tok[1])

    def rec(self, eng, emit, reads=(), writes=(), queue=None, tag=""):
        deps = {}

        def add(tok):
            k = self._src_key(tok)
            v = tok[2]
            if k not in deps or deps[k] < v:
                deps[k] = v

        for r in reads:
            t = self.last_write.get(r)
            if t is not None:
                add(t)
            if r.startswith("ps_"):
                t2 = self.last_psum.get(r)
                if t2 is not None:
                    add(t2)
        for w in writes:
            t = self.last_write.get(w)
            if t is not None:
                add(t)
            for kk, vv in self.readers.get(w, {}).items():
                add(("E", kk[1], vv) if kk[0] == "E" else ("Q", (kk[1], kk[2]), vv))
            if w.startswith("ps_"):
                t2 = self.last_psum.get(w)
                if t2 is not None:
                    add(t2)

        if queue is not None:
            idx = None
            n = self.qcount.get(queue, 0) + 1
            self.qcount[queue] = n
            slot = (n - 1) % self.NSLOT
            gen = (n - 1) // self.NSLOT + 1
            my = ("Q", (queue, slot), gen)
            if gen > 1:
                add(("Q", (queue, slot), gen - 1))
        else:
            self.nops[eng] += 1
            idx = self.nops[eng]
            my = ("E", eng, idx)
        myk = self._src_key(my)

        waits = []
        for k, v in deps.items():
            if k[0] == "E" and k[1] == "tensor" and eng == "tensor":
                continue
            got = self.waited[eng].get(k, 0)
            if v <= got:
                continue
            waits.append((k, v))
            self.waited[eng][k] = v
            if k[0] == "E":
                self.need_inc[k[1]].add(v)

        for r in reads:
            self.readers.setdefault(r, {})[myk] = max(
                self.readers.get(r, {}).get(myk, 0), my[2]
            )
            if r.startswith("ps_"):
                self.last_psum[r] = my
        for w in writes:
            self.last_write[w] = my
            self.readers[w] = {}
            if w.startswith("ps_"):
                self.last_psum[w] = my

        self.ops[eng].append(
            dict(emit=emit, waits=waits, queue=queue, idx=idx, tag=tag)
        )
        return my

    def _queue_toks(self):
        toks = []
        for q, n in self.qcount.items():
            for slot in range(min(n, self.NSLOT)):
                gen = (n - 1 - slot) // self.NSLOT + 1
                toks.append(("Q", (q, slot), gen))
        return toks

    def barrier(self):
        toks = []
        for e in ENGS:
            if self.nops[e]:
                toks.append(("E", e, self.nops[e]))
        toks += self._queue_toks()
        for e in ENGS:
            for t in toks:
                k = self._src_key(t)
                if k == ("E", e):
                    continue
                v = t[2]
                if self.waited[e].get(k, 0) < v:
                    self.waited[e][k] = v
                    if k[0] == "E":
                        self.need_inc[k[1]].add(v)
                    self.ops[e].append(dict(emit=None, waits=[(k, v)], queue=None,
                                            idx=None, tag="barrier"))

    def finalize(self, es, block):
        sems = {}
        for e in ENGS:
            sems[("E", e)] = es.enter_context(self.nc.semaphore(f"sem_{e}"))
        for q in self.qcount:
            for slot in range(self.NSLOT):
                sems[("Q", q, slot)] = es.enter_context(
                    self.nc.semaphore(f"sem_{q}_{slot}"))
        inc_prefix = {}
        for e in ENGS:
            pref = {}
            cnt = 0
            marks = self.need_inc[e]
            for i in range(1, self.nops[e] + 1):
                if i in marks:
                    cnt += 1
                pref[i] = cnt
            inc_prefix[e] = pref

        def val_of(k, v):
            if k[0] == "Q":
                return 16 * v
            return inc_prefix[k[1]].get(v, 0)

        qseen = {}

        def body(e):
            def run(eng):
                for op in self.ops[e]:
                    for (k, v) in op["waits"]:
                        eng.wait_ge(sems[k], val_of(k, v))
                    if op["emit"] is None:
                        continue
                    inst = op["emit"](eng)
                    if op["queue"] is not None:
                        q = op["queue"]
                        n = qseen.get(q, 0)
                        qseen[q] = n + 1
                        inst.then_inc(sems[("Q", q, n % self.NSLOT)], 16)
                    elif op["idx"] in self.need_inc[e]:
                        inst.then_inc(sems[("E", e)], 1)
            return run

        for e in ENGS:
            if self.ops[e]:
                getattr(block, e)(body(e))


def _tile_consts():
    i = np.arange(C)
    trib = ((i[:, None] <= i[None, :]).astype(np.float32) / 16.0).astype(BF)
    trir = ((i[:, None] > i[None, :]).astype(np.float32) / 16.0).astype(BF)
    mask = (i[:, None] <= i[None, :]).astype(np.float32)
    identb = np.eye(C, dtype=np.float32).astype(BF)
    identf = np.eye(C, dtype=np.float32)
    onesb = np.ones((C, C), dtype=np.float32).astype(BF)
    esel = np.zeros((8, 512), dtype=np.float32)
    for p in range(512):
        esel[p // 64, p] = 1.0
    return dict(trib=trib, trir=trir, mask=mask, identb=identb, identf=identf,
                onesb=onesb, esel=esel.astype(BF))


def _cmat(core):
    m_ = np.zeros((8, 8), dtype=np.float32)
    for r in range(8):
        for m in range(8):
            if r < m < core:
                m_[m, r] = 1.0
    for r in range(8):
        if r >= core:
            m_[:, r] = 0.0
    return m_.astype(BF)


def _wmask(core):
    m = np.zeros((8,), np.float32)
    m[:core] = 1.0
    return m



def build_phase_a():
    nc = bass.Bass("TRN2", target_bir_lowering=False, debug=False)
    ap = {}
    ap["x"] = nc.dram_tensor("x", [TOK, D], F32, kind="ExternalInput").ap()
    for n, shp in [("wq", [D, D]), ("wk", [D, D]), ("wv", [D, 2 * D]),
                   ("wgk1", [D, 16]), ("wgk2", [16, D]), ("bgk2", [1, D])]:
        ap[n] = nc.dram_tensor(n, shp, F32, kind="ExternalInput").ap()
    for n, shp in [("ctrib", [C, C]), ("ctrir", [C, C]), ("cidentb", [C, C]),
                   ("conesb", [C, C])]:
        ap[n] = nc.dram_tensor(n, shp, BF16, kind="ExternalInput").ap()
    ap["cmask"] = nc.dram_tensor("cmask", [C, C], F32, kind="ExternalInput").ap()
    ap["cidentf"] = nc.dram_tensor("cidentf", [C, C], F32, kind="ExternalInput").ap()
    ap["cln16"] = nc.dram_tensor("cln16", [C, 1], F32, kind="ExternalInput").ap()
    ap["o16_d"] = nc.dram_tensor("o16_d", [NCH, C, 2 * D], F16, kind="ExternalOutput").ap()
    ap["qgT_d"] = nc.dram_tensor("qgT_d", [NCH, C, D], BF16, kind="ExternalOutput").ap()
    ap["base_d"] = nc.dram_tensor("base_d", [NCH, C, D], F32, kind="ExternalOutput").ap()
    ap["xnT_d"] = nc.dram_tensor("xnT_d", [NCH, C, D], BF16, kind="ExternalOutput").ap()
    ap["T_d"] = nc.dram_tensor("T_d", [C, 8 * DV], BF16, kind="ExternalOutput").ap()
    ap["btot_d"] = nc.dram_tensor("btot_d", [1, D], F32, kind="ExternalOutput").ap()

    es = ExitStack()
    with es:
        def sb(n, s, t):
            return es.enter_context(nc.sbuf_tensor(n, s, t))[:]

        def psm(n, s, t):
            return es.enter_context(nc.psum_tensor(n, s, t))[:]

        w_q = sb("w_q", [128, 8 * D], BF16)
        w_k = sb("w_k", [128, 8 * D], BF16)
        w_v = sb("w_v", [128, 8 * 2 * D], BF16)
        w_gk1 = sb("w_gk1", [128, 8 * 16], BF16)
        w_gk2 = sb("w_gk2", [16, D], BF16)
        s_bgk2 = sb("s_bgk2", [1, D], BF16)
        c_trib = sb("c_trib", [C, C], BF16)
        c_trir = sb("c_trir", [C, C], BF16)
        c_identb = sb("c_identb", [C, C], BF16)
        c_ones = sb("c_ones", [C, C], BF16)
        c_mask = sb("c_mask", [C, C], F32)
        c_ln16 = sb("c_ln16", [C, 1], F32)
        s_S = sb("s_S", [128, 8 * DV], F32)
        s_Sb = sb("s_Sb", [128, 8 * DV], BF16)
        s_bpreT = sb("s_bpreT", [C, 8], F32)
        s_btrow = sb("s_btrow", [1, D], F32)
        c_identf = sb("c_identf", [C, C], F32)

        spec = {
            "x": ([C, D], F32, 2), "st": ([C, 8], F32, 2), "xn": ([C, D], F32, 2),
            "xnb": ([C, D], BF16, 2), "xnT": ([C, D], BF16, 3), "f1": ([C, D], F32, 1),
            "ghi": ([C, D], BF16, 2), "glo": ([C, D], BF16, 2),
            "z1e": ([C, 16], BF16, 2), "z1b": ([16, C], BF16, 2),
            "lam": ([C, D], BF16, 2), "lami": ([C, D], BF16, 2),
            "lrev": ([C, D], BF16, 2),
            "qh": ([C, D], BF16, 2),
            "bnd": ([C, 8], F32, 2), "expbp": ([C, 8], F32, 3),
            "kt": ([C, D], BF16, 2), "kh": ([C, D], BF16, 3),
            "v": ([C, 2 * D], BF16, 3),
            "qhT": ([C, D], BF16, 2), "ktT": ([C, D], BF16, 2), "qgT": ([C, D], BF16, 2),
            "pt": ([C, C], BF16, 2), "o16": ([C, 2 * D], F16, 2),
            "dc": ([C, 8], F32, 4),
        }
        tiles = {}
        for nm, (shp, dt2, nb) in spec.items():
            tiles[nm] = [sb(f"t_{nm}_{b}", shp, dt2) for b in range(nb)]

        def T(nm, i):
            nb = spec[nm][2]
            return tiles[nm][i % nb], f"t_{nm}_{i % nb}"

        ptb1 = psm("ptb1", [128, 1024], BF16)
        ptb4 = psm("ptb4", [128, 1024], BF16)
        pc = psm("pc", [128, 1024], F32)
        pp = psm("pp", [128, 1024], F32)
        psc = psm("psc", [128, 512], F32)
        psc2 = psm("psc2", [128, 512], F32)
        PTB1, PTB4, PC, PP, PSC, PSC2 = ("ps_ptb1", "ps_ptb4", "ps_pc", "ps_pp",
                                         "ps_psc", "ps_psc2")

        p = Prog(nc)
        R = p.rec

        def ld(engq, dst, src, rd, wr):
            R(engq[0], lambda e, d=dst, s=src: e.dma_start(out=d, in_=s),
              reads=rd, writes=wr, queue=engq[1])

        gq = ("gpsimd", "qg")
        sq = ("sync", "qs")
        for kt in range(8):
            ld(gq, w_q[:, kt * D:(kt + 1) * D],
               ap["wq"][kt * 128:(kt + 1) * 128, :], ["d:wq"], ["w_q"])
            ld(gq, w_k[:, kt * D:(kt + 1) * D],
               ap["wk"][kt * 128:(kt + 1) * 128, :], ["d:wk"], ["w_k"])
            ld(gq, w_v[:, kt * 2 * D:(kt + 1) * 2 * D],
               ap["wv"][kt * 128:(kt + 1) * 128, :], ["d:wv"], ["w_v"])
        ld(gq, w_gk1.rearrange("p (kt n) -> p kt n", kt=8),
           ap["wgk1"].rearrange("(kt p) n -> p kt n", p=128), ["d:wgk1"], ["w_gk1"])
        ld(gq, w_gk2, ap["wgk2"], ["d:wgk2"], ["w_gk2"])
        ld(gq, s_bgk2, ap["bgk2"], ["d:bgk2"], ["s_bgk2"])
        ld(gq, c_trib, ap["ctrib"], ["d:ctrib"], ["c_trib"])
        ld(gq, c_trir, ap["ctrir"], ["d:ctrir"], ["c_trir"])
        ld(gq, c_identb, ap["cidentb"], ["d:cidentb"], ["c_identb"])
        ld(gq, c_ones, ap["conesb"], ["d:conesb"], ["c_ones"])
        ld(sq, c_mask, ap["cmask"], ["d:cmask"], ["c_mask"])
        ld(sq, c_identf, ap["cidentf"], ["d:cidentf"], ["c_identf"])
        ld(sq, c_ln16, ap["cln16"], ["d:cln16"], ["c_ln16"])
        R("vector", lambda e: e.memset(s_S, 0.0), writes=["s_S"])
        R("vector", lambda e: e.memset(s_Sb, 0.0), writes=["s_Sb"])
        R("vector", lambda e: e.memset(s_bpreT, 0.0), writes=["s_bpreT"])

        def mm(out, lhsT, rhs, start, stop, rd, wr):
            R("tensor", lambda e, o=out, l=lhsT, r=rhs, s1=start, s2=stop:
              e.matmul(o, l, r, start=s1, stop=s2), reads=rd, writes=wr)

        def tr(out, in_, ident, rd, wr):
            R("tensor", lambda e, o=out, i=in_, d=ident: e.transpose(o, i, d),
              reads=rd, writes=wr)

        def S0(i):
            d, dn = T("x", i)
            R("sync", lambda e, dd=d, sl=ap["x"][i * C:(i + 1) * C, :]:
              e.dma_start(out=dd, in_=sl), reads=["d:x"], writes=[dn], queue="qs")

        def S1a(i):
            x, xn_ = T("x", i)
            st, stn = T("st", i)
            xn, xnn = T("xn", i)
            xnb, xnbn = T("xnb", i)
            xnT, xnTn = T("xnT", i)
            f1, f1n = T("f1", i)
            R("vector", lambda e: e.reduce_sum(st[:, 0:1], x, axis=AX.X),
              reads=[xn_], writes=[stn])
            R("scalar", lambda e: e.activation(out=f1, in_=x, func=AF.Square,
                                               accum_out=st[:, 1:2]),
              reads=[xn_], writes=[f1n, stn])
            R("vector", lambda e: e.tensor_scalar(out=st[:, 2:3], in0=st[:, 0:1],
                                                  scalar1=1.0 / D, scalar2=None, op0=ALU.mult),
              reads=[stn], writes=[stn])
            R("vector", lambda e: e.tensor_scalar(out=st[:, 3:4], in0=st[:, 1:2],
                                                  scalar1=1.0 / D, scalar2=None, op0=ALU.mult),
              reads=[stn], writes=[stn])
            R("vector", lambda e: e.tensor_mul(st[:, 4:5], st[:, 2:3], st[:, 2:3]),
              reads=[stn], writes=[stn])
            R("vector", lambda e: e.tensor_sub(st[:, 4:5], st[:, 3:4], st[:, 4:5]),
              reads=[stn], writes=[stn])
            R("vector", lambda e: e.tensor_scalar(out=st[:, 4:5], in0=st[:, 4:5],
                                                  scalar1=1.0, scalar2=EPS, op0=ALU.mult,
                                                  op1=ALU.add),
              reads=[stn], writes=[stn])
            R("scalar", lambda e: e.activation(out=st[:, 5:6], in_=st[:, 4:5], func=AF.Ln),
              reads=[stn], writes=[stn])
            R("scalar", lambda e: e.activation(out=st[:, 6:7], in_=st[:, 5:6], func=AF.Exp,
                                               scale=-0.5),
              reads=[stn], writes=[stn])
            R("vector", lambda e: e.tensor_scalar(out=xn, in0=x, scalar1=st[:, 2:3],
                                                  scalar2=st[:, 6:7], op0=ALU.subtract,
                                                  op1=ALU.mult),
              reads=[xn_, stn], writes=[xnn])
            R("vector", lambda e: e.tensor_copy(xnb, xn), reads=[xnn], writes=[xnbn])
            R("vector", lambda e: e.tensor_add(x, x, xn), reads=[xn_, xnn], writes=[xn_])
            R("sync", lambda e, cc=i: e.dma_start(out=ap["base_d"][cc], in_=x),
              reads=[xn_], writes=["d:base"], queue="qs")
            for j in range(8):
                tr(ptb1[:, j * C:(j + 1) * C], xnb[:, j * C:(j + 1) * C], c_identb,
                   [xnbn, "c_identb"], [PTB1])
            R("scalar", lambda e: e.copy(xnT, ptb1), reads=[PTB1], writes=[xnTn])
            R("sync", lambda e, cc=i: e.dma_start(out=ap["xnT_d"][cc], in_=xnT),
              reads=[xnTn], writes=["d:xnT"], queue="qs")

        def S1b(i):
            xnT, xnTn = T("xnT", i)
            xn, xnn = T("xn", i)
            f1, f1n = T("f1", i)
            ghi, ghin = T("ghi", i)
            glo, glon = T("glo", i)
            z1e, z1en = T("z1e", i)
            z1b, z1bn = T("z1b", i)
            for j in range(8):
                mm(pc[:, 0:16], xnT[:, j * C:(j + 1) * C],
                   w_gk1[:, j * 16:(j + 1) * 16], j == 0, j == 7, [xnTn, "w_gk1"], [PC])
            R("vector", lambda e: e.tensor_copy(z1e, pc[:, 0:16]), reads=[PC], writes=[z1en])
            tr(ptb1[0:16, 0:C], z1e, c_identb, [z1en, "c_identb"], [PTB1])
            R("scalar", lambda e: e.copy(z1b, ptb1[0:16, 0:C]), reads=[PTB1], writes=[z1bn])
            for nh in range(2):
                sl = slice(nh * 512, (nh + 1) * 512)
                mm(pc[:, sl], z1b, w_gk2[:, sl], True, False, [z1bn, "w_gk2"], [PC])
                mm(pc[:, sl], c_ones[0:1, :], s_bgk2[:, sl], False, True,
                   ["c_ones", "s_bgk2"], [PC])
            R("scalar", lambda e: e.activation(out=f1, in_=pc, func=AF.Abs),
              reads=[PC], writes=[f1n])
            R("scalar", lambda e: e.activation(out=f1, in_=f1, func=AF.Exp, scale=-1.0),
              reads=[f1n], writes=[f1n])
            R("scalar", lambda e: e.activation(out=f1, in_=f1, func=AF.Ln, bias=1.0),
              reads=[f1n], writes=[f1n])
            R("vector", lambda e: e.tensor_scalar(out=xn, in0=pc, scalar1=0.0,
                                                  scalar2=None, op0=ALU.min),
              reads=[PC], writes=[xnn])
            R("vector", lambda e: e.tensor_sub(xn, xn, f1), reads=[xnn, f1n], writes=[xnn])
            R("vector", lambda e: e.tensor_copy(ghi, xn), reads=[xnn], writes=[ghin])
            R("vector", lambda e: e.tensor_sub(glo, xn, ghi), reads=[xnn, ghin], writes=[glon])

        def S2a(i):
            ghi, ghin = T("ghi", i)
            glo, glon = T("glo", i)
            lam, lamn = T("lam", i)
            lami, lamin = T("lami", i)
            for nh in range(2):
                sl = slice(nh * 512, (nh + 1) * 512)
                mm(pc[:, sl], c_trib, ghi[:, sl], True, False, ["c_trib", ghin], [PC])
                mm(pc[:, sl], c_trib, glo[:, sl], False, True, ["c_trib", glon], [PC])
            R("scalar", lambda e: e.activation(out=lam, in_=pc, func=AF.Exp, bias=c_ln16),
              reads=[PC, "c_ln16"], writes=[lamn])
            R("scalar", lambda e: e.activation(out=lami, in_=pc, func=AF.Exp, scale=-1.0),
              reads=[PC], writes=[lamin])

        def S2b(i):
            ghi, ghin = T("ghi", i)
            glo, glon = T("glo", i)
            lrev, lrevn = T("lrev", i)
            for nh in range(2):
                sl = slice(nh * 512, (nh + 1) * 512)
                mm(pc[:, sl], c_trir, ghi[:, sl], True, False, ["c_trir", ghin], [PC])
                mm(pc[:, sl], c_trir, glo[:, sl], False, True, ["c_trir", glon], [PC])
            R("scalar", lambda e: e.activation(out=lrev, in_=pc, func=AF.Exp),
              reads=[PC], writes=[lrevn])

        def S2c(i):
            ghi, ghin = T("ghi", i)
            glo, glon = T("glo", i)
            dc, dcn = T("dc", i)
            bnd, bndn = T("bnd", i)
            expbp, expbpn = T("expbp", i)
            for j in range(8):
                mm(pc[:, j:j + 1], ghi[:, j * C:(j + 1) * C], c_ones[:, 0:1],
                   j == 0, False, [ghin, "c_ones"], [PC])
                mm(pc[:, j:j + 1], glo[:, j * C:(j + 1) * C], c_ones[:, 0:1],
                   False, j == 7, [glon, "c_ones"], [PC])
            R("scalar", lambda e: e.activation(out=dc, in_=pc[:, 0:8], func=AF.Exp),
              reads=[PC], writes=[dcn])
            R("vector", lambda e: e.tensor_copy(bnd, pc[:, 0:8]), reads=[PC], writes=[bndn])
            R("scalar", lambda e: e.activation(out=expbp, in_=s_bpreT, func=AF.Exp),
              reads=["s_bpreT"], writes=[expbpn])
            R("vector", lambda e: e.tensor_add(s_bpreT, s_bpreT, bnd),
              reads=["s_bpreT", bndn], writes=["s_bpreT"])

        def S3q(i):
            xnT, xnTn = T("xnT", i)
            lam, lamn = T("lam", i)
            qh, qhn = T("qh", i)
            for j in range(8):
                for nh in range(2):
                    sl = slice(nh * 512, (nh + 1) * 512)
                    mm(pp[:, sl], xnT[:, j * C:(j + 1) * C],
                       w_q[:, j * D + nh * 512: j * D + (nh + 1) * 512],
                       j == 0, j == 7, [xnTn, "w_q"], [PP])
            R("vector", lambda e: e.tensor_mul(qh, pp, lam), reads=[PP, lamn], writes=[qhn])

        def S3k(i):
            xnT, xnTn = T("xnT", i)
            lami, lamin = T("lami", i)
            lrev, lrevn = T("lrev", i)
            kt, ktn = T("kt", i)
            kh, khn = T("kh", i)
            for j in range(8):
                for nh in range(2):
                    sl = slice(nh * 512, (nh + 1) * 512)
                    mm(pp[:, sl], xnT[:, j * C:(j + 1) * C],
                       w_k[:, j * D + nh * 512: j * D + (nh + 1) * 512],
                       j == 0, j == 7, [xnTn, "w_k"], [PP])
            R("vector", lambda e: e.tensor_mul(kt, pp, lami), reads=[PP, lamin], writes=[ktn])
            R("vector", lambda e: e.tensor_mul(kh, pp, lrev), reads=[PP, lrevn], writes=[khn])

        def S3v(i, half):
            xnT, xnTn = T("xnT", i)
            v, vn = T("v", i)
            for j in range(8):
                for nh in range(2):
                    mm(pp[:, nh * 512:(nh + 1) * 512], xnT[:, j * C:(j + 1) * C],
                       w_v[:, j * 2 * D + half * D + nh * 512:
                           j * 2 * D + half * D + (nh + 1) * 512],
                       j == 0, j == 7, [xnTn, "w_v"], [PP])
            R("scalar", lambda e, h=half: e.copy(v[:, h * D:(h + 1) * D], pp),
              reads=[PP], writes=[vn])

        def S4(i):
            for src, dst in (("qh", "qhT"), ("kt", "ktT")):
                sT, sn = T(src, i)
                dT, dn = T(dst, i)
                for j in range(8):
                    tr(ptb4[:, j * C:(j + 1) * C], sT[:, j * C:(j + 1) * C], c_identb,
                       [sn, "c_identb"], [PTB4])
                R("scalar", lambda e, d=dT, ps=ptb4: e.copy(d, ps), reads=[PTB4], writes=[dn])
            qhT, qhTn = T("qhT", i)
            qgT, qgTn = T("qgT", i)
            expbp, expbpn = T("expbp", i)
            for j in range(8):
                R("vector", lambda e, jj=j: e.tensor_scalar(
                    out=qgT[:, jj * C:(jj + 1) * C], in0=qhT[:, jj * C:(jj + 1) * C],
                    scalar1=expbp[:, jj:jj + 1], scalar2=None, op0=ALU.mult),
                  reads=[qhTn, expbpn], writes=[qgTn])
            R("sync", lambda e, cc=i: e.dma_start(out=ap["qgT_d"][cc], in_=qgT),
              reads=[qgTn], writes=["d:qgT"], queue="qs")

        def S5h(i, h):
            qhT, qhTn = T("qhT", i)
            ktT, ktTn = T("ktT", i)
            kh, khn = T("kh", i)
            v, vn = T("v", i)
            pt, ptn = T("pt", i)
            o16, o16n = T("o16", i)
            dc, dcn = T("dc", i)
            for ji, j in enumerate((2 * h, 2 * h + 1)):
                mm(psc[:, 0:C], ktT[:, j * C:(j + 1) * C], qhT[:, j * C:(j + 1) * C],
                   ji == 0, ji == 1, [ktTn, qhTn], [PSC])
            R("vector", lambda e: e.tensor_mul(pt, psc[:, 0:C], c_mask),
              reads=[PSC, "c_mask"], writes=[ptn])
            mm(psc[:, 0:DV], pt, v[:, h * DV:(h + 1) * DV], True, False,
               [ptn, vn], [PSC])
            for ji, j in enumerate((2 * h, 2 * h + 1)):
                mm(psc[:, 0:DV], qhT[:, j * C:(j + 1) * C], s_Sb[:, j * DV:(j + 1) * DV],
                   False, ji == 1, [qhTn, "s_Sb"], [PSC])
            R("scalar", lambda e: e.copy(o16[:, h * DV:(h + 1) * DV], psc[:, 0:DV]),
              reads=[PSC], writes=[o16n])
            for ih in range(2):
                sg = 2 * h + ih
                pbank, pbn = (psc2, PSC2) if ih == 0 else (psc, PSC)
                mm(pbank[:, 0:DV], kh[:, h * DK + ih * C: h * DK + (ih + 1) * C],
                   v[:, h * DV:(h + 1) * DV], True, True, [khn, vn], [pbn])
                R("vector", lambda e, g=sg, pb=pbank: e.scalar_tensor_tensor(
                    out=s_S[:, g * DV:(g + 1) * DV], in0=s_S[:, g * DV:(g + 1) * DV],
                    scalar=dc[:, g:g + 1], in1=pb[:, 0:DV],
                    op0=ALU.mult, op1=ALU.add),
                  reads=["s_S", dcn, pbn], writes=["s_S"])
                R("scalar", lambda e, g=sg: e.copy(s_Sb[:, g * DV:(g + 1) * DV],
                                                   s_S[:, g * DV:(g + 1) * DV]),
                  reads=["s_S"], writes=["s_Sb"])
            if h == 3:
                R("sync", lambda e, cc=i: e.dma_start(out=ap["o16_d"][cc], in_=o16),
                  reads=[o16n], writes=["d:o16"], queue="qs")

        S0(0)
        for i in range(NCH + 5):
            if 0 <= i + 1 < NCH:
                S0(i + 1)
            if 0 <= i - 4 < NCH:
                S5h(i - 4, 0)
            if 0 <= i - 2 < NCH:
                S3q(i - 2)
            if 0 <= i - 4 < NCH:
                S5h(i - 4, 1)
            if 0 <= i - 2 < NCH:
                S3k(i - 2)
            if 0 <= i - 4 < NCH:
                S5h(i - 4, 2)
            if 0 <= i - 2 < NCH:
                S3v(i - 2, 0)
            if 0 <= i - 4 < NCH:
                S5h(i - 4, 3)
            if 0 <= i - 2 < NCH:
                S3v(i - 2, 1)
            if 0 <= i - 1 < NCH:
                S2a(i - 1)
            if 0 <= i - 3 < NCH:
                S4(i - 3)
            if 0 <= i - 1 < NCH:
                S2b(i - 1)
            if 0 <= i < NCH:
                S1a(i)
            if 0 <= i - 1 < NCH:
                S2c(i - 1)
            if 0 <= i < NCH:
                S1b(i)

        R("sync", lambda e: e.dma_start(out=ap["T_d"], in_=s_Sb),
          reads=["s_Sb"], writes=["d:T"], queue="qs")
        for j in range(8):
            tr(pc[0:1, j * C:(j + 1) * C], s_bpreT[:, j:j + 1], c_identf,
               ["s_bpreT", "c_identf"], [PC])
        R("vector", lambda e: e.tensor_copy(s_btrow, pc[0:1, :]),
          reads=[PC], writes=["s_btrow"])
        R("sync", lambda e: e.dma_start(out=ap["btot_d"], in_=s_btrow),
          reads=["s_btrow"], writes=["d:btot"], queue="qs")
        p.barrier()

        block = es.enter_context(nc.Block())
        p.finalize(es, block)
    return nc


def build_phase_b():
    nc = bass.Bass("TRN2", target_bir_lowering=False, debug=False)
    ap = {}
    for n, shp, dt in [("o16_d", [NCH, C, 2 * D], F16), ("qgT_d", [NCH, C, D], BF16),
                       ("base_d", [NCH, C, D], F32), ("xnT_d", [NCH, C, D], BF16),
                       ("T_all", [NCORES * C, 8 * DV], BF16), ("btot_all", [8, D], F32),
                       ("x0", [1, D], F32),
                       ("wg", [D, 2 * D], F32), ("wo", [2 * D, D], F32),
                       ("alay", [D, 512], F32), ("bwT", [512, D], F32),
                       ("wrT", [D, 8], F32), ("gnw", [128, 4], F32),
                       ("cmatx", [8, 8], BF16), ("wmrow", [1, 8], BF16),
                       ("conesb", [C, C], BF16),
                       ("cidentb", [C, C], BF16), ("cidentf", [C, C], F32),
                       ("cesel", [8, 512], BF16)]:
        ap[n] = nc.dram_tensor(n, shp, dt, kind="ExternalInput").ap()
    ap["y"] = nc.dram_tensor("y", [TOK, D], F32, kind="ExternalOutput").ap()

    es = ExitStack()
    with es:
        def sb(n, s, t):
            return es.enter_context(nc.sbuf_tensor(n, s, t))[:]

        def psm(n, s, t):
            return es.enter_context(nc.psum_tensor(n, s, t))[:]

        w_g = sb("w_g", [128, 8 * 2 * D], BF16)
        w_wo = sb("w_wo", [128, 16 * D], BF16)
        w_alay = sb("w_alay", [128, 8 * 512], BF16)
        w_bwT = sb("w_bwT", [128, 4 * D], BF16)
        w_wrT = sb("w_wrT", [128, 8 * 8], F32)
        s_gnw = sb("s_gnw", [128, 4], F32)
        c_identb = sb("c_identb", [C, C], BF16)
        c_identf = sb("c_identf", [C, C], F32)
        c_esel = sb("c_esel", [8, 512], BF16)
        c_cmat = sb("c_cmat", [8, 8], BF16)
        c_ones = sb("c_ones", [C, C], BF16)
        s_wmrow = sb("s_wmrow", [1, 8], BF16)
        s_x0 = sb("s_x0", [1, D], F32)
        s_x0b = sb("s_x0b", [1, D], F32)
        g_st = sb("g_st", [1, 8], F32)
        g_junk = sb("g_junk", [1, D], F32)
        s_x0T = sb("s_x0T", [128, 8], F32)
        g_lg8 = sb("g_lg8", [8, 1], F32)
        g_lg = sb("g_lg", [1, 8], F32)
        g_d = sb("g_d", [1, 8], F32)
        g_ind = sb("g_ind", [1, 8], F32)
        g_l2 = sb("g_l2", [1, 8], F32)
        g_keep = sb("g_keep", [1, 8], F32)
        g_e = sb("g_e", [1, 8], F32)
        g_gt = sb("g_gt", [1, 8], F32)
        g_gc8 = sb("g_gc8", [8, 1], F32)
        g_gcb = sb("g_gcb", [8, 4], BF16)
        s_gc = sb("s_gc", [128, 4], F32)
        b_bt = sb("b_bt", [8, D], F32)
        b_bthi = sb("b_bthi", [8, D], BF16)
        b_btlo = sb("b_btlo", [8, D], BF16)
        b_w = sb("b_w", [128, 64], F32)
        b_Tr = [sb("b_Tr_0", [128, 8 * DV], BF16)]
        s_Sin = sb("s_Sin", [128, 8 * DV], F32)
        s_Sinb = sb("s_Sinb", [128, 8 * DV], BF16)

        spec = {
            "xnT": ([C, D], BF16, 3), "o16": ([C, 2 * D], F16, 3),
            "qgT": ([C, D], BF16, 3), "base": ([C, D], F32, 4),
            "ob": ([C, 2 * D], BF16, 2), "rs": ([C, 4], F32, 2),
            "sd": ([C, 4], F32, 2), "rst": ([C, 4], F32, 2),
            "junk": ([C, DV], F32, 1), "sg": ([C, 2 * D], BF16, 2),
            "ogT": ([C, 2 * D], BF16, 2),
            "mid": ([C, 512], BF16, 2), "midT": ([C, 512], BF16, 2),
        }
        tiles = {}
        for nm, (shp, dt2, nb) in spec.items():
            tiles[nm] = [sb(f"b_{nm}_{b}", shp, dt2) for b in range(nb)]

        def T(nm, i):
            nb = spec[nm][2]
            return tiles[nm][i % nb], f"b_{nm}_{i % nb}"

        pgA = psm("pgA", [128, 1024], F32)
        pd = psm("pd", [128, 1024], F32)
        pm = psm("pm", [128, 512], F32)
        pwo = psm("pwo", [128, 1024], F32)
        ptbB = psm("ptbB", [128, 1024], BF16)
        PGA, PD, PM, PWO, PTBB = "ps_pgA", "ps_pd", "ps_pm", "ps_pwo", "ps_ptbB"

        p = Prog(nc)
        R = p.rec

        def mm(out, lhsT, rhs, start, stop, rd, wr):
            R("tensor", lambda e, o=out, l=lhsT, r=rhs, s1=start, s2=stop:
              e.matmul(o, l, r, start=s1, stop=s2), reads=rd, writes=wr)

        def tr(out, in_, ident, rd, wr):
            R("tensor", lambda e, o=out, i=in_, d=ident: e.transpose(o, i, d),
              reads=rd, writes=wr)

        def ldg(dst, src, rd, wr):
            R("gpsimd", lambda e, d=dst, s=src: e.dma_start(out=d, in_=s),
              reads=rd, writes=wr, queue="qg")

        for kt in range(8):
            ldg(w_g[:, kt * 2 * D:(kt + 1) * 2 * D],
                ap["wg"][kt * 128:(kt + 1) * 128, :], ["d:wg"], ["w_g"])
        for kt in range(16):
            ldg(w_wo[:, kt * D:(kt + 1) * D],
                ap["wo"][kt * 128:(kt + 1) * 128, :], ["d:wo"], ["w_wo"])
        ldg(w_alay.rearrange("p (kt n) -> p kt n", kt=8),
            ap["alay"].rearrange("(kt p) n -> p kt n", p=128), ["d:alay"], ["w_alay"])
        ldg(w_bwT.rearrange("p (kt n) -> p kt n", kt=4),
            ap["bwT"].rearrange("(kt p) n -> p kt n", p=128), ["d:bwT"], ["w_bwT"])
        ldg(c_identb, ap["cidentb"], ["d:cidentb"], ["c_identb"])
        ldg(c_esel, ap["cesel"], ["d:cesel"], ["c_esel"])
        ldg(c_cmat, ap["cmatx"], ["d:cmatx"], ["c_cmat"])
        ldg(c_ones, ap["conesb"], ["d:conesb"], ["c_ones"])
        ldg(s_wmrow, ap["wmrow"], ["d:wmrow"], ["s_wmrow"])
        R("sync", lambda e: e.dma_start(
            out=w_wrT.rearrange("p (kt n) -> p kt n", kt=8),
            in_=ap["wrT"].rearrange("(kt p) n -> p kt n", p=128)),
          reads=["d:wrT"], writes=["w_wrT"], queue="qs")
        R("sync", lambda e: e.dma_start(out=s_gnw, in_=ap["gnw"]),
          reads=["d:gnw"], writes=["s_gnw"], queue="qs")
        R("sync", lambda e: e.dma_start(out=c_identf, in_=ap["cidentf"]),
          reads=["d:cidentf"], writes=["c_identf"], queue="qs")
        R("sync", lambda e: e.dma_start(out=s_x0, in_=ap["x0"]),
          reads=["d:x0"], writes=["s_x0"], queue="qs")
        R("sync", lambda e: e.dma_start(out=b_bt, in_=ap["btot_all"]),
          reads=["d:btot"], writes=["b_bt"], queue="qs")

        R("vector", lambda e: e.reduce_sum(g_st[:, 0:1], s_x0, axis=AX.X),
          reads=["s_x0"], writes=["g_st"])
        R("scalar", lambda e: e.activation(out=g_junk, in_=s_x0, func=AF.Square,
                                           accum_out=g_st[:, 1:2]),
          reads=["s_x0"], writes=["g_junk", "g_st"])
        R("vector", lambda e: e.tensor_scalar(out=g_st[:, 2:3], in0=g_st[:, 0:1],
                                              scalar1=1.0 / D, scalar2=None, op0=ALU.mult),
          reads=["g_st"], writes=["g_st"])
        R("vector", lambda e: e.tensor_scalar(out=g_st[:, 3:4], in0=g_st[:, 1:2],
                                              scalar1=1.0 / D, scalar2=None, op0=ALU.mult),
          reads=["g_st"], writes=["g_st"])
        R("vector", lambda e: e.tensor_mul(g_st[:, 4:5], g_st[:, 2:3], g_st[:, 2:3]),
          reads=["g_st"], writes=["g_st"])
        R("vector", lambda e: e.tensor_sub(g_st[:, 4:5], g_st[:, 3:4], g_st[:, 4:5]),
          reads=["g_st"], writes=["g_st"])
        R("vector", lambda e: e.tensor_scalar(out=g_st[:, 4:5], in0=g_st[:, 4:5],
                                              scalar1=1.0, scalar2=EPS, op0=ALU.mult,
                                              op1=ALU.add),
          reads=["g_st"], writes=["g_st"])
        R("scalar", lambda e: e.activation(out=g_st[:, 5:6], in_=g_st[:, 4:5], func=AF.Ln),
          reads=["g_st"], writes=["g_st"])
        R("scalar", lambda e: e.activation(out=g_st[:, 6:7], in_=g_st[:, 5:6], func=AF.Exp,
                                           scale=-0.5),
          reads=["g_st"], writes=["g_st"])
        R("vector", lambda e: e.tensor_scalar(out=s_x0b, in0=s_x0, scalar1=g_st[:, 2:3],
                                              scalar2=g_st[:, 6:7], op0=ALU.subtract,
                                              op1=ALU.mult),
          reads=["s_x0", "g_st"], writes=["s_x0b"])
        for j in range(8):
            tr(pm[:, j:j + 1], s_x0b[:, j * C:(j + 1) * C], c_identf[0:1, 0:1],
               ["s_x0b", "c_identf"], [PM])
        R("scalar", lambda e: e.copy(s_x0T, pm[:, 0:8]), reads=[PM], writes=["s_x0T"])
        for j in range(8):
            mm(pd[0:8, 0:1], w_wrT[:, j * 8:(j + 1) * 8], s_x0T[:, j:j + 1],
               j == 0, j == 7, ["w_wrT", "s_x0T"], [PD])
        R("scalar", lambda e: e.copy(g_lg8, pd[0:8, 0:1]), reads=[PD], writes=["g_lg8"])
        tr(pm[0:1, 0:8], g_lg8, c_identf[0:8, 0:8], ["g_lg8", "c_identf"], [PM])
        R("scalar", lambda e: e.copy(g_lg, pm[0:1, 0:8]), reads=[PM], writes=["g_lg"])
        R("vector", lambda e: e.reduce_max(g_st[:, 7:8], g_lg, axis=AX.X),
          reads=["g_lg"], writes=["g_st"])
        R("vector", lambda e: e.tensor_scalar(out=g_d, in0=g_lg, scalar1=g_st[:, 7:8],
                                              scalar2=None, op0=ALU.subtract),
          reads=["g_lg", "g_st"], writes=["g_d"])
        R("vector", lambda e: e.tensor_scalar(out=g_ind, in0=g_d, scalar1=0.0,
                                              scalar2=None, op0=ALU.is_ge),
          reads=["g_d"], writes=["g_ind"])
        R("vector", lambda e: e.scalar_tensor_tensor(out=g_l2, in0=g_ind, scalar=-1e30,
                                                     in1=g_lg, op0=ALU.mult, op1=ALU.add),
          reads=["g_ind", "g_lg"], writes=["g_l2"])
        R("vector", lambda e: e.reduce_max(g_st[:, 0:1], g_l2, axis=AX.X),
          reads=["g_l2"], writes=["g_st"])
        R("vector", lambda e: e.tensor_scalar(out=g_keep, in0=g_lg, scalar1=g_st[:, 0:1],
                                              scalar2=None, op0=ALU.is_ge),
          reads=["g_lg", "g_st"], writes=["g_keep"])
        R("scalar", lambda e: e.activation(out=g_e, in_=g_d, func=AF.Exp),
          reads=["g_d"], writes=["g_e"])
        R("vector", lambda e: e.tensor_mul(g_e, g_e, g_keep),
          reads=["g_e", "g_keep"], writes=["g_e"])
        R("vector", lambda e: e.reduce_sum(g_st[:, 1:2], g_e, axis=AX.X),
          reads=["g_e"], writes=["g_st"])
        R("vector", lambda e: e.reciprocal(g_st[:, 2:3], g_st[:, 1:2]),
          reads=["g_st"], writes=["g_st"])
        R("vector", lambda e: e.tensor_scalar(out=g_gt, in0=g_e, scalar1=g_st[:, 2:3],
                                              scalar2=None, op0=ALU.mult),
          reads=["g_e", "g_st"], writes=["g_gt"])
        tr(pm[0:8, 8:9], g_gt, c_identf[0:1, 0:1], ["g_gt", "c_identf"], [PM])
        R("scalar", lambda e: e.copy(g_gc8, pm[0:8, 8:9]), reads=[PM], writes=["g_gc8"])
        R("vector", lambda e: e.tensor_copy(g_gcb[:, 0:1], g_gc8),
          reads=["g_gc8"], writes=["g_gcb"])
        R("vector", lambda e: e.tensor_sub(g_gcb[:, 2:3], g_gc8, g_gcb[:, 0:1]),
          reads=["g_gc8", "g_gcb"], writes=["g_gcb"])
        for tau in range(4):
            for hl in range(2):
                mm(pm[:, tau:tau + 1], c_esel[:, tau * 128:(tau + 1) * 128],
                   g_gcb[:, 2 * hl:2 * hl + 1], (tau == 0 and hl == 0),
                   (tau == 3 and hl == 1), ["c_esel", "g_gcb"], [PM])
        R("scalar", lambda e: e.copy(s_gc, pm[:, 0:4]), reads=[PM], writes=["s_gc"])
        for tau in range(4):
            R("vector", lambda e, u=tau: e.tensor_scalar(
                out=w_bwT[:, u * D:(u + 1) * D], in0=w_bwT[:, u * D:(u + 1) * D],
                scalar1=s_gc[:, u:u + 1], scalar2=None, op0=ALU.mult),
              reads=["w_bwT", "s_gc"], writes=["w_bwT"])
        for k in range(16):
            R("vector", lambda e, kk=k: e.tensor_scalar(
                out=w_wo[:, kk * D:(kk + 1) * D], in0=w_wo[:, kk * D:(kk + 1) * D],
                scalar1=s_gnw[:, kk % 4:kk % 4 + 1], scalar2=None, op0=ALU.mult),
              reads=["w_wo", "s_gnw"], writes=["w_wo"])

        R("vector", lambda e: e.tensor_copy(b_bthi, b_bt), reads=["b_bt"], writes=["b_bthi"])
        R("vector", lambda e: e.tensor_sub(b_btlo, b_bt, b_bthi),
          reads=["b_bt", "b_bthi"], writes=["b_btlo"])
        for tau in range(8):
            mm(pm[:, tau * 8:(tau + 1) * 8], b_bthi[:, tau * 128:(tau + 1) * 128],
               c_cmat, tau == 0, False, ["b_bthi", "c_cmat"], [PM])
            mm(pm[:, tau * 8:(tau + 1) * 8], b_btlo[:, tau * 128:(tau + 1) * 128],
               c_cmat, False, False, ["b_btlo", "c_cmat"], [PM])
            mm(pm[:, tau * 8:(tau + 1) * 8], c_ones[0:1, :], s_wmrow,
               False, tau == 7, ["c_ones", "s_wmrow"], [PM])
        R("scalar", lambda e: e.activation(out=b_w, in_=pm[:, 0:64], func=AF.Exp),
          reads=[PM], writes=["b_w"])
        R("vector", lambda e: e.memset(s_Sin, 0.0), writes=["s_Sin"])
        for r in range(NCORES):
            tb = b_Tr[0]
            tbn = "b_Tr_0"
            R("sync", lambda e, rr=r, d=tb: e.dma_start(
                out=d, in_=ap["T_all"][rr * C:(rr + 1) * C, :]),
              reads=["d:Tall"], writes=[tbn], queue="qs")
            for sg_ in range(8):
                R("vector", lambda e, g=sg_, rr=r, d=tb: e.scalar_tensor_tensor(
                    out=s_Sin[:, g * DV:(g + 1) * DV], in0=d[:, g * DV:(g + 1) * DV],
                    scalar=b_w[:, g * 8 + rr:g * 8 + rr + 1],
                    in1=s_Sin[:, g * DV:(g + 1) * DV],
                    op0=ALU.mult, op1=ALU.add),
                  reads=[tbn, "b_w", "s_Sin"], writes=["s_Sin"])
        R("scalar", lambda e: e.copy(s_Sinb, s_Sin), reads=["s_Sin"], writes=["s_Sinb"])

        def B0(i):
            for nm, key in [("xnT", "xnT_d"), ("o16", "o16_d"),
                            ("qgT", "qgT_d"), ("base", "base_d")]:
                d, dn = T(nm, i)
                R("sync", lambda e, cc=i, dd=d, k=key: e.dma_start(out=dd, in_=ap[k][cc]),
                  reads=["d:" + key], writes=[dn], queue="qs")

        def B1(i):
            xnT, xnTn = T("xnT", i)
            o16, o16n = T("o16", i)
            qgT, qgTn = T("qgT", i)
            ob, obn = T("ob", i)
            rs, rsn = T("rs", i)
            sd, sdn = T("sd", i)
            rst, rstn = T("rst", i)
            junk, junkn = T("junk", i)
            sg, sgn = T("sg", i)
            t1, t1n = ob, obn
            mid, midn = T("mid", i)
            for j in range(8):
                for nh in range(2):
                    mm(pgA[:, nh * 512:(nh + 1) * 512], xnT[:, j * C:(j + 1) * C],
                       w_g[:, j * 2 * D + nh * 512: j * 2 * D + (nh + 1) * 512],
                       j == 0, j == 7, [xnTn, "w_g"], [PGA])
            for hi in range(2):
                for ji, j in enumerate((2 * hi, 2 * hi + 1)):
                    mm(pd[:, hi * DV:(hi + 1) * DV], qgT[:, j * C:(j + 1) * C],
                       s_Sinb[:, j * DV:(j + 1) * DV], ji == 0, ji == 1,
                       [qgTn, "s_Sinb"], [PD])
            R("scalar", lambda e: e.activation(out=sg[:, 0:D], in_=pgA, func=AF.Silu),
              reads=[PGA], writes=[sgn])
            for j in range(8):
                for nh in range(2):
                    mm(pgA[:, nh * 512:(nh + 1) * 512], xnT[:, j * C:(j + 1) * C],
                       w_g[:, j * 2 * D + D + nh * 512: j * 2 * D + D + (nh + 1) * 512],
                       j == 0, j == 7, [xnTn, "w_g"], [PGA])
            for hi in range(2):
                R("vector", lambda e, h=hi: e.tensor_tensor(
                    ob[:, h * DV:(h + 1) * DV], o16[:, h * DV:(h + 1) * DV],
                    pd[:, h * DV:(h + 1) * DV], ALU.add),
                  reads=[o16n, PD], writes=[obn])
            for hi in range(2):
                h = 2 + hi
                for ji, j in enumerate((2 * h, 2 * h + 1)):
                    mm(pd[:, hi * DV:(hi + 1) * DV], qgT[:, j * C:(j + 1) * C],
                       s_Sinb[:, j * DV:(j + 1) * DV], ji == 0, ji == 1,
                       [qgTn, "s_Sinb"], [PD])
            R("scalar", lambda e: e.activation(out=sg[:, D:2 * D], in_=pgA, func=AF.Silu),
              reads=[PGA], writes=[sgn])
            for j in range(8):
                mm(pm[:, 0:512], xnT[:, j * C:(j + 1) * C],
                   w_alay[:, j * 512:(j + 1) * 512], j == 0, j == 7,
                   [xnTn, "w_alay"], [PM])
            for hi in range(2):
                h = 2 + hi
                R("vector", lambda e, h2=h, i2=hi: e.tensor_tensor(
                    ob[:, h2 * DV:(h2 + 1) * DV], o16[:, h2 * DV:(h2 + 1) * DV],
                    pd[:, i2 * DV:(i2 + 1) * DV], ALU.add),
                  reads=[o16n, PD], writes=[obn])
            R("vector", lambda e: e.tensor_copy(mid, pm[:, 0:512]), reads=[PM], writes=[midn])
            for h in range(4):
                R("scalar", lambda e, hh=h: e.activation(
                    out=junk, in_=ob[:, hh * DV:(hh + 1) * DV], func=AF.Square,
                    accum_out=rs[:, hh:hh + 1]),
                  reads=[obn], writes=[junkn, rsn])
            R("vector", lambda e: e.tensor_scalar(out=rs, in0=rs, scalar1=1.0 / DV,
                                                  scalar2=EPS, op0=ALU.mult, op1=ALU.add),
              reads=[rsn], writes=[rsn])
            R("vector", lambda e: e.tensor_scalar(
                out=rst[:].bitcast(I32), in0=rs[:].bitcast(I32), scalar1=1,
                scalar2=None, op0=ALU.logical_shift_right),
              reads=[rsn], writes=[rstn])
            R("vector", lambda e: e.tensor_scalar(
                out=rst[:].bitcast(I32), in0=rst[:].bitcast(I32), scalar1=-1,
                scalar2=0x5f3759df, op0=ALU.mult, op1=ALU.add),
              reads=[rstn], writes=[rstn])
            for _ in range(3):
                R("vector", lambda e: e.tensor_mul(sd, rst, rst), reads=[rstn], writes=[sdn])
                R("vector", lambda e: e.tensor_mul(sd, sd, rs), reads=[sdn, rsn], writes=[sdn])
                R("vector", lambda e: e.tensor_scalar(out=sd, in0=sd, scalar1=-0.5,
                                                      scalar2=1.5, op0=ALU.mult, op1=ALU.add),
                  reads=[sdn], writes=[sdn])
                R("vector", lambda e: e.tensor_mul(rst, rst, sd), reads=[rstn, sdn], writes=[rstn])
            R("vector", lambda e: e.tensor_mul(t1, ob, sg),
              reads=[obn, sgn], writes=[t1n])
            for h in range(4):
                R("vector", lambda e, hh=h: e.tensor_scalar(
                    out=t1[:, hh * DV:(hh + 1) * DV], in0=t1[:, hh * DV:(hh + 1) * DV],
                    scalar1=rst[:, hh:hh + 1], scalar2=None, op0=ALU.mult),
                  reads=[t1n, rstn], writes=[t1n])

        def B2(i):
            t1, t1n = T("ob", i)
            ogT, ogTn = T("ogT", i)
            mid, midn = T("mid", i)
            midT, midTn = T("midT", i)
            base, basen = T("base", i)
            for half in range(2):
                for j in range(8):
                    tr(ptbB[:, j * C:(j + 1) * C],
                       t1[:, half * D + j * C: half * D + (j + 1) * C],
                       c_identb, [t1n, "c_identb"], [PTBB])
                R("scalar", lambda e, h=half: e.copy(ogT[:, h * D:(h + 1) * D], ptbB),
                  reads=[PTBB], writes=[ogTn])
            for j in range(4):
                tr(ptbB[:, j * C:(j + 1) * C], mid[:, j * C:(j + 1) * C], c_identb,
                   [midn, "c_identb"], [PTBB])
            R("scalar", lambda e: e.copy(midT, ptbB[:, 0:512]), reads=[PTBB], writes=[midTn])
            for nh in range(2):
                sl = slice(nh * 512, (nh + 1) * 512)
                for k in range(16):
                    mm(pwo[:, sl], ogT[:, k * C:(k + 1) * C],
                       w_wo[:, k * D + nh * 512: k * D + (nh + 1) * 512],
                       k == 0, False, [ogTn, "w_wo"], [PWO])
                for j in range(4):
                    mm(pwo[:, sl], midT[:, j * C:(j + 1) * C],
                       w_bwT[:, j * D + nh * 512: j * D + (nh + 1) * 512],
                       False, j == 3, [midTn, "w_bwT"], [PWO])
            R("vector", lambda e: e.tensor_add(base, base, pwo),
              reads=[basen, PWO], writes=[basen])
            R("sync", lambda e, cc=i: e.dma_start(
                out=ap["y"][cc * C:(cc + 1) * C, :], in_=base),
              reads=[basen], writes=["d:y"], queue="qs")

        B0(0)
        B0(1)
        for i in range(NCH + 1):
            if 0 <= i + 2 < NCH:
                B0(i + 2)
            if 0 <= i - 1 < NCH:
                B2(i - 1)
            if 0 <= i < NCH:
                B1(i)

        p.barrier()
        block = es.enter_context(nc.Block())
        p.finalize(es, block)
    return nc


def _prep_shared(inputs):
    con = _tile_consts()
    w = {}
    w["wq"] = np.ascontiguousarray(inputs["L0_attn_Wq"], np.float32)
    w["wk"] = np.ascontiguousarray(inputs["L0_attn_Wk"], np.float32)
    w["wv"] = np.ascontiguousarray(inputs["L0_attn_Wv"], np.float32)
    w["wgk1"] = np.ascontiguousarray(inputs["L0_attn_Wgk1"], np.float32)
    w["wgk2"] = np.ascontiguousarray(inputs["L0_attn_Wgk2"], np.float32)
    w["bgk2"] = np.ascontiguousarray(inputs["L0_attn_bgk2"], np.float32).reshape(1, D)
    w["wg"] = np.ascontiguousarray(inputs["L0_attn_Wg"], np.float32)
    w["wo"] = np.ascontiguousarray(inputs["L0_attn_Wo"], np.float32)
    w["alay"] = np.ascontiguousarray(
        np.asarray(inputs["L0_ffn_A"], np.float32).reshape(512, D).T)
    w["bwT"] = np.ascontiguousarray(
        np.asarray(inputs["L0_ffn_Bw"], np.float32).transpose(0, 2, 1).reshape(512, D))
    w["wrT"] = np.ascontiguousarray(np.asarray(inputs["L0_ffn_Wr"], np.float32).T)
    w["gnw"] = np.ascontiguousarray(
        np.asarray(inputs["L0_attn_g_norm_weight"], np.float32).reshape(4, 128).T)
    x = np.asarray(inputs["L0_attn_x"], np.float32).reshape(NCORES * TOK, D)
    w["x0"] = np.ascontiguousarray(x[0:1, :])
    return con, w, x


def kernel(**inputs):
    con, w, x = _prep_shared(inputs)

    nc_a = build_phase_a()
    in_a = []
    for cidx in range(NCORES):
        m = dict(
            x=np.ascontiguousarray(x[cidx * TOK:(cidx + 1) * TOK]),
            wq=w["wq"], wk=w["wk"], wv=w["wv"], wgk1=w["wgk1"], wgk2=w["wgk2"],
            bgk2=w["bgk2"],
            ctrib=con["trib"], ctrir=con["trir"], cidentb=con["identb"],
            conesb=con["onesb"], cmask=con["mask"], cidentf=con["identf"],
            cln16=np.full((C, 1), -LN16, np.float32),
        )
        in_a.append(m)
    res_a = run_bass_kernel_spmd(nc_a, in_a, core_ids=list(range(NCORES)))
    ra = res_a.results

    T_all = np.concatenate([np.asarray(ra[cidx]["T_d"]) for cidx in range(NCORES)], axis=0)
    btot_all = np.zeros((8, D), np.float32)
    for cidx in range(NCORES):
        btot_all[cidx] = np.asarray(ra[cidx]["btot_d"])[0]

    nc_b = build_phase_b()
    in_b = []
    for cidx in range(NCORES):
        m = dict(
            o16_d=np.asarray(ra[cidx]["o16_d"]),
            qgT_d=np.asarray(ra[cidx]["qgT_d"]),
            base_d=np.asarray(ra[cidx]["base_d"]),
            xnT_d=np.asarray(ra[cidx]["xnT_d"]),
            T_all=T_all, btot_all=btot_all,
            x0=w["x0"], wg=w["wg"], wo=w["wo"], alay=w["alay"], bwT=w["bwT"],
            wrT=w["wrT"], gnw=w["gnw"],
            cmatx=_cmat(cidx),
            wmrow=np.where(np.arange(8) >= cidx, -1e4, 0.0).astype(np.float32)
                    .astype(BF).reshape(1, 8),
            conesb=con["onesb"],
            cidentb=con["identb"], cidentf=con["identf"], cesel=con["esel"],
        )
        in_b.append(m)
    res_b = run_bass_kernel_spmd(nc_b, in_b, core_ids=list(range(NCORES)))
    y = np.concatenate([np.asarray(res_b.results[cidx]["y"]) for cidx in range(NCORES)],
                       axis=0)
    return y.reshape(1, NCORES * TOK, D)


# version 08
# speedup vs baseline: 1.6352x; 1.1717x over previous
import numpy as np
import ml_dtypes
from contextlib import ExitStack

import concourse.bass as bass
import concourse.mybir as mybir
from concourse.bass_utils import run_bass_kernel_spmd

F32 = mybir.dt.float32
BF16 = mybir.dt.bfloat16
F16 = mybir.dt.float16
AF = mybir.ActivationFunctionType
ALU = mybir.AluOpType
AX = mybir.AxisListType
I32 = mybir.dt.int32
BF = ml_dtypes.bfloat16

NCORES = 8
TOK = 2048
D = 1024
C = 128
NCH = TOK // C
HEADS = 4
DK = 256
DV = 512
EPS = 1e-5
LN16 = float(np.log(16.0))

ENGS = ("tensor", "vector", "scalar", "gpsimd", "sync")


class Prog:

    NSLOT = 4

    def __init__(self, nc):
        self.nc = nc
        self.ops = {e: [] for e in ENGS}
        self.nops = {e: 0 for e in ENGS}
        self.qcount = {}
        self.last_write = {}
        self.readers = {}
        self.last_psum = {}
        self.waited = {e: {} for e in ENGS}
        self.need_inc = {e: set() for e in ENGS}

    def _src_key(self, tok):
        return ("E", tok[1]) if tok[0] == "E" else ("Q",) + tuple(tok[1])

    def rec(self, eng, emit, reads=(), writes=(), queue=None, tag=""):
        deps = {}

        def add(tok):
            k = self._src_key(tok)
            v = tok[2]
            if k not in deps or deps[k] < v:
                deps[k] = v

        for r in reads:
            t = self.last_write.get(r)
            if t is not None:
                add(t)
            if r.startswith("ps_"):
                t2 = self.last_psum.get(r)
                if t2 is not None:
                    add(t2)
        for w in writes:
            t = self.last_write.get(w)
            if t is not None:
                add(t)
            for kk, vv in self.readers.get(w, {}).items():
                add(("E", kk[1], vv) if kk[0] == "E" else ("Q", (kk[1], kk[2]), vv))
            if w.startswith("ps_"):
                t2 = self.last_psum.get(w)
                if t2 is not None:
                    add(t2)

        if queue is not None:
            idx = None
            n = self.qcount.get(queue, 0) + 1
            self.qcount[queue] = n
            slot = (n - 1) % self.NSLOT
            gen = (n - 1) // self.NSLOT + 1
            my = ("Q", (queue, slot), gen)
            if gen > 1:
                add(("Q", (queue, slot), gen - 1))
        else:
            self.nops[eng] += 1
            idx = self.nops[eng]
            my = ("E", eng, idx)
        myk = self._src_key(my)

        waits = []
        for k, v in deps.items():
            if k[0] == "E" and k[1] == "tensor" and eng == "tensor":
                continue
            got = self.waited[eng].get(k, 0)
            if v <= got:
                continue
            waits.append((k, v))
            self.waited[eng][k] = v
            if k[0] == "E":
                self.need_inc[k[1]].add(v)

        for r in reads:
            self.readers.setdefault(r, {})[myk] = max(
                self.readers.get(r, {}).get(myk, 0), my[2]
            )
            if r.startswith("ps_"):
                self.last_psum[r] = my
        for w in writes:
            self.last_write[w] = my
            self.readers[w] = {}
            if w.startswith("ps_"):
                self.last_psum[w] = my

        self.ops[eng].append(
            dict(emit=emit, waits=waits, queue=queue, idx=idx, tag=tag)
        )
        return my

    def _queue_toks(self):
        toks = []
        for q, n in self.qcount.items():
            for slot in range(min(n, self.NSLOT)):
                gen = (n - 1 - slot) // self.NSLOT + 1
                toks.append(("Q", (q, slot), gen))
        return toks

    def barrier(self):
        toks = []
        for e in ENGS:
            if self.nops[e]:
                toks.append(("E", e, self.nops[e]))
        toks += self._queue_toks()
        for e in ENGS:
            for t in toks:
                k = self._src_key(t)
                if k == ("E", e):
                    continue
                v = t[2]
                if self.waited[e].get(k, 0) < v:
                    self.waited[e][k] = v
                    if k[0] == "E":
                        self.need_inc[k[1]].add(v)
                    self.ops[e].append(dict(emit=None, waits=[(k, v)], queue=None,
                                            idx=None, tag="barrier"))

    def finalize(self, es, block):
        sems = {}
        for e in ENGS:
            sems[("E", e)] = es.enter_context(self.nc.semaphore(f"sem_{e}"))
        for q in self.qcount:
            for slot in range(self.NSLOT):
                sems[("Q", q, slot)] = es.enter_context(
                    self.nc.semaphore(f"sem_{q}_{slot}"))
        inc_prefix = {}
        for e in ENGS:
            pref = {}
            cnt = 0
            marks = self.need_inc[e]
            for i in range(1, self.nops[e] + 1):
                if i in marks:
                    cnt += 1
                pref[i] = cnt
            inc_prefix[e] = pref

        def val_of(k, v):
            if k[0] == "Q":
                return 16 * v
            return inc_prefix[k[1]].get(v, 0)

        qseen = {}

        def body(e):
            def run(eng):
                for op in self.ops[e]:
                    for (k, v) in op["waits"]:
                        eng.wait_ge(sems[k], val_of(k, v))
                    if op["emit"] is None:
                        continue
                    inst = op["emit"](eng)
                    if op["queue"] is not None:
                        q = op["queue"]
                        n = qseen.get(q, 0)
                        qseen[q] = n + 1
                        inst.then_inc(sems[("Q", q, n % self.NSLOT)], 16)
                    elif op["idx"] in self.need_inc[e]:
                        inst.then_inc(sems[("E", e)], 1)
            return run

        for e in ENGS:
            if self.ops[e]:
                getattr(block, e)(body(e))


def _tile_consts():
    i = np.arange(C)
    trib = ((i[:, None] <= i[None, :]).astype(np.float32) / 16.0).astype(BF)
    trir = ((i[:, None] > i[None, :]).astype(np.float32) / 16.0).astype(BF)
    mask = (i[:, None] <= i[None, :]).astype(np.float32)
    identb = np.eye(C, dtype=np.float32).astype(BF)
    identf = np.eye(C, dtype=np.float32)
    onesb = np.ones((C, C), dtype=np.float32).astype(BF)
    esel = np.zeros((8, 512), dtype=np.float32)
    for p in range(512):
        esel[p // 64, p] = 1.0
    return dict(trib=trib, trir=trir, mask=mask, identb=identb, identf=identf,
                onesb=onesb, esel=esel.astype(BF))


def _cmat(core):
    m_ = np.zeros((8, 8), dtype=np.float32)
    for r in range(8):
        for m in range(8):
            if r < m < core:
                m_[m, r] = 1.0
    for r in range(8):
        if r >= core:
            m_[:, r] = 0.0
    return m_.astype(BF)


def _wmask(core):
    m = np.zeros((8,), np.float32)
    m[:core] = 1.0
    return m



def build_phase_a():
    nc = bass.Bass("TRN2", target_bir_lowering=False, debug=False)
    ap = {}
    ap["x"] = nc.dram_tensor("x", [TOK, D], F32, kind="ExternalInput").ap()
    for n, shp in [("wq", [D, D]), ("wk", [D, D]), ("wv", [D, 2 * D]),
                   ("wgk1", [D, 16]), ("wgk2", [16, D]), ("bgk2", [1, D])]:
        ap[n] = nc.dram_tensor(n, shp, F32, kind="ExternalInput").ap()
    for n, shp in [("ctrib", [C, C]), ("ctrir", [C, C]), ("cidentb", [C, C]),
                   ("conesb", [C, C])]:
        ap[n] = nc.dram_tensor(n, shp, BF16, kind="ExternalInput").ap()
    ap["cmask"] = nc.dram_tensor("cmask", [C, C], F32, kind="ExternalInput").ap()
    ap["cidentf"] = nc.dram_tensor("cidentf", [C, C], F32, kind="ExternalInput").ap()
    ap["cln16"] = nc.dram_tensor("cln16", [C, 1], F32, kind="ExternalInput").ap()
    ap["o16_d"] = nc.dram_tensor("o16_d", [NCH, C, 2 * D], F16, kind="ExternalOutput").ap()
    ap["qgT_d"] = nc.dram_tensor("qgT_d", [NCH, C, D], BF16, kind="ExternalOutput").ap()
    ap["base_d"] = nc.dram_tensor("base_d", [NCH, C, D], F32, kind="ExternalOutput").ap()
    ap["xnT_d"] = nc.dram_tensor("xnT_d", [NCH, C, D], BF16, kind="ExternalOutput").ap()
    ap["T_d"] = nc.dram_tensor("T_d", [C, 8 * DV], BF16, kind="ExternalOutput").ap()
    ap["btot_d"] = nc.dram_tensor("btot_d", [1, D], F32, kind="ExternalOutput").ap()

    es = ExitStack()
    with es:
        def sb(n, s, t):
            return es.enter_context(nc.sbuf_tensor(n, s, t))[:]

        def psm(n, s, t):
            return es.enter_context(nc.psum_tensor(n, s, t))[:]

        w_q = sb("w_q", [128, 8 * D], BF16)
        w_k = sb("w_k", [128, 8 * D], BF16)
        w_v = sb("w_v", [128, 8 * 2 * D], BF16)
        w_gk1 = sb("w_gk1", [128, 8 * 16], BF16)
        w_gk2 = sb("w_gk2", [16, D], BF16)
        s_bgk2 = sb("s_bgk2", [1, D], BF16)
        c_trib = sb("c_trib", [C, C], BF16)
        c_trir = sb("c_trir", [C, C], BF16)
        c_identb = sb("c_identb", [C, C], BF16)
        c_ones = sb("c_ones", [C, C], BF16)
        c_mask = sb("c_mask", [C, C], F32)
        c_ln16 = sb("c_ln16", [C, 1], F32)
        s_S = sb("s_S", [128, 8 * DV], F32)
        s_Sb = sb("s_Sb", [128, 8 * DV], BF16)
        s_bpreT = sb("s_bpreT", [C, 8], F32)
        s_btrow = sb("s_btrow", [1, D], F32)
        c_identf = sb("c_identf", [C, C], F32)

        spec = {
            "x": ([C, D], F32, 2), "st": ([C, 8], F32, 2), "xn": ([C, D], F32, 2),
            "xnb": ([C, D], BF16, 2), "xnT": ([C, D], BF16, 3), "f1": ([C, D], F32, 1),
            "ghi": ([C, D], BF16, 2), "glo": ([C, D], BF16, 2),
            "z1e": ([C, 16], BF16, 2), "z1b": ([16, C], BF16, 2),
            "lam": ([C, D], BF16, 2), "lami": ([C, D], BF16, 2),
            "lrev": ([C, D], BF16, 2),
            "qh": ([C, D], BF16, 2),
            "bnd": ([C, 8], F32, 2), "expbp": ([C, 8], F32, 3),
            "kt": ([C, D], BF16, 2), "kh": ([C, D], BF16, 3),
            "v": ([C, 2 * D], BF16, 3),
            "qhT": ([C, D], BF16, 2), "ktT": ([C, D], BF16, 2), "qgT": ([C, D], BF16, 2),
            "pt": ([C, C], BF16, 2), "o16": ([C, 2 * D], F16, 2),
            "dc": ([C, 8], F32, 4),
        }
        tiles = {}
        for nm, (shp, dt2, nb) in spec.items():
            tiles[nm] = [sb(f"t_{nm}_{b}", shp, dt2) for b in range(nb)]

        def T(nm, i):
            nb = spec[nm][2]
            return tiles[nm][i % nb], f"t_{nm}_{i % nb}"

        ptb1 = psm("ptb1", [128, 1024], BF16)
        ptb4 = psm("ptb4", [128, 1024], BF16)
        pc = psm("pc", [128, 1024], F32)
        pp = psm("pp", [128, 1024], F32)
        psc = psm("psc", [128, 512], F32)
        psc2 = psm("psc2", [128, 512], F32)
        PTB1, PTB4, PC, PP, PSC, PSC2 = ("ps_ptb1", "ps_ptb4", "ps_pc", "ps_pp",
                                         "ps_psc", "ps_psc2")

        p = Prog(nc)
        R = p.rec

        def ld(engq, dst, src, rd, wr):
            R(engq[0], lambda e, d=dst, s=src: e.dma_start(out=d, in_=s),
              reads=rd, writes=wr, queue=engq[1])

        gq = ("gpsimd", "qg")
        sq = ("sync", "qs")
        for kt in range(8):
            ld(gq, w_q[:, kt * D:(kt + 1) * D],
               ap["wq"][kt * 128:(kt + 1) * 128, :], ["d:wq"], ["w_q"])
            ld(gq, w_k[:, kt * D:(kt + 1) * D],
               ap["wk"][kt * 128:(kt + 1) * 128, :], ["d:wk"], ["w_k"])
            ld(gq, w_v[:, kt * 2 * D:(kt + 1) * 2 * D],
               ap["wv"][kt * 128:(kt + 1) * 128, :], ["d:wv"], ["w_v"])
        ld(gq, w_gk1.rearrange("p (kt n) -> p kt n", kt=8),
           ap["wgk1"].rearrange("(kt p) n -> p kt n", p=128), ["d:wgk1"], ["w_gk1"])
        ld(gq, w_gk2, ap["wgk2"], ["d:wgk2"], ["w_gk2"])
        ld(gq, s_bgk2, ap["bgk2"], ["d:bgk2"], ["s_bgk2"])
        ld(gq, c_trib, ap["ctrib"], ["d:ctrib"], ["c_trib"])
        ld(gq, c_trir, ap["ctrir"], ["d:ctrir"], ["c_trir"])
        ld(gq, c_identb, ap["cidentb"], ["d:cidentb"], ["c_identb"])
        ld(gq, c_ones, ap["conesb"], ["d:conesb"], ["c_ones"])
        ld(sq, c_mask, ap["cmask"], ["d:cmask"], ["c_mask"])
        ld(sq, c_identf, ap["cidentf"], ["d:cidentf"], ["c_identf"])
        ld(sq, c_ln16, ap["cln16"], ["d:cln16"], ["c_ln16"])
        R("vector", lambda e: e.memset(s_S, 0.0), writes=["s_S"])
        R("vector", lambda e: e.memset(s_Sb, 0.0), writes=["s_Sb"])
        R("vector", lambda e: e.memset(s_bpreT, 0.0), writes=["s_bpreT"])

        def mm(out, lhsT, rhs, start, stop, rd, wr):
            R("tensor", lambda e, o=out, l=lhsT, r=rhs, s1=start, s2=stop:
              e.matmul(o, l, r, start=s1, stop=s2), reads=rd, writes=wr)

        def tr(out, in_, ident, rd, wr):
            R("tensor", lambda e, o=out, i=in_, d=ident: e.transpose(o, i, d),
              reads=rd, writes=wr)

        def S0(i):
            d, dn = T("x", i)
            R("sync", lambda e, dd=d, sl=ap["x"][i * C:(i + 1) * C, :]:
              e.dma_start(out=dd, in_=sl), reads=["d:x"], writes=[dn], queue="qs")

        def S1a(i):
            x, xn_ = T("x", i)
            st, stn = T("st", i)
            xn, xnn = T("xn", i)
            xnb, xnbn = T("xnb", i)
            xnT, xnTn = T("xnT", i)
            f1, f1n = T("f1", i)
            R("vector", lambda e: e.reduce_sum(st[:, 0:1], x, axis=AX.X),
              reads=[xn_], writes=[stn])
            R("scalar", lambda e: e.activation(out=f1, in_=x, func=AF.Square,
                                               accum_out=st[:, 1:2]),
              reads=[xn_], writes=[f1n, stn])
            R("vector", lambda e: e.tensor_scalar(out=st[:, 2:3], in0=st[:, 0:1],
                                                  scalar1=1.0 / D, scalar2=None, op0=ALU.mult),
              reads=[stn], writes=[stn])
            R("vector", lambda e: e.tensor_scalar(out=st[:, 3:4], in0=st[:, 1:2],
                                                  scalar1=1.0 / D, scalar2=None, op0=ALU.mult),
              reads=[stn], writes=[stn])
            R("vector", lambda e: e.tensor_mul(st[:, 4:5], st[:, 2:3], st[:, 2:3]),
              reads=[stn], writes=[stn])
            R("vector", lambda e: e.tensor_sub(st[:, 4:5], st[:, 3:4], st[:, 4:5]),
              reads=[stn], writes=[stn])
            R("vector", lambda e: e.tensor_scalar(out=st[:, 4:5], in0=st[:, 4:5],
                                                  scalar1=1.0, scalar2=EPS, op0=ALU.mult,
                                                  op1=ALU.add),
              reads=[stn], writes=[stn])
            R("scalar", lambda e: e.activation(out=st[:, 5:6], in_=st[:, 4:5], func=AF.Ln),
              reads=[stn], writes=[stn])
            R("scalar", lambda e: e.activation(out=st[:, 6:7], in_=st[:, 5:6], func=AF.Exp,
                                               scale=-0.5),
              reads=[stn], writes=[stn])
            R("vector", lambda e: e.tensor_scalar(out=xn, in0=x, scalar1=st[:, 2:3],
                                                  scalar2=st[:, 6:7], op0=ALU.subtract,
                                                  op1=ALU.mult),
              reads=[xn_, stn], writes=[xnn])
            R("vector", lambda e: e.tensor_copy(xnb, xn), reads=[xnn], writes=[xnbn])
            R("vector", lambda e: e.tensor_add(x, x, xn), reads=[xn_, xnn], writes=[xn_])
            R("sync", lambda e, cc=i: e.dma_start(out=ap["base_d"][cc], in_=x),
              reads=[xn_], writes=["d:base"], queue="qs")
            for j in range(8):
                tr(ptb1[:, j * C:(j + 1) * C], xnb[:, j * C:(j + 1) * C], c_identb,
                   [xnbn, "c_identb"], [PTB1])
            R("scalar", lambda e: e.copy(xnT, ptb1), reads=[PTB1], writes=[xnTn])
            R("sync", lambda e, cc=i: e.dma_start(out=ap["xnT_d"][cc], in_=xnT),
              reads=[xnTn], writes=["d:xnT"], queue="qs")

        def S1b(i):
            xnT, xnTn = T("xnT", i)
            xn, xnn = T("xn", i)
            f1, f1n = T("f1", i)
            ghi, ghin = T("ghi", i)
            glo, glon = T("glo", i)
            z1e, z1en = T("z1e", i)
            z1b, z1bn = T("z1b", i)
            for j in range(8):
                mm(pc[:, 0:16], xnT[:, j * C:(j + 1) * C],
                   w_gk1[:, j * 16:(j + 1) * 16], j == 0, j == 7, [xnTn, "w_gk1"], [PC])
            R("vector", lambda e: e.tensor_copy(z1e, pc[:, 0:16]), reads=[PC], writes=[z1en])
            tr(ptb1[0:16, 0:C], z1e, c_identb, [z1en, "c_identb"], [PTB1])
            R("scalar", lambda e: e.copy(z1b, ptb1[0:16, 0:C]), reads=[PTB1], writes=[z1bn])
            for nh in range(2):
                sl = slice(nh * 512, (nh + 1) * 512)
                mm(pc[:, sl], z1b, w_gk2[:, sl], True, False, [z1bn, "w_gk2"], [PC])
                mm(pc[:, sl], c_ones[0:1, :], s_bgk2[:, sl], False, True,
                   ["c_ones", "s_bgk2"], [PC])
            R("scalar", lambda e: e.activation(out=f1, in_=pc, func=AF.Abs),
              reads=[PC], writes=[f1n])
            R("scalar", lambda e: e.activation(out=f1, in_=f1, func=AF.Exp, scale=-1.0),
              reads=[f1n], writes=[f1n])
            R("scalar", lambda e: e.activation(out=f1, in_=f1, func=AF.Ln, bias=1.0),
              reads=[f1n], writes=[f1n])
            R("vector", lambda e: e.tensor_scalar(out=xn, in0=pc, scalar1=0.0,
                                                  scalar2=None, op0=ALU.min),
              reads=[PC], writes=[xnn])
            R("vector", lambda e: e.tensor_sub(xn, xn, f1), reads=[xnn, f1n], writes=[xnn])
            R("vector", lambda e: e.tensor_copy(ghi, xn), reads=[xnn], writes=[ghin])
            R("vector", lambda e: e.tensor_sub(glo, xn, ghi), reads=[xnn, ghin], writes=[glon])

        def S2a(i):
            ghi, ghin = T("ghi", i)
            glo, glon = T("glo", i)
            lam, lamn = T("lam", i)
            lami, lamin = T("lami", i)
            for nh in range(2):
                sl = slice(nh * 512, (nh + 1) * 512)
                mm(pc[:, sl], c_trib, ghi[:, sl], True, False, ["c_trib", ghin], [PC])
                mm(pc[:, sl], c_trib, glo[:, sl], False, True, ["c_trib", glon], [PC])
            R("scalar", lambda e: e.activation(out=lam, in_=pc, func=AF.Exp, bias=c_ln16),
              reads=[PC, "c_ln16"], writes=[lamn])
            R("scalar", lambda e: e.activation(out=lami, in_=pc, func=AF.Exp, scale=-1.0),
              reads=[PC], writes=[lamin])

        def S2b(i):
            ghi, ghin = T("ghi", i)
            glo, glon = T("glo", i)
            lrev, lrevn = T("lrev", i)
            for nh in range(2):
                sl = slice(nh * 512, (nh + 1) * 512)
                mm(pc[:, sl], c_trir, ghi[:, sl], True, False, ["c_trir", ghin], [PC])
                mm(pc[:, sl], c_trir, glo[:, sl], False, True, ["c_trir", glon], [PC])
            R("scalar", lambda e: e.activation(out=lrev, in_=pc, func=AF.Exp),
              reads=[PC], writes=[lrevn])

        def S2c(i):
            ghi, ghin = T("ghi", i)
            glo, glon = T("glo", i)
            dc, dcn = T("dc", i)
            bnd, bndn = T("bnd", i)
            expbp, expbpn = T("expbp", i)
            for j in range(8):
                mm(pc[:, j:j + 1], ghi[:, j * C:(j + 1) * C], c_ones[:, 0:1],
                   j == 0, False, [ghin, "c_ones"], [PC])
                mm(pc[:, j:j + 1], glo[:, j * C:(j + 1) * C], c_ones[:, 0:1],
                   False, j == 7, [glon, "c_ones"], [PC])
            R("scalar", lambda e: e.activation(out=dc, in_=pc[:, 0:8], func=AF.Exp),
              reads=[PC], writes=[dcn])
            R("vector", lambda e: e.tensor_copy(bnd, pc[:, 0:8]), reads=[PC], writes=[bndn])
            R("scalar", lambda e: e.activation(out=expbp, in_=s_bpreT, func=AF.Exp),
              reads=["s_bpreT"], writes=[expbpn])
            R("vector", lambda e: e.tensor_add(s_bpreT, s_bpreT, bnd),
              reads=["s_bpreT", bndn], writes=["s_bpreT"])

        def S3q(i):
            xnT, xnTn = T("xnT", i)
            lam, lamn = T("lam", i)
            qh, qhn = T("qh", i)
            for j in range(8):
                for nh in range(2):
                    sl = slice(nh * 512, (nh + 1) * 512)
                    mm(pp[:, sl], xnT[:, j * C:(j + 1) * C],
                       w_q[:, j * D + nh * 512: j * D + (nh + 1) * 512],
                       j == 0, j == 7, [xnTn, "w_q"], [PP])
            R("vector", lambda e: e.tensor_mul(qh, pp, lam), reads=[PP, lamn], writes=[qhn])

        def S3k(i):
            xnT, xnTn = T("xnT", i)
            lami, lamin = T("lami", i)
            lrev, lrevn = T("lrev", i)
            kt, ktn = T("kt", i)
            kh, khn = T("kh", i)
            for j in range(8):
                for nh in range(2):
                    sl = slice(nh * 512, (nh + 1) * 512)
                    mm(pp[:, sl], xnT[:, j * C:(j + 1) * C],
                       w_k[:, j * D + nh * 512: j * D + (nh + 1) * 512],
                       j == 0, j == 7, [xnTn, "w_k"], [PP])
            R("vector", lambda e: e.tensor_mul(kt, pp, lami), reads=[PP, lamin], writes=[ktn])
            R("vector", lambda e: e.tensor_mul(kh, pp, lrev), reads=[PP, lrevn], writes=[khn])

        def S3v(i, half):
            xnT, xnTn = T("xnT", i)
            v, vn = T("v", i)
            for j in range(8):
                for nh in range(2):
                    mm(pp[:, nh * 512:(nh + 1) * 512], xnT[:, j * C:(j + 1) * C],
                       w_v[:, j * 2 * D + half * D + nh * 512:
                           j * 2 * D + half * D + (nh + 1) * 512],
                       j == 0, j == 7, [xnTn, "w_v"], [PP])
            R("scalar", lambda e, h=half: e.copy(v[:, h * D:(h + 1) * D], pp),
              reads=[PP], writes=[vn])

        def S4(i):
            for src, dst in (("qh", "qhT"), ("kt", "ktT")):
                sT, sn = T(src, i)
                dT, dn = T(dst, i)
                for j in range(8):
                    tr(ptb4[:, j * C:(j + 1) * C], sT[:, j * C:(j + 1) * C], c_identb,
                       [sn, "c_identb"], [PTB4])
                R("scalar", lambda e, d=dT, ps=ptb4: e.copy(d, ps), reads=[PTB4], writes=[dn])
            qhT, qhTn = T("qhT", i)
            qgT, qgTn = T("qgT", i)
            expbp, expbpn = T("expbp", i)
            for j in range(8):
                R("vector", lambda e, jj=j: e.tensor_scalar(
                    out=qgT[:, jj * C:(jj + 1) * C], in0=qhT[:, jj * C:(jj + 1) * C],
                    scalar1=expbp[:, jj:jj + 1], scalar2=None, op0=ALU.mult),
                  reads=[qhTn, expbpn], writes=[qgTn])
            R("sync", lambda e, cc=i: e.dma_start(out=ap["qgT_d"][cc], in_=qgT),
              reads=[qgTn], writes=["d:qgT"], queue="qs")

        def S5h(i, h):
            qhT, qhTn = T("qhT", i)
            ktT, ktTn = T("ktT", i)
            kh, khn = T("kh", i)
            v, vn = T("v", i)
            pt, ptn = T("pt", i)
            o16, o16n = T("o16", i)
            dc, dcn = T("dc", i)
            for ji, j in enumerate((2 * h, 2 * h + 1)):
                mm(psc2[:, 0:C], ktT[:, j * C:(j + 1) * C], qhT[:, j * C:(j + 1) * C],
                   ji == 0, ji == 1, [ktTn, qhTn], [PSC2])
            R("vector", lambda e: e.tensor_mul(pt, psc2[:, 0:C], c_mask),
              reads=[PSC2, "c_mask"], writes=[ptn])
            mm(psc[:, 0:DV], pt, v[:, h * DV:(h + 1) * DV], True, False,
               [ptn, vn], [PSC])
            for ji, j in enumerate((2 * h, 2 * h + 1)):
                mm(psc[:, 0:DV], qhT[:, j * C:(j + 1) * C], s_Sb[:, j * DV:(j + 1) * DV],
                   False, ji == 1, [qhTn, "s_Sb"], [PSC])
            R("scalar", lambda e: e.copy(o16[:, h * DV:(h + 1) * DV], psc[:, 0:DV]),
              reads=[PSC], writes=[o16n])
            for ih in range(2):
                sg = 2 * h + ih
                pbank, pbn = (psc2, PSC2) if ih == 0 else (psc, PSC)
                mm(pbank[:, 0:DV], kh[:, h * DK + ih * C: h * DK + (ih + 1) * C],
                   v[:, h * DV:(h + 1) * DV], True, True, [khn, vn], [pbn])
                R("vector", lambda e, g=sg, pb=pbank: e.scalar_tensor_tensor(
                    out=s_S[:, g * DV:(g + 1) * DV], in0=s_S[:, g * DV:(g + 1) * DV],
                    scalar=dc[:, g:g + 1], in1=pb[:, 0:DV],
                    op0=ALU.mult, op1=ALU.add),
                  reads=["s_S", dcn, pbn], writes=["s_S"])
                R("scalar", lambda e, g=sg: e.copy(s_Sb[:, g * DV:(g + 1) * DV],
                                                   s_S[:, g * DV:(g + 1) * DV]),
                  reads=["s_S"], writes=["s_Sb"])
            if h == 3:
                R("sync", lambda e, cc=i: e.dma_start(out=ap["o16_d"][cc], in_=o16),
                  reads=[o16n], writes=["d:o16"], queue="qs")

        S0(0)
        for i in range(NCH + 5):
            if 0 <= i + 1 < NCH:
                S0(i + 1)
            if 0 <= i - 2 < NCH:
                S3q(i - 2)
            if 0 <= i - 4 < NCH:
                S5h(i - 4, 0)
            if 0 <= i - 2 < NCH:
                S3k(i - 2)
            if 0 <= i - 4 < NCH:
                S5h(i - 4, 1)
            if 0 <= i - 2 < NCH:
                S3v(i - 2, 0)
            if 0 <= i - 4 < NCH:
                S5h(i - 4, 2)
            if 0 <= i - 2 < NCH:
                S3v(i - 2, 1)
            if 0 <= i - 4 < NCH:
                S5h(i - 4, 3)
            if 0 <= i - 1 < NCH:
                S2a(i - 1)
            if 0 <= i - 3 < NCH:
                S4(i - 3)
            if 0 <= i - 1 < NCH:
                S2b(i - 1)
            if 0 <= i < NCH:
                S1a(i)
            if 0 <= i - 1 < NCH:
                S2c(i - 1)
            if 0 <= i < NCH:
                S1b(i)

        R("sync", lambda e: e.dma_start(out=ap["T_d"], in_=s_Sb),
          reads=["s_Sb"], writes=["d:T"], queue="qs")
        for j in range(8):
            tr(pc[0:1, j * C:(j + 1) * C], s_bpreT[:, j:j + 1], c_identf,
               ["s_bpreT", "c_identf"], [PC])
        R("vector", lambda e: e.tensor_copy(s_btrow, pc[0:1, :]),
          reads=[PC], writes=["s_btrow"])
        R("sync", lambda e: e.dma_start(out=ap["btot_d"], in_=s_btrow),
          reads=["s_btrow"], writes=["d:btot"], queue="qs")
        p.barrier()

        block = es.enter_context(nc.Block())
        p.finalize(es, block)
    return nc


def build_phase_b():
    nc = bass.Bass("TRN2", target_bir_lowering=False, debug=False)
    ap = {}
    for n, shp, dt in [("o16_d", [NCH, C, 2 * D], F16), ("qgT_d", [NCH, C, D], BF16),
                       ("base_d", [NCH, C, D], F32), ("xnT_d", [NCH, C, D], BF16),
                       ("T_all", [NCORES * C, 8 * DV], BF16), ("btot_all", [8, D], F32),
                       ("x0", [1, D], F32),
                       ("wg", [D, 2 * D], F32), ("wo", [2 * D, D], F32),
                       ("alay", [D, 512], F32), ("bwT", [512, D], F32),
                       ("wrT", [D, 8], F32), ("gnw", [128, 4], F32),
                       ("cmatx", [8, 8], BF16), ("wmrow", [1, 8], BF16),
                       ("conesb", [C, C], BF16),
                       ("cidentb", [C, C], BF16), ("cidentf", [C, C], F32),
                       ("cesel", [8, 512], BF16)]:
        ap[n] = nc.dram_tensor(n, shp, dt, kind="ExternalInput").ap()
    ap["y"] = nc.dram_tensor("y", [TOK, D], F32, kind="ExternalOutput").ap()

    es = ExitStack()
    with es:
        def sb(n, s, t):
            return es.enter_context(nc.sbuf_tensor(n, s, t))[:]

        def psm(n, s, t):
            return es.enter_context(nc.psum_tensor(n, s, t))[:]

        w_g = sb("w_g", [128, 8 * 2 * D], BF16)
        w_wo = sb("w_wo", [128, 16 * D], BF16)
        w_alay = sb("w_alay", [128, 8 * 512], BF16)
        w_bwT = sb("w_bwT", [128, 4 * D], BF16)
        w_wrT = sb("w_wrT", [128, 8 * 8], F32)
        s_gnw = sb("s_gnw", [128, 4], F32)
        c_identb = sb("c_identb", [C, C], BF16)
        c_identf = sb("c_identf", [C, C], F32)
        c_esel = sb("c_esel", [8, 512], BF16)
        c_cmat = sb("c_cmat", [8, 8], BF16)
        c_ones = sb("c_ones", [C, C], BF16)
        s_wmrow = sb("s_wmrow", [1, 8], BF16)
        s_x0 = sb("s_x0", [1, D], F32)
        s_x0b = sb("s_x0b", [1, D], F32)
        g_st = sb("g_st", [1, 8], F32)
        g_junk = sb("g_junk", [1, D], F32)
        s_x0T = sb("s_x0T", [128, 8], F32)
        g_lg8 = sb("g_lg8", [8, 1], F32)
        g_lg = sb("g_lg", [1, 8], F32)
        g_d = sb("g_d", [1, 8], F32)
        g_ind = sb("g_ind", [1, 8], F32)
        g_l2 = sb("g_l2", [1, 8], F32)
        g_keep = sb("g_keep", [1, 8], F32)
        g_e = sb("g_e", [1, 8], F32)
        g_gt = sb("g_gt", [1, 8], F32)
        g_gc8 = sb("g_gc8", [8, 1], F32)
        g_gcb = sb("g_gcb", [8, 4], BF16)
        s_gc = sb("s_gc", [128, 4], F32)
        b_bt = sb("b_bt", [8, D], F32)
        b_bthi = sb("b_bthi", [8, D], BF16)
        b_btlo = sb("b_btlo", [8, D], BF16)
        b_w = sb("b_w", [128, 64], F32)
        b_Tr = [sb("b_Tr_0", [128, 8 * DV], BF16)]
        s_Sin = sb("s_Sin", [128, 8 * DV], F32)
        s_Sinb = sb("s_Sinb", [128, 8 * DV], BF16)

        spec = {
            "xnT": ([C, D], BF16, 3), "o16": ([C, 2 * D], F16, 3),
            "qgT": ([C, D], BF16, 3), "base": ([C, D], F32, 4),
            "ob": ([C, 2 * D], BF16, 2), "rs": ([C, 4], F32, 2),
            "sd": ([C, 4], F32, 2), "rst": ([C, 4], F32, 2),
            "junk": ([C, DV], F32, 1), "sg": ([C, 2 * D], BF16, 2),
            "ogT": ([C, 2 * D], BF16, 2),
            "mid": ([C, 512], BF16, 2), "midT": ([C, 512], BF16, 2),
        }
        tiles = {}
        for nm, (shp, dt2, nb) in spec.items():
            tiles[nm] = [sb(f"b_{nm}_{b}", shp, dt2) for b in range(nb)]

        def T(nm, i):
            nb = spec[nm][2]
            return tiles[nm][i % nb], f"b_{nm}_{i % nb}"

        pgA = psm("pgA", [128, 1024], F32)
        pd = psm("pd", [128, 1024], F32)
        pm = psm("pm", [128, 512], F32)
        pwo = psm("pwo", [128, 1024], F32)
        ptbB = psm("ptbB", [128, 1024], BF16)
        PGA, PD, PM, PWO, PTBB = "ps_pgA", "ps_pd", "ps_pm", "ps_pwo", "ps_ptbB"

        p = Prog(nc)
        R = p.rec

        def mm(out, lhsT, rhs, start, stop, rd, wr):
            R("tensor", lambda e, o=out, l=lhsT, r=rhs, s1=start, s2=stop:
              e.matmul(o, l, r, start=s1, stop=s2), reads=rd, writes=wr)

        def tr(out, in_, ident, rd, wr):
            R("tensor", lambda e, o=out, i=in_, d=ident: e.transpose(o, i, d),
              reads=rd, writes=wr)

        def ldg(dst, src, rd, wr):
            R("gpsimd", lambda e, d=dst, s=src: e.dma_start(out=d, in_=s),
              reads=rd, writes=wr, queue="qg")

        for kt in range(8):
            ldg(w_g[:, kt * 2 * D:(kt + 1) * 2 * D],
                ap["wg"][kt * 128:(kt + 1) * 128, :], ["d:wg"], ["w_g"])
        for kt in range(16):
            ldg(w_wo[:, kt * D:(kt + 1) * D],
                ap["wo"][kt * 128:(kt + 1) * 128, :], ["d:wo"], ["w_wo"])
        ldg(w_alay.rearrange("p (kt n) -> p kt n", kt=8),
            ap["alay"].rearrange("(kt p) n -> p kt n", p=128), ["d:alay"], ["w_alay"])
        ldg(w_bwT.rearrange("p (kt n) -> p kt n", kt=4),
            ap["bwT"].rearrange("(kt p) n -> p kt n", p=128), ["d:bwT"], ["w_bwT"])
        ldg(c_identb, ap["cidentb"], ["d:cidentb"], ["c_identb"])
        ldg(c_esel, ap["cesel"], ["d:cesel"], ["c_esel"])
        ldg(c_cmat, ap["cmatx"], ["d:cmatx"], ["c_cmat"])
        ldg(c_ones, ap["conesb"], ["d:conesb"], ["c_ones"])
        ldg(s_wmrow, ap["wmrow"], ["d:wmrow"], ["s_wmrow"])
        R("sync", lambda e: e.dma_start(
            out=w_wrT.rearrange("p (kt n) -> p kt n", kt=8),
            in_=ap["wrT"].rearrange("(kt p) n -> p kt n", p=128)),
          reads=["d:wrT"], writes=["w_wrT"], queue="qs")
        R("sync", lambda e: e.dma_start(out=s_gnw, in_=ap["gnw"]),
          reads=["d:gnw"], writes=["s_gnw"], queue="qs")
        R("sync", lambda e: e.dma_start(out=c_identf, in_=ap["cidentf"]),
          reads=["d:cidentf"], writes=["c_identf"], queue="qs")
        R("sync", lambda e: e.dma_start(out=s_x0, in_=ap["x0"]),
          reads=["d:x0"], writes=["s_x0"], queue="qs")
        R("sync", lambda e: e.dma_start(out=b_bt, in_=ap["btot_all"]),
          reads=["d:btot"], writes=["b_bt"], queue="qs")

        R("vector", lambda e: e.reduce_sum(g_st[:, 0:1], s_x0, axis=AX.X),
          reads=["s_x0"], writes=["g_st"])
        R("scalar", lambda e: e.activation(out=g_junk, in_=s_x0, func=AF.Square,
                                           accum_out=g_st[:, 1:2]),
          reads=["s_x0"], writes=["g_junk", "g_st"])
        R("vector", lambda e: e.tensor_scalar(out=g_st[:, 2:3], in0=g_st[:, 0:1],
                                              scalar1=1.0 / D, scalar2=None, op0=ALU.mult),
          reads=["g_st"], writes=["g_st"])
        R("vector", lambda e: e.tensor_scalar(out=g_st[:, 3:4], in0=g_st[:, 1:2],
                                              scalar1=1.0 / D, scalar2=None, op0=ALU.mult),
          reads=["g_st"], writes=["g_st"])
        R("vector", lambda e: e.tensor_mul(g_st[:, 4:5], g_st[:, 2:3], g_st[:, 2:3]),
          reads=["g_st"], writes=["g_st"])
        R("vector", lambda e: e.tensor_sub(g_st[:, 4:5], g_st[:, 3:4], g_st[:, 4:5]),
          reads=["g_st"], writes=["g_st"])
        R("vector", lambda e: e.tensor_scalar(out=g_st[:, 4:5], in0=g_st[:, 4:5],
                                              scalar1=1.0, scalar2=EPS, op0=ALU.mult,
                                              op1=ALU.add),
          reads=["g_st"], writes=["g_st"])
        R("scalar", lambda e: e.activation(out=g_st[:, 5:6], in_=g_st[:, 4:5], func=AF.Ln),
          reads=["g_st"], writes=["g_st"])
        R("scalar", lambda e: e.activation(out=g_st[:, 6:7], in_=g_st[:, 5:6], func=AF.Exp,
                                           scale=-0.5),
          reads=["g_st"], writes=["g_st"])
        R("vector", lambda e: e.tensor_scalar(out=s_x0b, in0=s_x0, scalar1=g_st[:, 2:3],
                                              scalar2=g_st[:, 6:7], op0=ALU.subtract,
                                              op1=ALU.mult),
          reads=["s_x0", "g_st"], writes=["s_x0b"])
        for j in range(8):
            tr(pm[:, j:j + 1], s_x0b[:, j * C:(j + 1) * C], c_identf[0:1, 0:1],
               ["s_x0b", "c_identf"], [PM])
        R("scalar", lambda e: e.copy(s_x0T, pm[:, 0:8]), reads=[PM], writes=["s_x0T"])
        for j in range(8):
            mm(pd[0:8, 0:1], w_wrT[:, j * 8:(j + 1) * 8], s_x0T[:, j:j + 1],
               j == 0, j == 7, ["w_wrT", "s_x0T"], [PD])
        R("scalar", lambda e: e.copy(g_lg8, pd[0:8, 0:1]), reads=[PD], writes=["g_lg8"])
        tr(pm[0:1, 0:8], g_lg8, c_identf[0:8, 0:8], ["g_lg8", "c_identf"], [PM])
        R("scalar", lambda e: e.copy(g_lg, pm[0:1, 0:8]), reads=[PM], writes=["g_lg"])
        R("vector", lambda e: e.reduce_max(g_st[:, 7:8], g_lg, axis=AX.X),
          reads=["g_lg"], writes=["g_st"])
        R("vector", lambda e: e.tensor_scalar(out=g_d, in0=g_lg, scalar1=g_st[:, 7:8],
                                              scalar2=None, op0=ALU.subtract),
          reads=["g_lg", "g_st"], writes=["g_d"])
        R("vector", lambda e: e.tensor_scalar(out=g_ind, in0=g_d, scalar1=0.0,
                                              scalar2=None, op0=ALU.is_ge),
          reads=["g_d"], writes=["g_ind"])
        R("vector", lambda e: e.scalar_tensor_tensor(out=g_l2, in0=g_ind, scalar=-1e30,
                                                     in1=g_lg, op0=ALU.mult, op1=ALU.add),
          reads=["g_ind", "g_lg"], writes=["g_l2"])
        R("vector", lambda e: e.reduce_max(g_st[:, 0:1], g_l2, axis=AX.X),
          reads=["g_l2"], writes=["g_st"])
        R("vector", lambda e: e.tensor_scalar(out=g_keep, in0=g_lg, scalar1=g_st[:, 0:1],
                                              scalar2=None, op0=ALU.is_ge),
          reads=["g_lg", "g_st"], writes=["g_keep"])
        R("scalar", lambda e: e.activation(out=g_e, in_=g_d, func=AF.Exp),
          reads=["g_d"], writes=["g_e"])
        R("vector", lambda e: e.tensor_mul(g_e, g_e, g_keep),
          reads=["g_e", "g_keep"], writes=["g_e"])
        R("vector", lambda e: e.reduce_sum(g_st[:, 1:2], g_e, axis=AX.X),
          reads=["g_e"], writes=["g_st"])
        R("vector", lambda e: e.reciprocal(g_st[:, 2:3], g_st[:, 1:2]),
          reads=["g_st"], writes=["g_st"])
        R("vector", lambda e: e.tensor_scalar(out=g_gt, in0=g_e, scalar1=g_st[:, 2:3],
                                              scalar2=None, op0=ALU.mult),
          reads=["g_e", "g_st"], writes=["g_gt"])
        tr(pm[0:8, 8:9], g_gt, c_identf[0:1, 0:1], ["g_gt", "c_identf"], [PM])
        R("scalar", lambda e: e.copy(g_gc8, pm[0:8, 8:9]), reads=[PM], writes=["g_gc8"])
        R("vector", lambda e: e.tensor_copy(g_gcb[:, 0:1], g_gc8),
          reads=["g_gc8"], writes=["g_gcb"])
        R("vector", lambda e: e.tensor_sub(g_gcb[:, 2:3], g_gc8, g_gcb[:, 0:1]),
          reads=["g_gc8", "g_gcb"], writes=["g_gcb"])
        for tau in range(4):
            for hl in range(2):
                mm(pm[:, tau:tau + 1], c_esel[:, tau * 128:(tau + 1) * 128],
                   g_gcb[:, 2 * hl:2 * hl + 1], (tau == 0 and hl == 0),
                   (tau == 3 and hl == 1), ["c_esel", "g_gcb"], [PM])
        R("scalar", lambda e: e.copy(s_gc, pm[:, 0:4]), reads=[PM], writes=["s_gc"])
        for tau in range(4):
            R("vector", lambda e, u=tau: e.tensor_scalar(
                out=w_bwT[:, u * D:(u + 1) * D], in0=w_bwT[:, u * D:(u + 1) * D],
                scalar1=s_gc[:, u:u + 1], scalar2=None, op0=ALU.mult),
              reads=["w_bwT", "s_gc"], writes=["w_bwT"])
        for k in range(16):
            R("vector", lambda e, kk=k: e.tensor_scalar(
                out=w_wo[:, kk * D:(kk + 1) * D], in0=w_wo[:, kk * D:(kk + 1) * D],
                scalar1=s_gnw[:, kk % 4:kk % 4 + 1], scalar2=None, op0=ALU.mult),
              reads=["w_wo", "s_gnw"], writes=["w_wo"])

        R("vector", lambda e: e.tensor_copy(b_bthi, b_bt), reads=["b_bt"], writes=["b_bthi"])
        R("vector", lambda e: e.tensor_sub(b_btlo, b_bt, b_bthi),
          reads=["b_bt", "b_bthi"], writes=["b_btlo"])
        for tau in range(8):
            mm(pm[:, tau * 8:(tau + 1) * 8], b_bthi[:, tau * 128:(tau + 1) * 128],
               c_cmat, tau == 0, False, ["b_bthi", "c_cmat"], [PM])
            mm(pm[:, tau * 8:(tau + 1) * 8], b_btlo[:, tau * 128:(tau + 1) * 128],
               c_cmat, False, False, ["b_btlo", "c_cmat"], [PM])
            mm(pm[:, tau * 8:(tau + 1) * 8], c_ones[0:1, :], s_wmrow,
               False, tau == 7, ["c_ones", "s_wmrow"], [PM])
        R("scalar", lambda e: e.activation(out=b_w, in_=pm[:, 0:64], func=AF.Exp),
          reads=[PM], writes=["b_w"])
        R("vector", lambda e: e.memset(s_Sin, 0.0), writes=["s_Sin"])
        for r in range(NCORES):
            tb = b_Tr[0]
            tbn = "b_Tr_0"
            R("sync", lambda e, rr=r, d=tb: e.dma_start(
                out=d, in_=ap["T_all"][rr * C:(rr + 1) * C, :]),
              reads=["d:Tall"], writes=[tbn], queue="qs")
            for sg_ in range(8):
                R("vector", lambda e, g=sg_, rr=r, d=tb: e.scalar_tensor_tensor(
                    out=s_Sin[:, g * DV:(g + 1) * DV], in0=d[:, g * DV:(g + 1) * DV],
                    scalar=b_w[:, g * 8 + rr:g * 8 + rr + 1],
                    in1=s_Sin[:, g * DV:(g + 1) * DV],
                    op0=ALU.mult, op1=ALU.add),
                  reads=[tbn, "b_w", "s_Sin"], writes=["s_Sin"])
        R("scalar", lambda e: e.copy(s_Sinb, s_Sin), reads=["s_Sin"], writes=["s_Sinb"])

        def B0(i):
            for nm, key in [("xnT", "xnT_d"), ("o16", "o16_d"),
                            ("qgT", "qgT_d"), ("base", "base_d")]:
                d, dn = T(nm, i)
                R("sync", lambda e, cc=i, dd=d, k=key: e.dma_start(out=dd, in_=ap[k][cc]),
                  reads=["d:" + key], writes=[dn], queue="qs")

        def B1a(i):
            xnT, xnTn = T("xnT", i)
            qgT, qgTn = T("qgT", i)
            sg, sgn = T("sg", i)
            for j in range(8):
                for nh in range(2):
                    mm(pgA[:, nh * 512:(nh + 1) * 512], xnT[:, j * C:(j + 1) * C],
                       w_g[:, j * 2 * D + nh * 512: j * 2 * D + (nh + 1) * 512],
                       j == 0, j == 7, [xnTn, "w_g"], [PGA])
            for hi in range(2):
                for ji, j in enumerate((2 * hi, 2 * hi + 1)):
                    mm(pd[:, hi * DV:(hi + 1) * DV], qgT[:, j * C:(j + 1) * C],
                       s_Sinb[:, j * DV:(j + 1) * DV], ji == 0, ji == 1,
                       [qgTn, "s_Sinb"], [PD])
            R("scalar", lambda e: e.activation(out=sg[:, 0:D], in_=pgA, func=AF.Silu),
              reads=[PGA], writes=[sgn])

        def B1b(i):
            xnT, xnTn = T("xnT", i)
            o16, o16n = T("o16", i)
            qgT, qgTn = T("qgT", i)
            ob, obn = T("ob", i)
            sg, sgn = T("sg", i)
            mid, midn = T("mid", i)
            for j in range(8):
                for nh in range(2):
                    mm(pgA[:, nh * 512:(nh + 1) * 512], xnT[:, j * C:(j + 1) * C],
                       w_g[:, j * 2 * D + D + nh * 512: j * 2 * D + D + (nh + 1) * 512],
                       j == 0, j == 7, [xnTn, "w_g"], [PGA])
            for hi in range(2):
                R("vector", lambda e, h=hi: e.tensor_tensor(
                    ob[:, h * DV:(h + 1) * DV], o16[:, h * DV:(h + 1) * DV],
                    pd[:, h * DV:(h + 1) * DV], ALU.add),
                  reads=[o16n, PD], writes=[obn])
            for hi in range(2):
                h = 2 + hi
                for ji, j in enumerate((2 * h, 2 * h + 1)):
                    mm(pd[:, hi * DV:(hi + 1) * DV], qgT[:, j * C:(j + 1) * C],
                       s_Sinb[:, j * DV:(j + 1) * DV], ji == 0, ji == 1,
                       [qgTn, "s_Sinb"], [PD])
            R("scalar", lambda e: e.activation(out=sg[:, D:2 * D], in_=pgA, func=AF.Silu),
              reads=[PGA], writes=[sgn])
            for j in range(8):
                mm(pm[:, 0:512], xnT[:, j * C:(j + 1) * C],
                   w_alay[:, j * 512:(j + 1) * 512], j == 0, j == 7,
                   [xnTn, "w_alay"], [PM])
            for hi in range(2):
                h = 2 + hi
                R("vector", lambda e, h2=h, i2=hi: e.tensor_tensor(
                    ob[:, h2 * DV:(h2 + 1) * DV], o16[:, h2 * DV:(h2 + 1) * DV],
                    pd[:, i2 * DV:(i2 + 1) * DV], ALU.add),
                  reads=[o16n, PD], writes=[obn])
            R("vector", lambda e: e.tensor_copy(mid, pm[:, 0:512]), reads=[PM], writes=[midn])

        def B1c(i):
            ob, obn = T("ob", i)
            rs, rsn = T("rs", i)
            sd, sdn = T("sd", i)
            rst, rstn = T("rst", i)
            junk, junkn = T("junk", i)
            sg, sgn = T("sg", i)
            t1, t1n = ob, obn
            for h in range(4):
                R("scalar", lambda e, hh=h: e.activation(
                    out=junk, in_=ob[:, hh * DV:(hh + 1) * DV], func=AF.Square,
                    accum_out=rs[:, hh:hh + 1]),
                  reads=[obn], writes=[junkn, rsn])
            R("vector", lambda e: e.tensor_scalar(out=rs, in0=rs, scalar1=1.0 / DV,
                                                  scalar2=EPS, op0=ALU.mult, op1=ALU.add),
              reads=[rsn], writes=[rsn])
            R("vector", lambda e: e.tensor_scalar(
                out=rst[:].bitcast(I32), in0=rs[:].bitcast(I32), scalar1=1,
                scalar2=None, op0=ALU.logical_shift_right),
              reads=[rsn], writes=[rstn])
            R("vector", lambda e: e.tensor_scalar(
                out=rst[:].bitcast(I32), in0=rst[:].bitcast(I32), scalar1=-1,
                scalar2=0x5f3759df, op0=ALU.mult, op1=ALU.add),
              reads=[rstn], writes=[rstn])
            for _ in range(3):
                R("vector", lambda e: e.tensor_mul(sd, rst, rst), reads=[rstn], writes=[sdn])
                R("vector", lambda e: e.tensor_mul(sd, sd, rs), reads=[sdn, rsn], writes=[sdn])
                R("vector", lambda e: e.tensor_scalar(out=sd, in0=sd, scalar1=-0.5,
                                                      scalar2=1.5, op0=ALU.mult, op1=ALU.add),
                  reads=[sdn], writes=[sdn])
                R("vector", lambda e: e.tensor_mul(rst, rst, sd), reads=[rstn, sdn], writes=[rstn])
            R("vector", lambda e: e.tensor_mul(t1, ob, sg),
              reads=[obn, sgn], writes=[t1n])
            for h in range(4):
                R("vector", lambda e, hh=h: e.tensor_scalar(
                    out=t1[:, hh * DV:(hh + 1) * DV], in0=t1[:, hh * DV:(hh + 1) * DV],
                    scalar1=rst[:, hh:hh + 1], scalar2=None, op0=ALU.mult),
                  reads=[t1n, rstn], writes=[t1n])

        def B2mid(i):
            mid, midn = T("mid", i)
            midT, midTn = T("midT", i)
            for j in range(4):
                tr(ptbB[:, j * C:(j + 1) * C], mid[:, j * C:(j + 1) * C], c_identb,
                   [midn, "c_identb"], [PTBB])
            R("scalar", lambda e: e.copy(midT, ptbB[:, 0:512]), reads=[PTBB], writes=[midTn])

        def B2og(i, half):
            t1, t1n = T("ob", i)
            ogT, ogTn = T("ogT", i)
            for j in range(8):
                tr(ptbB[:, j * C:(j + 1) * C],
                   t1[:, half * D + j * C: half * D + (j + 1) * C],
                   c_identb, [t1n, "c_identb"], [PTBB])
            R("scalar", lambda e, h=half: e.copy(ogT[:, h * D:(h + 1) * D], ptbB),
              reads=[PTBB], writes=[ogTn])

        def B2wo(i):
            ogT, ogTn = T("ogT", i)
            midT, midTn = T("midT", i)
            base, basen = T("base", i)
            for nh in range(2):
                sl = slice(nh * 512, (nh + 1) * 512)
                for k in range(16):
                    mm(pwo[:, sl], ogT[:, k * C:(k + 1) * C],
                       w_wo[:, k * D + nh * 512: k * D + (nh + 1) * 512],
                       k == 0, False, [ogTn, "w_wo"], [PWO])
                for j in range(4):
                    mm(pwo[:, sl], midT[:, j * C:(j + 1) * C],
                       w_bwT[:, j * D + nh * 512: j * D + (nh + 1) * 512],
                       False, j == 3, [midTn, "w_bwT"], [PWO])
            R("vector", lambda e: e.tensor_add(base, base, pwo),
              reads=[basen, PWO], writes=[basen])
            R("sync", lambda e, cc=i: e.dma_start(
                out=ap["y"][cc * C:(cc + 1) * C, :], in_=base),
              reads=[basen], writes=["d:y"], queue="qs")

        B0(0)
        B0(1)
        for i in range(NCH + 1):
            if 0 <= i + 2 < NCH:
                B0(i + 2)
            if 0 <= i - 1 < NCH:
                B2mid(i - 1)
            if 0 <= i < NCH:
                B1a(i)
            if 0 <= i - 1 < NCH:
                B2og(i - 1, 0)
            if 0 <= i < NCH:
                B1b(i)
            if 0 <= i - 1 < NCH:
                B2og(i - 1, 1)
            if 0 <= i < NCH:
                B1c(i)
            if 0 <= i - 1 < NCH:
                B2wo(i - 1)

        p.barrier()
        block = es.enter_context(nc.Block())
        p.finalize(es, block)
    return nc


def _prep_shared(inputs):
    con = _tile_consts()
    w = {}
    w["wq"] = np.ascontiguousarray(inputs["L0_attn_Wq"], np.float32)
    w["wk"] = np.ascontiguousarray(inputs["L0_attn_Wk"], np.float32)
    w["wv"] = np.ascontiguousarray(inputs["L0_attn_Wv"], np.float32)
    w["wgk1"] = np.ascontiguousarray(inputs["L0_attn_Wgk1"], np.float32)
    w["wgk2"] = np.ascontiguousarray(inputs["L0_attn_Wgk2"], np.float32)
    w["bgk2"] = np.ascontiguousarray(inputs["L0_attn_bgk2"], np.float32).reshape(1, D)
    w["wg"] = np.ascontiguousarray(inputs["L0_attn_Wg"], np.float32)
    w["wo"] = np.ascontiguousarray(inputs["L0_attn_Wo"], np.float32)
    w["alay"] = np.ascontiguousarray(
        np.asarray(inputs["L0_ffn_A"], np.float32).reshape(512, D).T)
    w["bwT"] = np.ascontiguousarray(
        np.asarray(inputs["L0_ffn_Bw"], np.float32).transpose(0, 2, 1).reshape(512, D))
    w["wrT"] = np.ascontiguousarray(np.asarray(inputs["L0_ffn_Wr"], np.float32).T)
    w["gnw"] = np.ascontiguousarray(
        np.asarray(inputs["L0_attn_g_norm_weight"], np.float32).reshape(4, 128).T)
    x = np.asarray(inputs["L0_attn_x"], np.float32).reshape(NCORES * TOK, D)
    w["x0"] = np.ascontiguousarray(x[0:1, :])
    return con, w, x


def kernel(**inputs):
    con, w, x = _prep_shared(inputs)

    nc_a = build_phase_a()
    in_a = []
    for cidx in range(NCORES):
        m = dict(
            x=np.ascontiguousarray(x[cidx * TOK:(cidx + 1) * TOK]),
            wq=w["wq"], wk=w["wk"], wv=w["wv"], wgk1=w["wgk1"], wgk2=w["wgk2"],
            bgk2=w["bgk2"],
            ctrib=con["trib"], ctrir=con["trir"], cidentb=con["identb"],
            conesb=con["onesb"], cmask=con["mask"], cidentf=con["identf"],
            cln16=np.full((C, 1), -LN16, np.float32),
        )
        in_a.append(m)
    res_a = run_bass_kernel_spmd(nc_a, in_a, core_ids=list(range(NCORES)))
    ra = res_a.results

    T_all = np.concatenate([np.asarray(ra[cidx]["T_d"]) for cidx in range(NCORES)], axis=0)
    btot_all = np.zeros((8, D), np.float32)
    for cidx in range(NCORES):
        btot_all[cidx] = np.asarray(ra[cidx]["btot_d"])[0]

    nc_b = build_phase_b()
    in_b = []
    for cidx in range(NCORES):
        m = dict(
            o16_d=np.asarray(ra[cidx]["o16_d"]),
            qgT_d=np.asarray(ra[cidx]["qgT_d"]),
            base_d=np.asarray(ra[cidx]["base_d"]),
            xnT_d=np.asarray(ra[cidx]["xnT_d"]),
            T_all=T_all, btot_all=btot_all,
            x0=w["x0"], wg=w["wg"], wo=w["wo"], alay=w["alay"], bwT=w["bwT"],
            wrT=w["wrT"], gnw=w["gnw"],
            cmatx=_cmat(cidx),
            wmrow=np.where(np.arange(8) >= cidx, -1e4, 0.0).astype(np.float32)
                    .astype(BF).reshape(1, 8),
            conesb=con["onesb"],
            cidentb=con["identb"], cidentf=con["identf"], cesel=con["esel"],
        )
        in_b.append(m)
    res_b = run_bass_kernel_spmd(nc_b, in_b, core_ids=list(range(NCORES)))
    y = np.concatenate([np.asarray(res_b.results[cidx]["y"]) for cidx in range(NCORES)],
                       axis=0)
    return y.reshape(1, NCORES * TOK, D)
